# version 04
# baseline (speedup 1.0000x reference)
import numpy as np
import ml_dtypes
from contextlib import ExitStack

import concourse.bass as bass
import concourse.mybir as mybir
from concourse.bass_utils import run_bass_kernel_spmd

F32 = mybir.dt.float32
BF16 = mybir.dt.bfloat16
AX = mybir.AxisListType.X
OP = mybir.AluOpType
AF = mybir.ActivationFunctionType
BF = ml_dtypes.bfloat16

NCORES = 8
D = 1024
DC = 8
NT_OWN = 2048
HALO = 128
NT = NT_OWN + HALO
NH, DHD = 16, 64
NE, HF = 8, 2048
HFC = HF // 128
EPS_LN = 1e-5
EPS_LA = 1e-6
NEG = -1e9

FUSED = True
DEBUG = False

TILES_S1 = [[(0, 128), (128, 512), (640, 512)], [(1152, 512), (1664, 512)]]
TILES_OWN = [[(128, 512), (640, 512)], [(1152, 512), (1664, 512)]]
ALL_PHASES = ("s1", "s2", "s3", "s4", "s5", "s6")

ENGINES = ("sync", "tensor", "vector", "scalar", "gpsimd")


class Sem:
    def __init__(self, h):
        self.h = h
        self.n = 0


class Prog:
    def __init__(self, nc, es):
        self.nc = nc
        self.es = es
        self.q = {e: [] for e in ENGINES}
        self.sems = {}
        for name in ("pe", "ve", "ac", "po", "dpo", "dsy", "cc"):
            self.sems[name] = Sem(es.enter_context(nc.semaphore(name)))
        self.eng_sem = {"tensor": "pe", "vector": "ve", "scalar": "ac",
                        "gpsimd": "po"}

    def op(self, eng, fn, waits=(), inc=False):
        own = self.sems.get(self.eng_sem.get(eng))
        wl = []
        for w in waits:
            if w is None:
                continue
            s, v = w
            if s is own:
                continue
            assert v <= s.n, "wait on future event"
            wl.append((s, v))
        evt = None
        incs = ()
        selfwait = None
        if eng in ("vector", "scalar"):
            selfwait = own.n
            own.n += 1
            incs = ((own, 1),)
            evt = (own, own.n)
        elif inc:
            own.n += 1
            incs = ((own, 1),)
            evt = (own, own.n)
        self.q[eng].append((tuple(wl), fn, incs, selfwait))
        return evt

    def now(self, eng):
        s = self.sems[self.eng_sem[eng]]
        return (s, s.n)

    def uid(self, base):
        self._uid = getattr(self, "_uid", 0) + 1
        return f"{base}{self._uid}"

    def end_phase(self):
        self.mem_gate = tuple(self.now(e) for e in
                              ("tensor", "vector", "scalar", "gpsimd"))

    def dma(self, eng, out_ap, in_ap, waits=()):
        s = self.sems["dpo" if eng == "gpsimd" else "dsy"]
        wl = []
        for w in tuple(waits) + tuple(getattr(self, "mem_gate", ())):
            if w is None:
                continue
            ss, v = w
            assert v <= ss.n
            wl.append((ss, v))
        s.n += 16
        self.q[eng].append(
            (tuple(wl), lambda e, o=out_ap, i=in_ap: e.dma_start(o, i),
             ((s, 16),), None))
        return (s, s.n)

    def replay(self, name, eng):
        seen = {}
        own = self.sems.get(self.eng_sem.get(name))
        for waits, fn, incs, selfwait in self.q[name]:
            if selfwait is not None and selfwait > 0:
                eng.wait_ge(own.h, selfwait)
            for (s, v) in waits:
                if seen.get(id(s), -1) >= v:
                    continue
                eng.wait_ge(s.h, v)
                seen[id(s)] = v
            inst = fn(eng)
            for (s, n) in incs:
                inst.then_inc(s.h, n)


class Pool:

    def __init__(self, P, n=6):
        self.P = P
        self.banks = [P.es.enter_context(
            P.nc.psum_tensor(f"pool{i}", [128, 512], F32)) for i in range(n)]
        self.gate = [None] * n
        self.used = [False] * n
        self.i = 0

    def acquire(self):
        for _ in range(len(self.banks)):
            i = self.i
            self.i = (self.i + 1) % len(self.banks)
            if not self.used[i]:
                self.used[i] = True
                return self.banks[i], self.gate[i], i
        raise RuntimeError("psum pool exhausted")

    def release(self, idx, evt):
        self.used[idx] = False
        self.gate[idx] = evt


class Buf:
    def __init__(self, t):
        self.t = t
        self.evt = None

    def __getitem__(self, idx):
        return self.t[idx]


def sq(ap):
    return ap.squeeze()


def row(ap):
    return ap.rearrange("p a b -> p (a b)")


def build_program(phases, x_in_f32):
    nc = bass.Bass("TRN2", target_bir_lowering=False)
    es = ExitStack()
    P = Prog(nc, es)
    P.pool = Pool(P, 6)
    P.tp = [es.enter_context(nc.psum_tensor(f"tp{i}", [128, 1024], BF16))
            for i in range(2)]
    P.tp_gate = [None, None]
    P.tp_i = 0

    need = set(phases)
    ffn0 = bool(need & {"s1", "s3"})
    ffn1 = bool(need & {"s4", "s6"})

    dt_in = {}

    def din(name, shape, dtype=F32):
        dt_in[name] = nc.dram_tensor(name, list(shape), dtype,
                                     kind="ExternalInput")
        return dt_in[name]

    xin = din("xin", [NT, D], F32 if x_in_f32 else BF16)
    store_f32 = "s6" in need
    xout = nc.dram_tensor("xout", [NT_OWN, D],
                          F32 if store_f32 else BF16, kind="ExternalOutput")
    din("ident", [128, 128])
    if ffn0 or ffn1:
        din("esel", [8, 8 * 128])
    if ffn0:
        din("gWg", [8, D])
        din("gW1", [NE, 4, 128, 4096])
        din("gb1", [1, NE * HF])
        din("gW2", [NE, 4, 128, 4096])
        din("gb2", [NE, D])
    if ffn1:
        din("hWg", [8, D])
        din("hW1", [NE, 4, 128, 4096])
        din("hW3", [NE, 4, 128, 4096])
        din("hW2", [NE, 4, 128, 4096])
    if "s2" in need:
        din("pswap", [128, 128])
        din("aWqT", [D, D]); din("aWkT", [D, D])
        din("aWvT", [D, D]); din("aWoT", [D, D])
        din("abq", [1, D]); din("abk", [1, D])
        din("abv", [1, D]); din("abo", [1, D])
        din("mask0", [128, 256]); din("maskr", [128, 256])
        din("ropecos", [128, NT]); din("ropesin", [128, NT])
    if "s5" in need:
        din("lWq", [D, D]); din("lWk", [D, D]); din("lWv", [D, D])
        din("lWo", [D, D])
        din("lbq", [1, D]); din("lbk", [1, D]); din("lbv", [1, D])
        din("lbo", [1, D])
        din("seqmask", [128, 2])
        P.cc_in = nc.dram_tensor("cc_in", [128, 1040], F32)
        P.cc_out = nc.dram_tensor("cc_out", [128, 1040], F32)

    xT = Buf(es.enter_context(nc.sbuf_tensor("xT", [128, DC, NT], BF16)))
    identb = es.enter_context(nc.sbuf_tensor("identb", [128, 128], BF16))
    ones_r = es.enter_context(nc.sbuf_tensor("ones_r", [1, 512], BF16))
    ones_c = es.enter_context(nc.sbuf_tensor("ones_c", [128, 1], BF16))
    ones_f = es.enter_context(nc.sbuf_tensor("ones_f", [1, 128], F32))
    P.xT, P.identb = xT, identb
    P.ones_r, P.ones_c, P.ones_f = ones_r, ones_c, ones_f

    P.ld_id = P.dma("gpsimd", identb[:, :], dt_in["ident"][:, :])
    P.op("gpsimd", lambda e: e.memset(ones_r[:, :], 1.0))
    P.op("gpsimd", lambda e: e.memset(ones_c[:, :], 1.0))
    P.ev_ones = P.op("gpsimd", lambda e: e.memset(ones_f[:, :], 1.0),
                     inc=True)

    emit_load_x(P, xin, cast=x_in_f32)

    for ph in ALL_PHASES:
        if ph not in need:
            continue
        if ph == "s1":
            tiles = TILES_S1 if len(need) > 1 else TILES_OWN
            emit_ffn(P, dt_in, layer=0, halves=tiles)
        elif ph == "s3":
            emit_ffn(P, dt_in, layer=0, halves=TILES_OWN)
        elif ph in ("s4", "s6"):
            emit_ffn(P, dt_in, layer=1, halves=TILES_OWN)
        elif ph == "s2":
            emit_attn_l0(P, dt_in)
        elif ph == "s5":
            emit_attn_l1(P, dt_in)
        P.end_phase()

    emit_store_x(P, xout, as_f32=store_f32)

    fin = [(P.sems["dsy"], P.sems["dsy"].n), (P.sems["dpo"], P.sems["dpo"].n)]
    P.q["gpsimd"].append((tuple(fin), lambda e: e.engine_nop(), (), None))

    with nc.Block() as block:
        @block.sync
        def _(eng):
            P.replay("sync", eng)

        @block.tensor
        def _(eng):
            P.replay("tensor", eng)

        @block.vector
        def _(eng):
            P.replay("vector", eng)

        @block.scalar
        def _(eng):
            P.replay("scalar", eng)

        @block.gpsimd
        def _(eng):
            P.replay("gpsimd", eng)

    es.close()
    return nc


def emit_transpose(P, dst_ap, src_ap, waits=()):
    i = P.tp_i
    P.tp_i ^= 1
    t = P.tp[i]
    npart = src_ap.shape[0]
    nf = src_ap.shape[-1]
    pe = P.op("tensor",
              lambda e, o=t, s=src_ap, np_=npart, nf_=nf:
              e.transpose(o[0:nf_, 0:np_], s, P.identb[0:np_, 0:np_]),
              waits=list(waits) + [P.tp_gate[i], P.ld_id], inc=True)
    ve = P.op("vector",
              lambda e, o=dst_ap, i_=t, np_=npart, nf_=nf:
              e.tensor_copy(o, i_[0:nf_, 0:np_]),
              waits=[pe], inc=True)
    P.tp_gate[i] = ve
    return ve


def emit_load_x(P, xin, cast):
    nc = P.nc
    with ExitStack() as les:
        xtok = les.enter_context(nc.sbuf_tensor(P.uid("xtok"), [128, 17, D], BF16))
        src = xin.ap().rearrange("(tc p) f -> p tc f", p=128)
        eng = "gpsimd" if cast else "sync"
        ev = P.dma(eng, xtok[:, :, :], src)
        for tc in range(17):
            for fc in range(DC):
                emit_transpose(P, sq(P.xT[:, fc, tc * 128:(tc + 1) * 128]),
                               sq(xtok[:, tc, fc * 128:(fc + 1) * 128]),
                               waits=[ev])
        P.xT.evt = P.now("vector")
    P.end_phase()


def emit_store_x(P, xout, as_f32):
    nc = P.nc
    es = P.es
    dtype = F32 if as_f32 else BF16
    obuf = [es.enter_context(nc.sbuf_tensor(f"obuf{i}", [128, D], dtype))
            for i in range(2)]
    ogate = [None, None]
    dst = xout.ap().rearrange("(tc p) f -> p tc f", p=128)
    for tc in range(16):
        ob = obuf[tc % 2]
        last = None
        for fc in range(DC):
            last = emit_transpose(
                P, ob[:, fc * 128:(fc + 1) * 128],
                sq(P.xT[:, fc, HALO + tc * 128:HALO + (tc + 1) * 128]),
                waits=[P.xT.evt, ogate[tc % 2]])
        ogate[tc % 2] = P.dma("sync", sq(dst[:, tc, :]), ob[:, :],
                              waits=[last])


def emit_ln(P, tiles, out_xhat, les=None, toff=0):
    nc = P.nc
    les = ExitStack()
    xsq = les.enter_context(nc.sbuf_tensor(P.uid("ln_xsq"), [128, 512], BF16))
    rows = les.enter_context(nc.sbuf_tensor(P.uid("ln_rows"), [1, 2048], F32))
    nmB = les.enter_context(nc.sbuf_tensor(P.uid("ln_nmB"), [128, 512], BF16))
    rsB = les.enter_context(nc.sbuf_tensor(P.uid("ln_rsB"), [128, 512], BF16))
    tb = les.enter_context(nc.sbuf_tensor(P.uid("ln_tb"), [128, 512], BF16))
    tile_evts = []
    for (t0, tl) in tiles:
        bank1, g1, i1 = P.pool.acquire()
        bank2, g2, i2 = P.pool.acquire()
        sx = bank1[0:1, 0:tl]
        sx2 = bank2[0:1, 0:tl]
        pe2 = None
        pe = None
        for c in range(DC):
            xc = sq(P.xT[:, c, t0:t0 + tl])
            ev_sq = P.op("scalar",
                         lambda e, o=xsq[:, 0:tl], i=xc:
                         e.activation(o, i, AF.Square),
                         waits=[P.xT.evt, g1, g2, pe2], inc=True)
            pe = P.op("tensor",
                      lambda e, o=sx, i=xc, st=(c == 0), sp=(c == DC - 1):
                      e.matmul(o, P.ones_c[:, :], i, start=st, stop=sp),
                      waits=[P.xT.evt, g1, P.ev_ones],
                      inc=(c == DC - 1))
            pe2 = P.op("tensor",
                       lambda e, o=sx2, i=xsq[:, 0:tl], st=(c == 0),
                       sp=(c == DC - 1):
                       e.matmul(o, P.ones_c[:, :], i, start=st, stop=sp),
                       waits=[ev_sq, g2], inc=True)
        mrow = rows[:, 0:tl]
        vrow = rows[:, 512:512 + tl]
        rrow = rows[:, 1024:1024 + tl]
        nrow = rows[:, 1536:1536 + tl]
        e1 = P.op("vector", lambda e, o=mrow, i=sx: e.tensor_scalar(
            o, i, 1.0 / D, None, OP.mult), waits=[pe], inc=True)
        P.op("vector", lambda e, o=vrow, i=sx2: e.tensor_scalar(
            o, i, 1.0 / D, None, OP.mult), waits=[pe2])
        P.op("vector", lambda e, o=nrow, i=mrow: e.scalar_tensor_tensor(
            o, i, -1.0, i, OP.mult, OP.mult))
        P.op("vector", lambda e, o=vrow, i=nrow: e.tensor_tensor(
            o, o, i, OP.add))
        e3 = P.op("vector", lambda e, o=vrow: e.tensor_scalar(
            o, o, EPS_LN, None, OP.add), inc=True)
        P.op("scalar", lambda e, o=vrow: e.activation(o, o, AF.Ln),
             waits=[e3])
        e5 = P.op("scalar", lambda e, o=rrow, i=vrow: e.activation(
            o, i, AF.Exp, scale=-0.5), inc=True)
        e6 = P.op("vector", lambda e, o=nrow, i=mrow: e.tensor_scalar(
            o, i, -1.0, None, OP.mult), waits=[e1], inc=True)
        P.pool.release(i1, e6)
        bank3, g3, i3 = P.pool.acquire()
        bank4, g4, i4 = P.pool.acquire()
        pb1 = P.op("tensor", lambda e, o=bank3[:, 0:tl], r=nrow: e.matmul(
            o, P.ones_f[:, :], r, start=True, stop=True),
            waits=[e6, g3, P.ev_ones], inc=True)
        pb2 = P.op("tensor", lambda e, o=bank4[:, 0:tl], r=rrow: e.matmul(
            o, P.ones_f[:, :], r, start=True, stop=True),
            waits=[e5, g4], inc=True)
        P.pool.release(i2, pb2)
        c1 = P.op("vector", lambda e, o=nmB[:, 0:tl], i=bank3[:, 0:tl]:
                  e.tensor_copy(o, i), waits=[pb1], inc=True)
        c2 = P.op("vector", lambda e, o=rsB[:, 0:tl], i=bank4[:, 0:tl]:
                  e.tensor_copy(o, i), waits=[pb2], inc=True)
        P.pool.release(i3, c1)
        P.pool.release(i4, c2)
        last = None
        for c in range(DC):
            xc = sq(P.xT[:, c, t0:t0 + tl])
            oc_ = sq(out_xhat[:, c, t0 - toff:t0 - toff + tl])
            P.op("vector", lambda e, o=tb[:, 0:tl], i=xc, m=nmB[:, 0:tl]:
                 e.tensor_tensor(o, i, m, OP.add), waits=[P.xT.evt, c1])
            last = P.op("vector", lambda e, o=oc_, i=tb[:, 0:tl],
                        r=rsB[:, 0:tl]:
                        e.tensor_tensor(o, i, r, OP.mult),
                        waits=[c2], inc=True)
        tile_evts.append(last)
    out_xhat.evt = P.now("vector")
    P.end_phase()
    les.close()
    return tile_evts


def emit_routing(P, dt_in, layer, tiles, xhat, cT, les_, wgB, ev_wg,
                 toff=0):
    nc = P.nc
    les = ExitStack()
    s = les.enter_context(nc.sbuf_tensor(P.uid("rt_s"), [128, 8], F32))
    pr = les.enter_context(nc.sbuf_tensor(P.uid("rt_pr"), [128, 8], F32))
    eq = les.enter_context(nc.sbuf_tensor(P.uid("rt_eq"), [128, 8], F32))
    aa = les.enter_context(nc.sbuf_tensor(P.uid("rt_aa"), [128, 8], F32))
    bb = les.enter_context(nc.sbuf_tensor(P.uid("rt_bb"), [128, 8], F32))
    f1 = les.enter_context(nc.sbuf_tensor(P.uid("rt_f1"), [128, 8], F32))
    f2 = les.enter_context(nc.sbuf_tensor(P.uid("rt_f2"), [128, 8], F32))
    cw = les.enter_context(nc.sbuf_tensor(P.uid("rt_cw"), [128, 8], F32))
    cwb = les.enter_context(nc.sbuf_tensor(P.uid("rt_cwb"), [128, 8], BF16))
    col = les.enter_context(nc.sbuf_tensor(P.uid("rt_col"), [128, 4], F32))

    def cumsum_excl():
        P.op("vector", lambda e: e.memset(aa[:, 0:1], 0.0))
        P.op("vector", lambda e: e.tensor_copy(aa[:, 1:8], eq[:, 0:7]))
        P.op("vector", lambda e: e.tensor_copy(bb[:, :], aa[:, :]))
        P.op("vector", lambda e: e.tensor_tensor(
            bb[:, 2:8], aa[:, 2:8], aa[:, 0:6], OP.add))
        P.op("vector", lambda e: e.tensor_copy(aa[:, :], bb[:, :]))
        P.op("vector", lambda e: e.tensor_tensor(
            aa[:, 4:8], bb[:, 4:8], bb[:, 0:4], OP.add))

    prev_act = None
    for (t0, tl) in tiles:
        for k in range(tl // 128):
            tt = t0 + k * 128
            bank, gate, bi = P.pool.acquire()
            lg = bank[:, 0:8]
            pe = None
            for dc in range(DC):
                pe = P.op("tensor",
                          lambda e, o=lg,
                          l=sq(xhat[:, dc, tt - toff:tt - toff + 128]),
                          r=sq(wgB[:, dc, :]), st=(dc == 0),
                          sp=(dc == DC - 1):
                          e.matmul(o, l, r, start=st, stop=sp),
                          waits=[xhat.evt, ev_wg, gate],
                          inc=(dc == DC - 1))
            if layer == 0:
                m = P.op("vector", lambda e, i=lg: e.reduce_max(
                    col[:, 0:1], i, axis=AX, negate=True),
                    waits=[pe, prev_act], inc=True)
                ex = P.op("scalar", lambda e, i=lg: e.activation(
                    pr[:, :], i, AF.Exp, bias=col[:, 0:1],
                    accum_out=col[:, 1:2]), waits=[pe, m], inc=True)
                P.op("scalar", lambda e: e.activation(
                    col[:, 2:3], col[:, 1:2], AF.Ln))
                iv = P.op("scalar", lambda e: e.activation(
                    col[:, 2:3], col[:, 2:3], AF.Exp, scale=-1.0), inc=True)
                P.op("vector", lambda e: e.tensor_scalar(
                    s[:, :], pr[:, :], col[:, 2:3], None, OP.mult),
                    waits=[ex, iv])
            else:
                P.op("vector", lambda e, i=lg: e.tensor_copy(s[:, :], i),
                     waits=[pe, prev_act])
            P.op("vector", lambda e: e.reduce_max(col[:, 0:1], s[:, :],
                                                  axis=AX))
            P.op("vector", lambda e: e.tensor_scalar(
                eq[:, :], s[:, :], col[:, 0:1], 0.0, OP.subtract, OP.is_ge))
            cumsum_excl()
            P.op("vector", lambda e: e.tensor_scalar(
                f1[:, :], aa[:, :], 0.0, None, OP.is_equal))
            P.op("vector", lambda e: e.tensor_tensor(
                f1[:, :], f1[:, :], eq[:, :], OP.mult))
            P.op("vector", lambda e: e.scalar_tensor_tensor(
                pr[:, :], f1[:, :], NEG, s[:, :], OP.mult, OP.add))
            P.op("vector", lambda e: e.reduce_max(col[:, 1:2], pr[:, :],
                                                  axis=AX))
            P.op("vector", lambda e: e.tensor_scalar(
                eq[:, :], pr[:, :], col[:, 1:2], 0.0, OP.subtract, OP.is_ge))
            cumsum_excl()
            P.op("vector", lambda e: e.tensor_scalar(
                f2[:, :], aa[:, :], 0.0, None, OP.is_equal))
            P.op("vector", lambda e: e.tensor_tensor(
                f2[:, :], f2[:, :], eq[:, :], OP.mult))
            if layer == 0:
                P.op("vector", lambda e: e.tensor_tensor(
                    f1[:, :], f1[:, :], f2[:, :], OP.add))
                P.op("vector", lambda e: e.tensor_tensor(
                    cw[:, :], f1[:, :], s[:, :], OP.mult))
                cl = P.op("vector", lambda e: e.tensor_scalar(
                    cw[:, :], cw[:, :], 0.5, None, OP.mult), inc=True)
            else:
                dd = P.op("vector", lambda e: e.tensor_tensor(
                    col[:, 2:3], col[:, 1:2], col[:, 0:1], OP.subtract),
                    inc=True)
                x1 = P.op("scalar", lambda e: e.activation(
                    col[:, 2:3], col[:, 2:3], AF.Exp), waits=[dd], inc=True)
                a1 = P.op("vector", lambda e: e.tensor_scalar(
                    col[:, 2:3], col[:, 2:3], 1.0, None, OP.add),
                    waits=[x1], inc=True)
                P.op("scalar", lambda e: e.activation(
                    col[:, 2:3], col[:, 2:3], AF.Ln), waits=[a1])
                g1 = P.op("scalar", lambda e: e.activation(
                    col[:, 2:3], col[:, 2:3], AF.Exp, scale=-1.0), inc=True)
                P.op("vector", lambda e: e.tensor_scalar(
                    col[:, 2:3], col[:, 2:3], 0.5, None, OP.mult),
                    waits=[g1])
                P.op("vector", lambda e: e.tensor_scalar(
                    col[:, 3:4], col[:, 2:3], -1.0, 0.5, OP.mult, OP.add))
                P.op("vector", lambda e: e.tensor_scalar(
                    f1[:, :], f1[:, :], col[:, 2:3], None, OP.mult))
                P.op("vector", lambda e: e.tensor_scalar(
                    f2[:, :], f2[:, :], col[:, 3:4], None, OP.mult))
                cl = P.op("vector", lambda e: e.tensor_tensor(
                    cw[:, :], f1[:, :], f2[:, :], OP.add), inc=True)
            P.pool.release(bi, cl)
            cb = P.op("vector", lambda e: e.tensor_copy(cwb[:, :], cw[:, :]),
                      inc=True)
            emit_transpose(P, cT[0:8, tt:tt + 128], cwb[:, :], waits=[cb])
            prev_act = P.now("scalar")
    if DEBUG and getattr(P, "_rt_dumped", False) is False:
        P._rt_dumped = True
        ve_now = P.now("vector")
        for nm, tt_ in (("rs", s), ("rf1", f1), ("rf2", f2), ("rcw", cw),
                        ("raa", aa), ("req", eq), ("rcol", col)):
            dd = nc.dram_tensor("dbg_" + nm, list(tt_.t.shape
                                if hasattr(tt_, 't') else
                                [128, tt_.free_size]), F32,
                                kind="ExternalOutput") \
                if False else nc.dram_tensor(
                "dbg_" + nm, [128, 8 if nm != "rcol" else 4], F32,
                kind="ExternalOutput")
            P.dma("sync", dd[:, :], tt_[:, :], waits=[ve_now])
    cT.evt = P.now("vector")
    P.end_phase()
    les.close()


def emit_ffn(P, dt_in, layer, halves):
    nc = P.nc
    with ExitStack() as les:
        hw = max(sum(tl for _, tl in tg) for tg in halves)
        xhat = Buf(les.enter_context(
            nc.sbuf_tensor(P.uid("ffn_xhat"), [128, DC, hw], BF16)))
        z = Buf(les.enter_context(
            nc.sbuf_tensor(P.uid("ffn_z"), [128, HFC, hw], BF16)))
        acc = les.enter_context(
            nc.sbuf_tensor(P.uid("ffn_acc"), [128, DC, hw], BF16))
        cT = Buf(les.enter_context(nc.sbuf_tensor(P.uid("ffn_cT"), [8, NT], BF16)))
        gBs = les.enter_context(nc.sbuf_tensor(P.uid("ffn_gB"), [128, 3, 512], F32))
        tbf = les.enter_context(nc.sbuf_tensor(P.uid("ffn_tbf"), [128, 512], BF16))
        sres = les.enter_context(nc.sbuf_tensor(P.uid("ffn_sres"), [128, 512], F32))
        tmg = les.enter_context(nc.sbuf_tensor(P.uid("ffn_tmg"), [128, 512], F32))
        eselb = les.enter_context(nc.sbuf_tensor(P.uid("eselb"), [8, 8, 128], BF16))
        wgT = les.enter_context(nc.sbuf_tensor(P.uid("rt_wgT"), [8, D], BF16))
        wgB = les.enter_context(nc.sbuf_tensor(P.uid("rt_wgB"), [128, DC, 8], BF16))
        nslot = 6 if layer == 1 else 5
        slots = [les.enter_context(
            nc.sbuf_tensor(P.uid("wslot"), [128, 4096], BF16))
            for i in range(nslot)]
        slot_free = [None] * nslot
        slot_ready = [None] * nslot
        slot_i = [0]

        ev_es = P.dma("gpsimd", eselb[:, :, :],
                      dt_in["esel"].ap().rearrange("e (g f) -> e g f", f=128))
        ev_wgT = P.dma("gpsimd", wgT[:, :],
                       dt_in["gWg" if layer == 0 else "hWg"][:, :])
        for dc in range(DC):
            emit_transpose(P, sq(wgB[:, dc, :]),
                           wgT[:, dc * 128:(dc + 1) * 128], waits=[ev_wgT])
        ev_wg = P.now("vector")

        if layer == 0:
            w1ap = dt_in["gW1"]
            w2ap = dt_in["gW2"]
            b1sb = [les.enter_context(
                nc.sbuf_tensor(P.uid("b1sb"), [1, HF], BF16)) for i in range(2)]
            b1_ready = [None, None]
            b1_free = [None, None]
            b2sb = les.enter_context(nc.sbuf_tensor(P.uid("b2sb"), [NE, D], BF16))
            ev_b2 = P.dma("gpsimd", b2sb[:, :], dt_in["gb2"][:, :])
            b1ap = dt_in["gb1"].ap().rearrange("o (e h) -> o e h", e=NE)
        else:
            w1ap = dt_in["hW1"]
            w3ap = dt_in["hW3"]
            w2ap = dt_in["hW2"]
            ev_b1 = ev_b2 = None

        def slot_load(src_ap, split):
            i = slot_i[0]
            slot_i[0] = (slot_i[0] + 1) % len(slots)
            ev = P.dma("gpsimd", slots[i][:, :], src_ap,
                       waits=[slot_free[i]])
            slot_ready[i] = ev
            return i

        def w1_slab(w, e, qq):
            return sq(w[e:e + 1, qq, :, :])

        def w2_slab(e, qq):
            return sq(w2ap[e:e + 1, qq, :, :])

        for half, tiles in enumerate(halves):
            t0h = tiles[0][0]
            sched = []
            for e in range(NE):
                for qq in range(4):
                    sched.append(("w1", e, qq))
                    if layer == 1:
                        sched.append(("w3", e, qq))
                for qq in range(4):
                    sched.append(("w2", e, qq))
            slot_of = {}
            nload = [0]

            def load_next():
                if nload[0] >= len(sched):
                    return
                kind, e, qq = sched[nload[0]]
                if kind == "w1":
                    src, split = w1_slab(w1ap, e, qq), False
                elif kind == "w3":
                    src, split = w1_slab(w3ap, e, qq), False
                else:
                    src, split = w2_slab(e, qq), True
                slot_of[(kind, e, qq)] = slot_load(src, split)
                nload[0] += 1

            for _ in range(min(nslot, len(sched))):
                load_next()

            xh_evts = emit_ln(P, tiles, xhat, toff=t0h)
            emit_routing(P, dt_in, layer, tiles, xhat, cT, les, wgB, ev_wg,
                         toff=t0h)
            if DEBUG and half == 0:
                d1 = nc.dram_tensor("dbg_xhat", [128, DC * hw], BF16,
                                    kind="ExternalOutput")
                P.dma("sync", d1[:, :],
                      xhat.t[:, :, :].rearrange("p a b -> p (a b)"),
                      waits=[xhat.evt])
                d2 = nc.dram_tensor("dbg_cT", [8, NT], BF16,
                                    kind="ExternalOutput")
                P.dma("sync", d2[:, :], cT.t[:, :], waits=[cT.evt])

            last_ze = None
            if layer == 0:
                b1_ready[0] = P.dma("gpsimd", b1sb[0][:, :], row(b1ap[:, 0:1, :]))
            for e in range(NE):
                if layer == 0 and e + 1 < NE:
                    b1_ready[(e + 1) % 2] = P.dma(
                        "gpsimd", b1sb[(e + 1) % 2][:, :],
                        row(b1ap[:, e + 1:e + 2, :]), waits=[b1_free[(e + 1) % 2]])
                gev = []
                for ti, (t0, tl) in enumerate(tiles):
                    bank, gate, bi = P.pool.acquire()
                    pe = P.op("tensor", lambda e_, o=bank[:, 0:tl],
                              l=sq(eselb[:, e:e + 1, :]),
                              r=cT[0:8, t0:t0 + tl]:
                              e_.matmul(o, l, r, start=True, stop=True),
                              waits=[cT.evt, ev_es, gate], inc=True)
                    ge = P.op("vector", lambda e_, o=sq(gBs[:, ti, 0:tl]),
                              i=bank[:, 0:tl]: e_.tensor_copy(o, i),
                              waits=[pe], inc=True)
                    gev.append(ge)
                    P.pool.release(bi, ge)

                for qq in range(4):
                    si1 = slot_of[("w1", e, qq)]
                    sl1 = slots[si1][:, :].rearrange("p (a b) -> p a b", a=8)
                    if layer == 1:
                        si3 = slot_of[("w3", e, qq)]
                        sl3 = slots[si3][:, :].rearrange("p (a b) -> p a b",
                                                         a=8)
                    for hci in range(4):
                        hc = qq * 4 + hci
                        nb = len(tiles)
                        bks = [P.pool.acquire() for _ in range(nb)]
                        pe1_t = [None] * len(tiles)
                        for dc in range(DC):
                            for ti, (t0, tl) in enumerate(tiles):
                                bank, gate, bi = bks[ti]
                                lastmm = (dc == DC - 1) and (layer == 1)
                                pe1_t[ti] = P.op(
                                    "tensor",
                                    lambda e_, o=bank[:, 0:tl],
                                    l=sq(sl1[:, dc,
                                             hci * 128:(hci + 1) * 128]),
                                    r=sq(xhat[:, dc, t0 - t0h:t0 - t0h + tl]),
                                    st=(dc == 0), sp=lastmm:
                                    e_.matmul(o, l, r, start=st, stop=sp),
                                    waits=[slot_ready[si1], xh_evts[ti],
                                           gate],
                                    inc=lastmm)
                        if layer == 0:
                            for ti, (t0, tl) in enumerate(tiles):
                                bank, gate, bi = bks[ti]
                                pe = P.op(
                                    "tensor",
                                    lambda e_, o=bank[:, 0:tl],
                                    l=b1sb[e % 2][0:1,
                                                  hc * 128:(hc + 1) * 128],
                                    r=P.ones_r[:, 0:tl]:
                                    e_.matmul(o, l, r, start=False,
                                              stop=True),
                                    waits=[b1_ready[e % 2], P.ev_ones],
                                    inc=True)
                                ze = P.op(
                                    "scalar",
                                    lambda e_,
                                    o=sq(z[:, hc, t0 - t0h:t0 - t0h + tl]),
                                    i=bank[:, 0:tl]:
                                    e_.activation(o, i, AF.Relu),
                                    waits=[pe], inc=True)
                                P.pool.release(bi, ze)
                        else:
                            bks2 = [P.pool.acquire() for _ in range(nb)]
                            pe2_t = [None] * nb
                            for dc in range(DC):
                                for ti, (t0, tl) in enumerate(tiles):
                                    bank2, gate2, bi2 = bks2[ti]
                                    pe2_t[ti] = P.op(
                                        "tensor",
                                        lambda e_, o=bank2[:, 0:tl],
                                        l=sq(sl3[:, dc,
                                                 hci * 128:(hci + 1) * 128]),
                                        r=sq(xhat[:, dc,
                                                  t0 - t0h:t0 - t0h + tl]),
                                        st=(dc == 0), sp=(dc == DC - 1):
                                        e_.matmul(o, l, r, start=st, stop=sp),
                                        waits=[slot_ready[si3], xh_evts[ti],
                                               gate2],
                                        inc=(dc == DC - 1))
                            for ti, (t0, tl) in enumerate(tiles):
                                bank, gate, bi = bks[ti]
                                bank2, gate2, bi2 = bks2[ti]
                                pe = pe1_t[ti]
                                ge_ = P.op(
                                    "scalar",
                                    lambda e_, o=tmg[:, 0:tl],
                                    i=bank[:, 0:tl]:
                                    e_.activation(o, i, AF.Silu),
                                    waits=[pe, last_ze], inc=True)
                                ze = P.op(
                                    "vector",
                                    lambda e_,
                                    o=sq(z[:, hc, t0 - t0h:t0 - t0h + tl]),
                                    a=tmg[:, 0:tl], b=bank2[:, 0:tl]:
                                    e_.tensor_tensor(o, a, b, OP.mult),
                                    waits=[ge_, pe2_t[ti]], inc=True)
                                last_ze = ze
                                P.pool.release(bi, ge_)
                                P.pool.release(bi2, ze)
                    slot_free[si1] = P.now("tensor")
                    if layer == 1:
                        slot_free[si3] = P.now("tensor")
                    load_next()
                    if layer == 1:
                        load_next()
                if layer == 0:
                    b1_free[e % 2] = P.now("tensor")
                z.evt = P.now("vector" if layer == 1 else "scalar")
                if DEBUG and half == 0 and e == 0:
                    d3 = nc.dram_tensor("dbg_z0", [128, HFC * hw], BF16,
                                        kind="ExternalOutput")
                    P.dma("sync", d3[:, :],
                          z.t[:, :, :].rearrange("p a b -> p (a b)"),
                          waits=[z.evt])

                for qq in range(4):
                    si2 = slot_of[("w2", e, qq)]
                    sl2 = slots[si2][:, :].rearrange("p (a b) -> p a b", a=16)
                    for oci in range(2):
                        oc = qq * 2 + oci
                        nb = len(tiles)
                        bks = [P.pool.acquire() for _ in range(nb)]
                        pe_t = [None] * nb
                        for fc in range(HFC):
                            for ti, (t0, tl) in enumerate(tiles):
                                bank, gate, bi = bks[ti]
                                pe_t[ti] = P.op(
                                    "tensor",
                                    lambda e_, o=bank[:, 0:tl],
                                    l=sq(sl2[:, fc,
                                             oci * 128:(oci + 1) * 128]),
                                    r=sq(z[:, fc, t0 - t0h:t0 - t0h + tl]),
                                    st=(fc == 0), sp=(fc == HFC - 1):
                                    e_.matmul(o, l, r, start=st, stop=sp),
                                    waits=[slot_ready[si2], z.evt, gate],
                                    inc=(fc == HFC - 1))
                        for ti, (t0, tl) in enumerate(tiles):
                            bank, gate, bi = bks[ti]
                            accv = sq(acc[:, oc, t0 - t0h:t0 - t0h + tl])
                            if e == 0:
                                ae = P.op("vector",
                                          lambda e_, o=accv, i=bank[:, 0:tl],
                                          g=sq(gBs[:, ti, 0:tl]):
                                          e_.tensor_tensor(o, i, g, OP.mult),
                                          waits=[pe_t[ti], gev[ti]],
                                          inc=True)
                            else:
                                P.op("vector",
                                     lambda e_, o=tbf[:, 0:tl],
                                     i=bank[:, 0:tl], g=sq(gBs[:, ti, 0:tl]):
                                     e_.tensor_tensor(o, i, g, OP.mult),
                                     waits=[pe_t[ti], gev[ti]])
                                ae = P.op("vector",
                                          lambda e_, o=accv, i=tbf[:, 0:tl]:
                                          e_.tensor_tensor(o, o, i, OP.add),
                                          inc=True)
                            P.pool.release(bi, ae)
                    slot_free[si2] = P.now("tensor")
                    load_next()
            acc_evt = P.now("vector")
            if DEBUG and half == 0:
                d4 = nc.dram_tensor("dbg_acc", [128, DC * hw], BF16,
                                    kind="ExternalOutput")
                P.dma("sync", d4[:, :],
                      acc[:, :, :].rearrange("p a b -> p (a b)"),
                      waits=[acc_evt])

            for ti, (t0, tl) in enumerate(tiles):
                for oc in range(DC):
                    accv = sq(acc[:, oc, t0 - t0h:t0 - t0h + tl])
                    xv = sq(P.xT[:, oc, t0:t0 + tl])
                    if layer == 0:
                        bank, gate, bi = P.pool.acquire()
                        pe = P.op("tensor",
                                  lambda e_, o=bank[:, 0:tl],
                                  l=b2sb[:, oc * 128:(oc + 1) * 128],
                                  r=cT[0:8, t0:t0 + tl]:
                                  e_.matmul(o, l, r, start=True, stop=True),
                                  waits=[ev_b2, cT.evt, gate], inc=True)
                        P.op("vector", lambda e_, o=sres[:, 0:tl], a=xv,
                             b=accv: e_.tensor_tensor(o, a, b, OP.add),
                             waits=[acc_evt, P.xT.evt])
                        re = P.op("vector",
                                  lambda e_, o=xv, a=sres[:, 0:tl],
                                  b=bank[:, 0:tl]:
                                  e_.tensor_tensor(o, a, b, OP.add),
                                  waits=[pe], inc=True)
                        P.pool.release(bi, re)
                    else:
                        P.op("vector",
                             lambda e_, o=xv, a=xv, b=accv:
                             e_.tensor_tensor(o, a, b, OP.add),
                             waits=[acc_evt, P.xT.evt], inc=True)
            P.xT.evt = P.now("vector")


def emit_attn_l0(P, dt_in):
    nc = P.nc
    with ExitStack() as les:
        xhat = Buf(les.enter_context(
            nc.sbuf_tensor(P.uid("a0_xhat"), [128, DC, NT], BF16)))
        aoT = Buf(les.enter_context(
            nc.sbuf_tensor(P.uid("a0_aoT"), [128, DC, NT_OWN], BF16)))
        cosT = les.enter_context(nc.sbuf_tensor(P.uid("a0_cos"), [128, NT], F32))
        sinT = les.enter_context(nc.sbuf_tensor(P.uid("a0_sin"), [128, NT], F32))
        pswb = les.enter_context(nc.sbuf_tensor(P.uid("a0_psw"), [128, 128], BF16))
        mask0 = les.enter_context(nc.sbuf_tensor(P.uid("a0_m0"), [128, 256], F32))
        maskr = les.enter_context(nc.sbuf_tensor(P.uid("a0_mr"), [128, 256], F32))
        wsl = [les.enter_context(
            nc.sbuf_tensor(P.uid("a0_w"), [128, DC, 128], BF16))
            for i in range(3)]
        woT = les.enter_context(nc.sbuf_tensor(P.uid("a0_wo"), [128, DC, D], BF16))
        bqkv = [les.enter_context(nc.sbuf_tensor(P.uid("a0_bias"),
                                                 [1, 3, 128], BF16))
                for i in range(2)]
        bo_sb = les.enter_context(nc.sbuf_tensor(P.uid("a0_bo"), [1, D],
                                                 BF16))
        qT = les.enter_context(nc.sbuf_tensor(P.uid("a0_qT"), [128, NT_OWN], BF16))
        kT = les.enter_context(nc.sbuf_tensor(P.uid("a0_kT"), [128, NT], BF16))
        v2 = les.enter_context(nc.sbuf_tensor(P.uid("a0_v2"), [128, 17, 128], BF16))
        qtmp = les.enter_context(nc.sbuf_tensor(P.uid("a0_qtmp"), [128, 512], BF16))
        rt1 = les.enter_context(nc.sbuf_tensor(P.uid("a0_rt1"), [128, 512], F32))
        rt2 = les.enter_context(nc.sbuf_tensor(P.uid("a0_rt2"), [128, 512], F32))
        sm = [les.enter_context(nc.sbuf_tensor(P.uid("a0_sm"), [128, 256],
                                               F32)) for _ in range(2)]
        exs = [les.enter_context(nc.sbuf_tensor(P.uid("a0_exs"), [128, 256],
                                                F32)) for _ in range(2)]
        prb = [les.enter_context(nc.sbuf_tensor(P.uid("a0_prb"), [128, 256],
                                                BF16)) for _ in range(2)]
        prT = [les.enter_context(nc.sbuf_tensor(P.uid("a0_prT"), [128, 256],
                                                BF16)) for _ in range(2)]
        srow = [les.enter_context(nc.sbuf_tensor(P.uid("a0_srow"), [128, 4],
                                                 F32)) for _ in range(2)]
        tbf = les.enter_context(nc.sbuf_tensor(P.uid("a0_tbf"), [128, 512], BF16))

        ev_cos = P.dma("sync", cosT[:, :], dt_in["ropecos"][:, :])
        ev_sin = P.dma("sync", sinT[:, :], dt_in["ropesin"][:, :])
        ev_psw = P.dma("gpsimd", pswb[:, :], dt_in["pswap"][:, :])
        ev_m0 = P.dma("sync", mask0[:, :], dt_in["mask0"][:, :])
        ev_mr = P.dma("sync", maskr[:, :], dt_in["maskr"][:, :])
        ev_wo = P.dma("gpsimd", woT[:, :, :],
                      dt_in["aWoT"].ap().rearrange("(fc p) o -> p fc o",
                                                   p=128))
        ev_bo = P.dma("gpsimd", bo_sb[:, :], dt_in["abo"][:, :])

        emit_ln(P, [t for tg in TILES_S1 for t in tg], xhat)

        wfree = None
        bias_free = [None, None]
        prev_soft = [None, None]
        for hp in range(8):
            evw = []
            for i, n in enumerate(("aWqT", "aWkT", "aWvT")):
                src = dt_in[n].ap().rearrange("(dc p) f -> p dc f", p=128)[
                    :, :, hp * 128:(hp + 1) * 128]
                evw.append(P.dma("gpsimd", wsl[i][:, :, :], src,
                                 waits=[wfree]))
            bb_ = bqkv[hp % 2]
            ev_b = [P.dma("gpsimd", row(bb_[:, i:i + 1, :]),
                          dt_in[n][0:1, hp * 128:(hp + 1) * 128],
                          waits=[bias_free[hp % 2]])
                    for i, n in enumerate(("abq", "abk", "abv"))]
            for which in (0, 1):
                tiles = [t for tg in (TILES_OWN if which == 0 else TILES_S1)
                         for t in tg]
                dstT = qT if which == 0 else kT
                off = HALO if which == 0 else 0
                scale = 0.125 if which == 0 else 1.0
                for (t0, tl) in tiles:
                    bank, gate, bi = P.pool.acquire()
                    for dc in range(DC):
                        P.op("tensor",
                             lambda e_, o=bank[:, 0:tl],
                             l=sq(wsl[which][:, dc, :]),
                             r=sq(xhat[:, dc, t0:t0 + tl]), st=(dc == 0):
                             e_.matmul(o, l, r, start=st, stop=False),
                             waits=[evw[which], xhat.evt, gate])
                    pe = P.op("tensor",
                              lambda e_, o=bank[:, 0:tl],
                              l=row(bb_[:, which:which + 1, :]),
                              r=P.ones_r[:, 0:tl]:
                              e_.matmul(o, l, r, start=False, stop=True),
                              waits=[ev_b[which], P.ev_ones], inc=True)
                    cb = P.op("vector",
                              lambda e_, o=qtmp[:, 0:tl], i=bank[:, 0:tl]:
                              e_.tensor_copy(o, i), waits=[pe], inc=True)
                    bank2, gate2, bi2 = P.pool.acquire()
                    pes = P.op("tensor",
                               lambda e_, o=bank2[:, 0:tl],
                               r=qtmp[:, 0:tl]:
                               e_.matmul(o, pswb[:, :], r, start=True,
                                         stop=True),
                               waits=[cb, ev_psw, gate2], inc=True)
                    P.op("vector",
                         lambda e_, o=rt1[:, 0:tl], i=bank[:, 0:tl],
                         c=cosT[:, t0:t0 + tl], sc=scale:
                         e_.scalar_tensor_tensor(o, i, sc, c, OP.mult,
                                                 OP.mult),
                         waits=[ev_cos])
                    P.op("vector",
                         lambda e_, o=rt2[:, 0:tl], i=bank2[:, 0:tl],
                         c=sinT[:, t0:t0 + tl], sc=scale:
                         e_.scalar_tensor_tensor(o, i, sc, c, OP.mult,
                                                 OP.mult),
                         waits=[pes, ev_sin])
                    ve2 = P.op("vector",
                               lambda e_, o=dstT[:, t0 - off:t0 - off + tl],
                               a=rt1[:, 0:tl], b=rt2[:, 0:tl]:
                               e_.tensor_tensor(o, a, b, OP.add),
                               inc=True)
                    P.pool.release(bi, ve2)
                    P.pool.release(bi2, ve2)
            ev_q = P.now("vector")
            for tc in range(17):
                bank, gate, bi = P.pool.acquire()
                for dc in range(DC):
                    P.op("tensor",
                         lambda e_, o=bank[:, 0:128],
                         l=sq(xhat[:, dc, tc * 128:(tc + 1) * 128]),
                         r=sq(wsl[2][:, dc, :]), st=(dc == 0):
                         e_.matmul(o, l, r, start=st, stop=False),
                         waits=[evw[2], xhat.evt, gate])
                pe = P.op("tensor",
                          lambda e_, o=bank[:, 0:128],
                          r=row(bb_[:, 2:3, :]):
                          e_.matmul(o, P.ones_r[0:1, 0:128], r, start=False,
                                    stop=True),
                          waits=[ev_b[2], P.ev_ones], inc=True)
                ve = P.op("vector",
                          lambda e_, o=sq(v2[:, tc, :]), i=bank[:, 0:128]:
                          e_.tensor_copy(o, i), waits=[pe], inc=True)
                P.pool.release(bi, ve)
            ev_v = P.now("vector")
            wfree = ev_v
            bias_free[hp % 2] = P.now("tensor")

            for hh in range(2):
                h64 = hh * 64
                for qb in range(16):
                    par = qb % 2
                    sm_, exs_, prb_, prT_, srow_ = (sm[par], exs[par],
                                                    prb[par], prT[par],
                                                    srow[par])
                    bank, gate, bi = P.pool.acquire()
                    pes = P.op("tensor",
                               lambda e_, o=bank[:, 0:256],
                               l=qT[h64:h64 + 64, qb * 128:(qb + 1) * 128],
                               r=kT[h64:h64 + 64, qb * 128:qb * 128 + 256]:
                               e_.matmul(o, l, r, start=True, stop=True),
                               waits=[ev_q, gate], inc=True)
                    mk = mask0 if qb == 0 else maskr
                    e1 = P.op("vector",
                              lambda e_, o=sm_, a=bank[:, 0:256], b=mk:
                              e_.tensor_tensor(o[:, :], a, b[:, :], OP.add),
                              waits=[pes, ev_m0, ev_mr,
                                     prev_soft[par]], inc=True)
                    P.pool.release(bi, e1)
                    e2 = P.op("vector",
                              lambda e_, o=srow_, i=sm_:
                              e_.reduce_max(o[:, 0:1], i[:, :], axis=AX,
                                            negate=True),
                              inc=True)
                    e3 = P.op("scalar",
                              lambda e_, o=exs_, i=sm_, s=srow_:
                              e_.activation(o[:, :], i[:, :], AF.Exp,
                                            bias=s[:, 0:1],
                                            accum_out=s[:, 1:2]),
                              waits=[e1, e2], inc=True)
                    P.op("scalar", lambda e_, s=srow_: e_.activation(
                        s[:, 2:3], s[:, 1:2], AF.Ln))
                    e4 = P.op("scalar", lambda e_, s=srow_: e_.activation(
                        s[:, 2:3], s[:, 2:3], AF.Exp, scale=-1.0),
                        inc=True)
                    e5 = P.op("vector", lambda e_, o=prb_, i=exs_, s=srow_:
                              e_.tensor_scalar(o[:, :], i[:, :], s[:, 2:3],
                                               None, OP.mult),
                              waits=[e3, e4], inc=True)
                    prev_soft[par] = e4
                    for b in range(2):
                        emit_transpose(P, prT_[:, b * 128:(b + 1) * 128],
                                       prb_[:, b * 128:(b + 1) * 128],
                                       waits=[e5])
                    eT = P.now("vector")
                    bank2, gate2, bi2 = P.pool.acquire()
                    pe2 = None
                    for b in range(2):
                        pe2 = P.op("tensor",
                                   lambda e_, o=bank2[0:64, 0:128],
                                   l=sq(v2[:, qb + b, h64:h64 + 64]),
                                   r=prT_[:, b * 128:(b + 1) * 128],
                                   st=(b == 0):
                                   e_.matmul(o, l, r, start=st,
                                             stop=(b == 1)),
                                   waits=[ev_v, eT, gate2], inc=(b == 1))
                    ve = P.op("vector",
                              lambda e_,
                              o=aoT.t[h64:h64 + 64, hp,
                                      qb * 128:(qb + 1) * 128].squeeze(),
                              i=bank2[0:64, 0:128]:
                              e_.tensor_copy(o, i), waits=[pe2], inc=True)
                    P.pool.release(bi2, ve)
        aoT.evt = P.now("vector")

        for (t0, tl) in [t for tg in TILES_OWN for t in tg]:
            for oc in range(DC):
                bank, gate, bi = P.pool.acquire()
                for fc in range(DC):
                    P.op("tensor",
                         lambda e_, o=bank[:, 0:tl],
                         l=sq(woT[:, fc, oc * 128:(oc + 1) * 128]),
                         r=sq(aoT[:, fc, t0 - HALO:t0 - HALO + tl]),
                         st=(fc == 0):
                         e_.matmul(o, l, r, start=st, stop=False),
                         waits=[ev_wo, aoT.evt, gate])
                pe = P.op("tensor",
                          lambda e_, o=bank[:, 0:tl],
                          l=bo_sb[0:1, oc * 128:(oc + 1) * 128],
                          r=P.ones_r[:, 0:tl]:
                          e_.matmul(o, l, r, start=False, stop=True),
                          waits=[ev_bo, P.ev_ones], inc=True)
                P.op("vector",
                     lambda e_, o=tbf[:, 0:tl], i=bank[:, 0:tl]:
                     e_.tensor_copy(o, i), waits=[pe])
                re = P.op("vector",
                          lambda e_, o=sq(P.xT[:, oc, t0:t0 + tl]),
                          i=tbf[:, 0:tl]:
                          e_.tensor_tensor(o, o, i, OP.add),
                          waits=[P.xT.evt], inc=True)
                P.pool.release(bi, re)
        P.xT.evt = P.now("vector")


def emit_elu(P, bank, tl, t1, t2, out_ap, pe):
    np_ = bank.shape[0]
    mn = P.op("vector", lambda e_, o=t1[0:np_, 0:tl], i=bank[0:np_, 0:tl]:
              e_.tensor_scalar(o, i, 0.0, None, OP.min),
              waits=[pe, getattr(P, "_elu_last", None)], inc=True)
    a1 = P.op("scalar", lambda e_, o=t1[0:np_, 0:tl]:
              e_.activation(o, o, AF.Exp), waits=[mn], inc=True)
    P.op("vector", lambda e_, o=t2[0:np_, 0:tl], i=bank[0:np_, 0:tl]:
         e_.tensor_scalar(o, i, 0.0, None, OP.max), waits=[pe])
    ve = P.op("vector", lambda e_, o=out_ap, a=t1[0:np_, 0:tl],
              b=t2[0:np_, 0:tl]:
              e_.tensor_tensor(o, a, b, OP.add), waits=[a1], inc=True)
    P._elu_last = a1
    return ve


def emit_attn_l1(P, dt_in):
    nc = P.nc
    with ExitStack() as les:
        xhat = Buf(les.enter_context(
            nc.sbuf_tensor(P.uid("a1_xhat"), [128, DC, NT_OWN], BF16)))
        wk = les.enter_context(nc.sbuf_tensor(P.uid("a1_wk"), [128, DC, D], BF16))
        wv = les.enter_context(nc.sbuf_tensor(P.uid("a1_wv"), [128, DC, D], BF16))
        wo = les.enter_context(nc.sbuf_tensor(P.uid("a1_wo"), [128, DC, D], BF16))
        biasr = les.enter_context(nc.sbuf_tensor(P.uid("a1_bias"), [1, 4, D], BF16))
        evk = P.dma("gpsimd", wk[:, :, :],
                    dt_in["lWk"].ap().rearrange("(dc p) f -> p dc f", p=128))
        evv = P.dma("gpsimd", wv[:, :, :],
                    dt_in["lWv"].ap().rearrange("(dc p) f -> p dc f", p=128))
        evo = P.dma("gpsimd", wo[:, :, :],
                    dt_in["lWo"].ap().rearrange("(fc p) o -> p fc o", p=128))
        ev_b = [P.dma("gpsimd", row(biasr[:, i:i + 1, :]), dt_in[n][:, :])
                for i, n in enumerate(("lbq", "lbk", "lbv", "lbo"))]

        emit_ln(P, [t for tg in TILES_OWN for t in tg], xhat, toff=HALO)

        kfc = les.enter_context(nc.sbuf_tensor(P.uid("a1_kfc"), [128, D], BF16))
        vc = les.enter_context(nc.sbuf_tensor(P.uid("a1_vc"), [128, D], BF16))
        t1 = les.enter_context(nc.sbuf_tensor(P.uid("a1_t1"), [128, 512], F32))
        t2 = les.enter_context(nc.sbuf_tensor(P.uid("a1_t2"), [128, 512], F32))
        kvs = les.enter_context(nc.sbuf_tensor(P.uid("a1_kvs"), [64, 1040], F32))
        kvz = les.enter_context(nc.sbuf_tensor(P.uid("a1_kvz"), [64, 1040], F32))
        kvb = les.enter_context(nc.sbuf_tensor(P.uid("a1_kvb"), [64, 1040], BF16))
        smk = les.enter_context(nc.sbuf_tensor(P.uid("a1_smk"), [128, 2], F32))
        dens = [les.enter_context(nc.sbuf_tensor(P.uid("a1_dens"),
                                                  [128, 16], F32))
                for _ in range(2)]
        tbf = les.enter_context(nc.sbuf_tensor(P.uid("a1_tbf"), [128, 512], BF16))
        ev_sm = P.dma("sync", smk[:, :], dt_in["seqmask"][:, :])

        wq = les.enter_context(
            nc.sbuf_tensor(P.uid("a1_wq"), [128, DC, D], BF16))
        evq = P.dma("gpsimd", wq[:, :, :],
                    dt_in["lWq"].ap().rearrange("(dc p) f -> p dc f", p=128))
        qfT = les.enter_context(
            nc.sbuf_tensor(P.uid("a1_qfT"), [64, NH, 512], BF16))
        yz = les.enter_context(nc.sbuf_tensor(P.uid("a1_yz"),
                                              [128, 4, D], BF16))
        yzT = les.enter_context(nc.sbuf_tensor(P.uid("a1_yzT"),
                                               [128, DC, 512], BF16))

        kvbankA, gA, iA = P.pool.acquire()
        kvbankB, gB_, iB = P.pool.acquire()
        kvbankC, gC_, iC = P.pool.acquire()
        started = [False, False, False]
        pe_kv = None
        for tc in range(16):
            trow = tc * 128
            for half in range(2):
                bank, gate, bi = P.pool.acquire()
                peh = None
                for dc in range(DC):
                    peh = P.op("tensor",
                               lambda e_, o=bank[:, 0:512],
                               l=sq(xhat[:, dc, trow:trow + 128]),
                               r=sq(wk[:, dc,
                                       half * 512:half * 512 + 512]),
                               st=(dc == 0):
                               e_.matmul(o, l, r, start=st, stop=False),
                               waits=[evk, xhat.evt, gate])
                peh = P.op("tensor",
                           lambda e_, o=bank[:, 0:512],
                           r=row(biasr[0:1, 1:2,
                                       half * 512:half * 512 + 512]):
                           e_.matmul(o, P.ones_r[0:1, 0:128], r, start=False,
                                     stop=True),
                           waits=[ev_b[1], P.ev_ones], inc=True)
                ve = emit_elu(P, bank, 512, t1, t2,
                              kfc[:, half * 512:half * 512 + 512], peh)
                P.pool.release(bi, ve)
                ev_kf = P.now("vector")
                bank2, gate2, bi2 = P.pool.acquire()
                peh = None
                for dc in range(DC):
                    peh = P.op("tensor",
                               lambda e_, o=bank2[:, 0:512],
                               l=sq(xhat[:, dc, trow:trow + 128]),
                               r=sq(wv[:, dc,
                                       half * 512:half * 512 + 512]),
                               st=(dc == 0):
                               e_.matmul(o, l, r, start=st, stop=False),
                               waits=[evv, xhat.evt, gate2])
                peh = P.op("tensor",
                           lambda e_, o=bank2[:, 0:512],
                           r=row(biasr[0:1, 2:3,
                                       half * 512:half * 512 + 512]):
                           e_.matmul(o, P.ones_r[0:1, 0:128], r, start=False,
                                     stop=True),
                           waits=[ev_b[2], P.ev_ones], inc=True)
                ve = P.op("vector",
                          lambda e_, o=vc[:, half * 512:half * 512 + 512],
                          i=bank2[:, 0:512]: e_.tensor_copy(o, i),
                          waits=[peh], inc=True)
                P.pool.release(bi2, ve)
                ev_vc = P.now("vector")
                for hh in range(8):
                    h = half * 8 + hh
                    bk = kvbankA if half == 0 else kvbankB
                    bix = half
                    st = not started[bix]
                    started[bix] = True
                    P.op("tensor",
                         lambda e_, o=bk[0:64, hh * 64:(hh + 1) * 64],
                         l=kfc[:, h * 64:(h + 1) * 64],
                         r=vc[:, h * 64:(h + 1) * 64], s=st:
                         e_.matmul(o, l, r, start=s, stop=False,
                                   skip_group_check=True),
                         waits=[ev_kf, ev_vc,
                                (gA if half == 0 else gB_) if st else None])
                    stC = not started[2]
                    started[2] = True
                    pe_kv = P.op("tensor",
                                 lambda e_, o=kvbankC[0:64, h:h + 1],
                                 l=kfc[:, h * 64:(h + 1) * 64], s=stC:
                                 e_.matmul(o, l, P.ones_c[:, 0:1], start=s,
                                           stop=(tc == 15 and hh == 7),
                                           skip_group_check=True),
                                 waits=[P.ev_ones, gC_ if stC else None],
                                 inc=(tc == 15 and h == NH - 1))
        kvs3 = kvs[:, :].rearrange("p (h c) -> p h c", c=65)
        ev1 = P.op("vector", lambda e_,
                   o=kvs3[:, 0:8, 0:64],
                   i=kvbankA[0:64, 0:512].rearrange("p (h c) -> p h c", c=64):
                   e_.tensor_copy(o, i), waits=[pe_kv], inc=True)
        ev2 = P.op("vector", lambda e_,
                   o=kvs3[:, 8:16, 0:64],
                   i=kvbankB[0:64, 0:512].rearrange("p (h c) -> p h c", c=64):
                   e_.tensor_copy(o, i), waits=[pe_kv], inc=True)
        ev3 = P.op("vector", lambda e_,
                   o=kvs3[:, 0:16, 64:65],
                   i=kvbankC[0:64, 0:16].rearrange("p (h c) -> p h c", c=1):
                   e_.tensor_copy(o, i), waits=[pe_kv], inc=True)
        if DEBUG:
            dkv = nc.dram_tensor("dbg_kvs", [64, 1040], F32,
                                 kind="ExternalOutput")
            P.dma("sync", dkv[:, :], kvs[:, :], waits=[ev1, ev2, ev3])
        P.pool.release(iA, ev1)
        P.pool.release(iB, ev2)
        P.pool.release(iC, ev3)
        z1 = P.op("vector", lambda e_: e_.tensor_scalar(
            kvz[:, :], kvs[:, :], smk[0:64, 0:1], None, OP.mult),
            waits=[ev1, ev2, ev3, ev_sm], inc=True)
        d1 = P.dma("sync", P.cc_in[0:64, :], kvz[:, :], waits=[z1])
        z2 = P.op("vector", lambda e_: e_.tensor_scalar(
            kvz[:, :], kvs[:, :], smk[0:64, 1:2], None, OP.mult),
            waits=[d1], inc=True)
        d2 = P.dma("sync", P.cc_in[64:128, :], kvz[:, :], waits=[z2])
        ccs = P.sems["cc"]
        ccs.n += 1
        P.q["gpsimd"].append((
            ((P.sems["dsy"], P.sems["dsy"].n),),
            lambda e: e.collective_compute(
                "AllReduce", OP.add,
                replica_groups=[list(range(NCORES))],
                ins=[P.cc_in.ap().opt()],
                outs=[P.cc_out.ap().opt()]),
            ((ccs, 1),), None))
        ev_cc = (ccs, ccs.n)
        l1 = P.dma("sync", kvz[:, :], P.cc_out[0:64, :], waits=[ev_cc])
        m1 = P.op("vector", lambda e_: e_.tensor_scalar(
            kvs[:, :], kvz[:, :], smk[0:64, 0:1], None, OP.mult),
            waits=[l1], inc=True)
        l2 = P.dma("sync", kvz[:, :], P.cc_out[64:128, :], waits=[m1])
        P.op("vector", lambda e_: e_.tensor_scalar(
            kvz[:, :], kvz[:, :], smk[0:64, 1:2], None, OP.mult), waits=[l2])
        P.op("vector", lambda e_: e_.tensor_tensor(
            kvs[:, :], kvs[:, :], kvz[:, :], OP.add))
        ev_kvb = P.op("vector", lambda e_: e_.tensor_copy(
            kvb[:, :], kvs[:, :]), inc=True)
        if DEBUG:
            dkb = nc.dram_tensor("dbg_kvb", [64, 1040], BF16,
                                 kind="ExternalOutput")
            P.dma("sync", dkb[:, :], kvb[:, :], waits=[ev_kvb])

        prev_z = None
        for (t0, tl) in [t for tg in TILES_OWN for t in tg]:
            for h in range(NH):
                bank, gate, bi = P.pool.acquire()
                for dc in range(DC):
                    P.op("tensor",
                         lambda e_, o=bank[0:64, 0:tl],
                         l=sq(wq[:, dc, h * 64:(h + 1) * 64]),
                         r=sq(xhat[:, dc, t0 - HALO:t0 - HALO + tl]),
                         st=(dc == 0):
                         e_.matmul(o, l, r, start=st, stop=False),
                         waits=[evq, xhat.evt, gate])
                pe = P.op("tensor",
                          lambda e_, o=bank[0:64, 0:tl],
                          l=row(biasr[0:1, 0:1, h * 64:(h + 1) * 64]),
                          r=P.ones_r[:, 0:tl]:
                          e_.matmul(o, l, r, start=False, stop=True),
                          waits=[ev_b[0], P.ev_ones], inc=True)
                ve = emit_elu(P, bank[0:64, :], tl, t1, t2,
                              sq(qfT[0:64, h, 0:tl]), pe)
                P.pool.release(bi, ve)
            ev_qf = P.now("vector")
            for k in range(tl // 128):
                par = k % 2
                dn = dens[par]
                bY = [P.pool.acquire() for _ in range(2)]
                bD = P.pool.acquire()
                peY = [None, None]
                peD = None
                for h in range(NH):
                    bk, gk, _ = bY[h // 8]
                    cb = (h % 8) * 64
                    st = (h % 8 == 0)
                    peY[h // 8] = P.op(
                        "tensor",
                        lambda e_, o=bk[:, cb:cb + 64],
                        l=sq(qfT[0:64, h, k * 128:(k + 1) * 128]),
                        r=kvb[0:64, h * 65:h * 65 + 64], s=st:
                        e_.matmul(o, l, r, start=s, stop=False,
                                  skip_group_check=True),
                        waits=[ev_qf, ev_kvb, gk if st else None],
                        inc=(h % 8 == 7))
                    peD = P.op(
                        "tensor",
                        lambda e_, o=bD[0][:, h:h + 1],
                        l=sq(qfT[0:64, h, k * 128:(k + 1) * 128]),
                        r=kvb[0:64, h * 65 + 64:h * 65 + 65], s=(h == 0):
                        e_.matmul(o, l, r, start=s, stop=(h == NH - 1),
                                  skip_group_check=True),
                        waits=[bD[1] if h == 0 else None],
                        inc=(h == NH - 1))
                zc = P.op("vector", lambda e_, o=dn, i=bD[0][:, 0:16]:
                          e_.tensor_scalar(o[:, :], i, EPS_LA, None, OP.add),
                          waits=[peD], inc=True)
                P.pool.release(bD[2], zc)
                P.op("scalar", lambda e_, o=dn: e_.activation(
                    o[:, :], o[:, :], AF.Ln), waits=[zc])
                za = P.op("scalar", lambda e_, o=dn: e_.activation(
                    o[:, :], o[:, :], AF.Exp, scale=-1.0), inc=True)
                last = [None, None]
                for h in range(NH):
                    bk, gk, bik = bY[h // 8]
                    cb = (h % 8) * 64
                    last[h // 8] = P.op(
                        "vector",
                        lambda e_, o=sq(yz[:, k, h * 64:(h + 1) * 64]),
                        i=bk[:, cb:cb + 64], s=dn[:, h:h + 1]:
                        e_.tensor_scalar(o, i, s, None, OP.mult),
                        waits=[za, peY[h // 8]], inc=True)
                for j in range(2):
                    P.pool.release(bY[j][2], last[j])
            for k in range(tl // 128):
                for fc in range(DC):
                    emit_transpose(P, sq(yzT[:, fc, k * 128:(k + 1) * 128]),
                                   sq(yz[:, k, fc * 128:(fc + 1) * 128]))
            ev_yzT = P.now("vector")
            for oc in range(DC):
                bank, gate, bi = P.pool.acquire()
                for fc in range(DC):
                    P.op("tensor",
                         lambda e_, o=bank[:, 0:tl],
                         l=sq(wo[:, fc, oc * 128:(oc + 1) * 128]),
                         r=sq(yzT[:, fc, 0:tl]), st=(fc == 0):
                         e_.matmul(o, l, r, start=st, stop=False),
                         waits=[evo, ev_yzT, gate])
                pe = P.op("tensor",
                          lambda e_, o=bank[:, 0:tl],
                          l=row(biasr[0:1, 3:4, oc * 128:(oc + 1) * 128]),
                          r=P.ones_r[:, 0:tl]:
                          e_.matmul(o, l, r, start=False, stop=True),
                          waits=[ev_b[3], P.ev_ones], inc=True)
                P.op("vector",
                     lambda e_, o=tbf[:, 0:tl], i=bank[:, 0:tl]:
                     e_.tensor_copy(o, i), waits=[pe])
                re = P.op("vector",
                          lambda e_, o=sq(P.xT[:, oc, t0:t0 + tl]),
                          i=tbf[:, 0:tl]:
                          e_.tensor_tensor(o, o, i, OP.add),
                          waits=[P.xT.evt], inc=True)
                P.pool.release(bi, re)
        P.xT.evt = P.now("vector")


_LAYCACHE = {}


def _lay1(w):
    key = id(w)
    if key in _LAYCACHE:
        return _LAYCACHE[key]
    w = np.asarray(w)
    r = np.ascontiguousarray(
        w.reshape(NE, DC, 128, 4, 512).transpose(0, 3, 2, 1, 4)
    ).reshape(NE, 4, 128, 4096)
    _LAYCACHE[key] = r
    return r


def _lay2(w):
    key = id(w)
    if key in _LAYCACHE:
        return _LAYCACHE[key]
    w = np.asarray(w)
    r = np.ascontiguousarray(
        w.reshape(NE, HFC, 128, 4, 256).transpose(0, 3, 2, 1, 4)
    ).reshape(NE, 4, 128, 4096)
    _LAYCACHE[key] = r
    return r


def _swap_mat():
    p = np.zeros((128, 128), dtype=np.float32)
    for i in range(128):
        d = i % 64
        j = (i - d) + (d + 32) % 64
        p[j, i] = 1.0
    return p


def _rope_tables(core):
    cs = core % 4
    rows = np.arange(NT)
    pos = np.maximum(cs * NT_OWN - HALO + rows, 0).astype(np.float64)
    pp = np.arange(128)
    j = (pp % 64) % 32
    inv = 10000.0 ** (-j / 32.0)
    ang = pos[None, :] * inv[:, None]
    cos = np.cos(ang).astype(np.float32)
    sin = np.sin(ang).astype(np.float32)
    sgn = np.where((pp % 64) < 32, -1.0, 1.0).astype(np.float32)
    return cos, (sin * sgn[:, None]).astype(np.float32)


def _masks(core):
    ii = np.arange(128)[:, None]
    jj = np.arange(256)[None, :]
    prev_ok = (jj < 128) & (jj >= ii)
    cur_ok = (jj >= 128) & ((jj - 128) <= ii)
    maskr = np.where(prev_ok | cur_ok, 0.0, -1e30).astype(np.float32)
    if core % 4 == 0:
        mask0 = np.where(cur_ok, 0.0, -1e30).astype(np.float32)
    else:
        mask0 = maskr
    return mask0, maskr


def _host_core_inputs(core, need, x_rows, inputs):
    m = {"xin": x_rows, "ident": np.eye(128, dtype=np.float32)}
    ffn0 = bool(need & {"s1", "s3"})
    ffn1 = bool(need & {"s4", "s6"})
    if ffn0 or ffn1:
        sel = np.zeros((8, 8, 128), dtype=np.float32)
        for e in range(8):
            sel[e, e, :] = 1.0
        m["esel"] = sel.reshape(8, 8 * 128)
    if ffn0:
        m["gWg"] = np.ascontiguousarray(inputs["L0_ffn_Wg"].T)
        m["gW1"] = _lay1(inputs["L0_ffn_W1"])
        m["gb1"] = np.asarray(inputs["L0_ffn_b1"]).reshape(1, -1)
        m["gW2"] = _lay2(inputs["L0_ffn_W2"])
        m["gb2"] = inputs["L0_ffn_b2"]
    if ffn1:
        m["hWg"] = np.ascontiguousarray(inputs["L1_ffn_router_w"].T)
        m["hW1"] = _lay1(inputs["L1_ffn_w1"])
        m["hW3"] = _lay1(inputs["L1_ffn_w3"])
        m["hW2"] = _lay2(inputs["L1_ffn_w2"])
    if "s2" in need:
        m["pswap"] = _swap_mat()
        for k, n in (("aWqT", "L0_attn_Wq"), ("aWkT", "L0_attn_Wk"),
                     ("aWvT", "L0_attn_Wv"), ("aWoT", "L0_attn_Wo")):
            m[k] = np.ascontiguousarray(inputs[n].T)
        for k, n in (("abq", "L0_attn_bq"), ("abk", "L0_attn_bk"),
                     ("abv", "L0_attn_bv"), ("abo", "L0_attn_bo")):
            m[k] = np.ascontiguousarray(inputs[n]).reshape(1, -1)
        m["mask0"], m["maskr"] = _masks(core)
        m["ropecos"], m["ropesin"] = _rope_tables(core)
    if "s5" in need:
        for k, n in (("lWq", "L1_attn_Wq"), ("lWk", "L1_attn_Wk"),
                     ("lWv", "L1_attn_Wv")):
            m[k] = np.ascontiguousarray(np.asarray(inputs[n]).reshape(D, D))
        m["lWo"] = np.ascontiguousarray(
            np.asarray(inputs["L1_attn_Wo"]).reshape(D, D))
        for k, n in (("lbq", "L1_attn_bq"), ("lbk", "L1_attn_bk"),
                     ("lbv", "L1_attn_bv"), ("lbo", "L1_attn_bo")):
            m[k] = np.ascontiguousarray(np.asarray(inputs[n])).reshape(1, -1)
        sm = np.zeros((128, 2), dtype=np.float32)
        sm[:, 0 if core < 4 else 1] = 1.0
        m["seqmask"] = sm
    return m


def _shard_rows(xfull, dtype):
    shards = []
    for c in range(NCORES):
        lo = c * NT_OWN - HALO
        if c % 4 == 0:
            pad = np.zeros((HALO, D), dtype=dtype)
            rows = np.concatenate(
                [pad, xfull[c * NT_OWN:(c + 1) * NT_OWN].astype(dtype,
                                                                copy=False)],
                axis=0)
        else:
            rows = xfull[lo:(c + 1) * NT_OWN].astype(dtype, copy=False)
        shards.append(np.ascontiguousarray(rows))
    return shards


def _run(phases, xfull, inputs, x_in_f32):
    need = set(phases)
    nc = build_program(phases, x_in_f32)
    dt = np.float32 if x_in_f32 else BF
    shards = _shard_rows(xfull, dt)
    in_maps = [_host_core_inputs(c, need, shards[c], inputs)
               for c in range(NCORES)]
    res = run_bass_kernel_spmd(nc, in_maps, core_ids=list(range(NCORES)))
    outs = [np.asarray(r["xout"]) for r in res.results]
    return np.concatenate(outs, axis=0)


def kernel(**inputs):
    x = np.asarray(inputs["L0_attn_x"], dtype=np.float32)
    xfull = np.ascontiguousarray(x.reshape(NT_OWN * NCORES, D))
    if FUSED:
        out = _run(ALL_PHASES, xfull, inputs, x_in_f32=True)
    else:
        cur = xfull
        f32 = True
        for ph in ALL_PHASES:
            out = _run((ph,), cur, inputs, x_in_f32=f32)
            cur = out
            f32 = False
    return np.ascontiguousarray(
        out.reshape(2, 8192, D).astype(np.float32, copy=False))


# version 05
# speedup vs baseline: 1.0026x; 1.0026x over previous
import numpy as np
import ml_dtypes
from contextlib import ExitStack

import concourse.bass as bass
import concourse.mybir as mybir
from concourse.bass_utils import run_bass_kernel_spmd

F32 = mybir.dt.float32
BF16 = mybir.dt.bfloat16
AX = mybir.AxisListType.X
OP = mybir.AluOpType
AF = mybir.ActivationFunctionType
BF = ml_dtypes.bfloat16

NCORES = 8
D = 1024
DC = 8
NT_OWN = 2048
HALO = 128
NT = NT_OWN + HALO
NH, DHD = 16, 64
NE, HF = 8, 2048
HFC = HF // 128
EPS_LN = 1e-5
EPS_LA = 1e-6
NEG = -1e9

FUSED = True
DEBUG = False

TILES_S1 = [[(0, 128), (128, 512), (640, 512)], [(1152, 512), (1664, 512)]]
TILES_OWN = [[(128, 512), (640, 512)], [(1152, 512), (1664, 512)]]
ALL_PHASES = ("s1", "s2", "s3", "s4", "s5", "s6")

ENGINES = ("sync", "tensor", "vector", "scalar", "gpsimd")


class Sem:
    def __init__(self, h):
        self.h = h
        self.n = 0


class Prog:
    def __init__(self, nc, es):
        self.nc = nc
        self.es = es
        self.q = {e: [] for e in ENGINES}
        self.sems = {}
        for name in ("pe", "ve", "ac", "po", "dpo", "dsy", "cc"):
            self.sems[name] = Sem(es.enter_context(nc.semaphore(name)))
        self.eng_sem = {"tensor": "pe", "vector": "ve", "scalar": "ac",
                        "gpsimd": "po"}

    def op(self, eng, fn, waits=(), inc=False):
        own = self.sems.get(self.eng_sem.get(eng))
        wl = []
        for w in waits:
            if w is None:
                continue
            s, v = w
            if s is own:
                continue
            assert v <= s.n, "wait on future event"
            wl.append((s, v))
        evt = None
        incs = ()
        selfwait = None
        if eng in ("vector", "scalar"):
            selfwait = own.n
            own.n += 1
            incs = ((own, 1),)
            evt = (own, own.n)
        elif inc:
            own.n += 1
            incs = ((own, 1),)
            evt = (own, own.n)
        self.q[eng].append((tuple(wl), fn, incs, selfwait))
        return evt

    def now(self, eng):
        s = self.sems[self.eng_sem[eng]]
        return (s, s.n)

    def uid(self, base):
        self._uid = getattr(self, "_uid", 0) + 1
        return f"{base}{self._uid}"

    def end_phase(self):
        self.mem_gate = tuple(self.now(e) for e in
                              ("tensor", "vector", "scalar", "gpsimd"))

    def dma(self, eng, out_ap, in_ap, waits=()):
        s = self.sems["dpo" if eng == "gpsimd" else "dsy"]
        wl = []
        for w in tuple(waits) + tuple(getattr(self, "mem_gate", ())):
            if w is None:
                continue
            ss, v = w
            assert v <= ss.n
            wl.append((ss, v))
        s.n += 16
        self.q[eng].append(
            (tuple(wl), lambda e, o=out_ap, i=in_ap: e.dma_start(o, i),
             ((s, 16),), None))
        return (s, s.n)

    def replay(self, name, eng):
        seen = {}
        own = self.sems.get(self.eng_sem.get(name))
        for waits, fn, incs, selfwait in self.q[name]:
            if selfwait is not None and selfwait > 0:
                eng.wait_ge(own.h, selfwait)
            for (s, v) in waits:
                if seen.get(id(s), -1) >= v:
                    continue
                eng.wait_ge(s.h, v)
                seen[id(s)] = v
            inst = fn(eng)
            for (s, n) in incs:
                inst.then_inc(s.h, n)


class Pool:

    def __init__(self, P, n=6):
        self.P = P
        self.banks = [P.es.enter_context(
            P.nc.psum_tensor(f"pool{i}", [128, 512], F32)) for i in range(n)]
        self.gate = [None] * n
        self.used = [False] * n
        self.i = 0

    def acquire(self):
        for _ in range(len(self.banks)):
            i = self.i
            self.i = (self.i + 1) % len(self.banks)
            if not self.used[i]:
                self.used[i] = True
                return self.banks[i], self.gate[i], i
        raise RuntimeError("psum pool exhausted")

    def release(self, idx, evt):
        self.used[idx] = False
        self.gate[idx] = evt


class Buf:
    def __init__(self, t):
        self.t = t
        self.evt = None

    def __getitem__(self, idx):
        return self.t[idx]


def sq(ap):
    return ap.squeeze()


def row(ap):
    return ap.rearrange("p a b -> p (a b)")


def build_program(phases, x_in_f32):
    nc = bass.Bass("TRN2", target_bir_lowering=False)
    es = ExitStack()
    P = Prog(nc, es)
    P.pool = Pool(P, 6)
    P.tp = [es.enter_context(nc.psum_tensor(f"tp{i}", [128, 1024], BF16))
            for i in range(2)]
    P.tp_gate = [None, None]
    P.tp_i = 0

    need = set(phases)
    ffn0 = bool(need & {"s1", "s3"})
    ffn1 = bool(need & {"s4", "s6"})

    dt_in = {}

    def din(name, shape, dtype=F32):
        dt_in[name] = nc.dram_tensor(name, list(shape), dtype,
                                     kind="ExternalInput")
        return dt_in[name]

    xin = din("xin", [NT, D], F32 if x_in_f32 else BF16)
    store_f32 = "s6" in need
    xout = nc.dram_tensor("xout", [NT_OWN, D],
                          F32 if store_f32 else BF16, kind="ExternalOutput")
    din("ident", [128, 128])
    if ffn0 or ffn1:
        din("esel", [8, 8 * 128])
    if ffn0:
        din("gWg", [8, D])
        din("gW1", [NE, 4, 128, 4096])
        din("gb1", [1, NE * HF])
        din("gW2", [NE, 4, 128, 4096])
        din("gb2", [NE, D])
    if ffn1:
        din("hWg", [8, D])
        din("hW1", [NE, 4, 128, 4096])
        din("hW3", [NE, 4, 128, 4096])
        din("hW2", [NE, 4, 128, 4096])
    if "s2" in need:
        din("pswap", [128, 128])
        din("aWqT", [D, D]); din("aWkT", [D, D])
        din("aWvT", [D, D]); din("aWoT", [D, D])
        din("abq", [1, D]); din("abk", [1, D])
        din("abv", [1, D]); din("abo", [1, D])
        din("mask0", [128, 256]); din("maskr", [128, 256])
        din("ropecos", [128, NT]); din("ropesin", [128, NT])
    if "s5" in need:
        din("lWq", [D, D]); din("lWk", [D, D]); din("lWv", [D, D])
        din("lWo", [D, D])
        din("lbq", [1, D]); din("lbk", [1, D]); din("lbv", [1, D])
        din("lbo", [1, D])
        din("seqmask", [128, 2])
        P.cc_in = nc.dram_tensor("cc_in", [128, 1040], F32)
        P.cc_out = nc.dram_tensor("cc_out", [128, 1040], F32)

    xT = Buf(es.enter_context(nc.sbuf_tensor("xT", [128, DC, NT], BF16)))
    identb = es.enter_context(nc.sbuf_tensor("identb", [128, 128], BF16))
    ones_r = es.enter_context(nc.sbuf_tensor("ones_r", [1, 512], BF16))
    ones_c = es.enter_context(nc.sbuf_tensor("ones_c", [128, 1], BF16))
    ones_f = es.enter_context(nc.sbuf_tensor("ones_f", [1, 128], F32))
    P.xT, P.identb = xT, identb
    P.ones_r, P.ones_c, P.ones_f = ones_r, ones_c, ones_f

    P.ld_id = P.dma("gpsimd", identb[:, :], dt_in["ident"][:, :])
    P.op("gpsimd", lambda e: e.memset(ones_r[:, :], 1.0))
    P.op("gpsimd", lambda e: e.memset(ones_c[:, :], 1.0))
    P.ev_ones = P.op("gpsimd", lambda e: e.memset(ones_f[:, :], 1.0),
                     inc=True)

    emit_load_x(P, xin, cast=x_in_f32)

    for ph in ALL_PHASES:
        if ph not in need:
            continue
        if ph == "s1":
            tiles = TILES_S1 if len(need) > 1 else TILES_OWN
            emit_ffn(P, dt_in, layer=0, halves=tiles)
        elif ph == "s3":
            emit_ffn(P, dt_in, layer=0, halves=TILES_OWN)
        elif ph in ("s4", "s6"):
            emit_ffn(P, dt_in, layer=1, halves=TILES_OWN)
        elif ph == "s2":
            emit_attn_l0(P, dt_in)
        elif ph == "s5":
            emit_attn_l1(P, dt_in)
        P.end_phase()

    emit_store_x(P, xout, as_f32=store_f32)

    fin = [(P.sems["dsy"], P.sems["dsy"].n), (P.sems["dpo"], P.sems["dpo"].n)]
    P.q["gpsimd"].append((tuple(fin), lambda e: e.engine_nop(), (), None))

    with nc.Block() as block:
        @block.sync
        def _(eng):
            P.replay("sync", eng)

        @block.tensor
        def _(eng):
            P.replay("tensor", eng)

        @block.vector
        def _(eng):
            P.replay("vector", eng)

        @block.scalar
        def _(eng):
            P.replay("scalar", eng)

        @block.gpsimd
        def _(eng):
            P.replay("gpsimd", eng)

    es.close()
    return nc


def emit_transpose(P, dst_ap, src_ap, waits=()):
    i = P.tp_i
    P.tp_i ^= 1
    t = P.tp[i]
    npart = src_ap.shape[0]
    nf = src_ap.shape[-1]
    pe = P.op("tensor",
              lambda e, o=t, s=src_ap, np_=npart, nf_=nf:
              e.transpose(o[0:nf_, 0:np_], s, P.identb[0:np_, 0:np_]),
              waits=list(waits) + [P.tp_gate[i], P.ld_id], inc=True)
    ve = P.op("vector",
              lambda e, o=dst_ap, i_=t, np_=npart, nf_=nf:
              e.tensor_copy(o, i_[0:nf_, 0:np_]),
              waits=[pe], inc=True)
    P.tp_gate[i] = ve
    return ve


def emit_load_x(P, xin, cast):
    nc = P.nc
    with ExitStack() as les:
        xtok = les.enter_context(nc.sbuf_tensor(P.uid("xtok"), [128, 17, D], BF16))
        src = xin.ap().rearrange("(tc p) f -> p tc f", p=128)
        eng = "gpsimd" if cast else "sync"
        ev = P.dma(eng, xtok[:, :, :], src)
        for tc in range(17):
            for fc in range(DC):
                emit_transpose(P, sq(P.xT[:, fc, tc * 128:(tc + 1) * 128]),
                               sq(xtok[:, tc, fc * 128:(fc + 1) * 128]),
                               waits=[ev])
        P.xT.evt = P.now("vector")
    P.end_phase()


def emit_store_x(P, xout, as_f32):
    nc = P.nc
    es = P.es
    dtype = F32 if as_f32 else BF16
    NB = 4
    obuf = [es.enter_context(nc.sbuf_tensor(f"obuf{i}", [128, D], dtype))
            for i in range(NB)]
    ogate = [None] * NB
    halves = getattr(P, "xT_half", [])
    hev = [halves[-2] if len(halves) >= 2 else P.xT.evt,
           halves[-1] if halves else P.xT.evt]
    dst = xout.ap().rearrange("(tc p) f -> p tc f", p=128)
    for tc in range(16):
        ob = obuf[tc % NB]
        gate_ev = hev[0] if tc < 8 else hev[1]
        last = None
        for fc in range(DC):
            last = emit_transpose(
                P, ob[:, fc * 128:(fc + 1) * 128],
                sq(P.xT[:, fc, HALO + tc * 128:HALO + (tc + 1) * 128]),
                waits=[gate_ev, ogate[tc % NB]])
        ogate[tc % NB] = P.dma("sync", sq(dst[:, tc, :]), ob[:, :],
                               waits=[last])


def emit_ln(P, tiles, out_xhat, les=None, toff=0):
    nc = P.nc
    les = ExitStack()
    xsq = les.enter_context(nc.sbuf_tensor(P.uid("ln_xsq"), [128, 512], BF16))
    rows = les.enter_context(nc.sbuf_tensor(P.uid("ln_rows"), [1, 2048], F32))
    nmB = les.enter_context(nc.sbuf_tensor(P.uid("ln_nmB"), [128, 512], BF16))
    rsB = les.enter_context(nc.sbuf_tensor(P.uid("ln_rsB"), [128, 512], BF16))
    tb = les.enter_context(nc.sbuf_tensor(P.uid("ln_tb"), [128, 512], BF16))
    tile_evts = []
    for (t0, tl) in tiles:
        bank1, g1, i1 = P.pool.acquire()
        bank2, g2, i2 = P.pool.acquire()
        sx = bank1[0:1, 0:tl]
        sx2 = bank2[0:1, 0:tl]
        pe2 = None
        pe = None
        for c in range(DC):
            xc = sq(P.xT[:, c, t0:t0 + tl])
            ev_sq = P.op("scalar",
                         lambda e, o=xsq[:, 0:tl], i=xc:
                         e.activation(o, i, AF.Square),
                         waits=[P.xT.evt, g1, g2, pe2], inc=True)
            pe = P.op("tensor",
                      lambda e, o=sx, i=xc, st=(c == 0), sp=(c == DC - 1):
                      e.matmul(o, P.ones_c[:, :], i, start=st, stop=sp),
                      waits=[P.xT.evt, g1, P.ev_ones],
                      inc=(c == DC - 1))
            pe2 = P.op("tensor",
                       lambda e, o=sx2, i=xsq[:, 0:tl], st=(c == 0),
                       sp=(c == DC - 1):
                       e.matmul(o, P.ones_c[:, :], i, start=st, stop=sp),
                       waits=[ev_sq, g2], inc=True)
        mrow = rows[:, 0:tl]
        vrow = rows[:, 512:512 + tl]
        rrow = rows[:, 1024:1024 + tl]
        nrow = rows[:, 1536:1536 + tl]
        e1 = P.op("vector", lambda e, o=mrow, i=sx: e.tensor_scalar(
            o, i, 1.0 / D, None, OP.mult), waits=[pe], inc=True)
        P.op("vector", lambda e, o=vrow, i=sx2: e.tensor_scalar(
            o, i, 1.0 / D, None, OP.mult), waits=[pe2])
        P.op("vector", lambda e, o=nrow, i=mrow: e.scalar_tensor_tensor(
            o, i, -1.0, i, OP.mult, OP.mult))
        P.op("vector", lambda e, o=vrow, i=nrow: e.tensor_tensor(
            o, o, i, OP.add))
        e3 = P.op("vector", lambda e, o=vrow: e.tensor_scalar(
            o, o, EPS_LN, None, OP.add), inc=True)
        P.op("scalar", lambda e, o=vrow: e.activation(o, o, AF.Ln),
             waits=[e3])
        e5 = P.op("scalar", lambda e, o=rrow, i=vrow: e.activation(
            o, i, AF.Exp, scale=-0.5), inc=True)
        e6 = P.op("vector", lambda e, o=nrow, i=mrow: e.tensor_scalar(
            o, i, -1.0, None, OP.mult), waits=[e1], inc=True)
        P.pool.release(i1, e6)
        bank3, g3, i3 = P.pool.acquire()
        bank4, g4, i4 = P.pool.acquire()
        pb1 = P.op("tensor", lambda e, o=bank3[:, 0:tl], r=nrow: e.matmul(
            o, P.ones_f[:, :], r, start=True, stop=True),
            waits=[e6, g3, P.ev_ones], inc=True)
        pb2 = P.op("tensor", lambda e, o=bank4[:, 0:tl], r=rrow: e.matmul(
            o, P.ones_f[:, :], r, start=True, stop=True),
            waits=[e5, g4], inc=True)
        P.pool.release(i2, pb2)
        c1 = P.op("vector", lambda e, o=nmB[:, 0:tl], i=bank3[:, 0:tl]:
                  e.tensor_copy(o, i), waits=[pb1], inc=True)
        c2 = P.op("vector", lambda e, o=rsB[:, 0:tl], i=bank4[:, 0:tl]:
                  e.tensor_copy(o, i), waits=[pb2], inc=True)
        P.pool.release(i3, c1)
        P.pool.release(i4, c2)
        last = None
        for c in range(DC):
            xc = sq(P.xT[:, c, t0:t0 + tl])
            oc_ = sq(out_xhat[:, c, t0 - toff:t0 - toff + tl])
            P.op("vector", lambda e, o=tb[:, 0:tl], i=xc, m=nmB[:, 0:tl]:
                 e.tensor_tensor(o, i, m, OP.add), waits=[P.xT.evt, c1])
            last = P.op("vector", lambda e, o=oc_, i=tb[:, 0:tl],
                        r=rsB[:, 0:tl]:
                        e.tensor_tensor(o, i, r, OP.mult),
                        waits=[c2], inc=True)
        tile_evts.append(last)
    out_xhat.evt = P.now("vector")
    P.end_phase()
    les.close()
    return tile_evts


def emit_routing(P, dt_in, layer, tiles, xhat, cT, les_, wgB, ev_wg,
                 toff=0):
    nc = P.nc
    les = ExitStack()
    s = les.enter_context(nc.sbuf_tensor(P.uid("rt_s"), [128, 8], F32))
    pr = les.enter_context(nc.sbuf_tensor(P.uid("rt_pr"), [128, 8], F32))
    eq = les.enter_context(nc.sbuf_tensor(P.uid("rt_eq"), [128, 8], F32))
    aa = les.enter_context(nc.sbuf_tensor(P.uid("rt_aa"), [128, 8], F32))
    bb = les.enter_context(nc.sbuf_tensor(P.uid("rt_bb"), [128, 8], F32))
    f1 = les.enter_context(nc.sbuf_tensor(P.uid("rt_f1"), [128, 8], F32))
    f2 = les.enter_context(nc.sbuf_tensor(P.uid("rt_f2"), [128, 8], F32))
    cw = les.enter_context(nc.sbuf_tensor(P.uid("rt_cw"), [128, 8], F32))
    cwb = les.enter_context(nc.sbuf_tensor(P.uid("rt_cwb"), [128, 8], BF16))
    col = les.enter_context(nc.sbuf_tensor(P.uid("rt_col"), [128, 4], F32))

    def cumsum_excl():
        P.op("vector", lambda e: e.memset(aa[:, 0:1], 0.0))
        P.op("vector", lambda e: e.tensor_copy(aa[:, 1:8], eq[:, 0:7]))
        P.op("vector", lambda e: e.tensor_copy(bb[:, :], aa[:, :]))
        P.op("vector", lambda e: e.tensor_tensor(
            bb[:, 2:8], aa[:, 2:8], aa[:, 0:6], OP.add))
        P.op("vector", lambda e: e.tensor_copy(aa[:, :], bb[:, :]))
        P.op("vector", lambda e: e.tensor_tensor(
            aa[:, 4:8], bb[:, 4:8], bb[:, 0:4], OP.add))

    prev_act = None
    for (t0, tl) in tiles:
        for k in range(tl // 128):
            tt = t0 + k * 128
            bank, gate, bi = P.pool.acquire()
            lg = bank[:, 0:8]
            pe = None
            for dc in range(DC):
                pe = P.op("tensor",
                          lambda e, o=lg,
                          l=sq(xhat[:, dc, tt - toff:tt - toff + 128]),
                          r=sq(wgB[:, dc, :]), st=(dc == 0),
                          sp=(dc == DC - 1):
                          e.matmul(o, l, r, start=st, stop=sp),
                          waits=[xhat.evt, ev_wg, gate],
                          inc=(dc == DC - 1))
            if layer == 0:
                m = P.op("vector", lambda e, i=lg: e.reduce_max(
                    col[:, 0:1], i, axis=AX, negate=True),
                    waits=[pe, prev_act], inc=True)
                ex = P.op("scalar", lambda e, i=lg: e.activation(
                    pr[:, :], i, AF.Exp, bias=col[:, 0:1],
                    accum_out=col[:, 1:2]), waits=[pe, m], inc=True)
                P.op("scalar", lambda e: e.activation(
                    col[:, 2:3], col[:, 1:2], AF.Ln))
                iv = P.op("scalar", lambda e: e.activation(
                    col[:, 2:3], col[:, 2:3], AF.Exp, scale=-1.0), inc=True)
                P.op("vector", lambda e: e.tensor_scalar(
                    s[:, :], pr[:, :], col[:, 2:3], None, OP.mult),
                    waits=[ex, iv])
            else:
                P.op("vector", lambda e, i=lg: e.tensor_copy(s[:, :], i),
                     waits=[pe, prev_act])
            P.op("vector", lambda e: e.reduce_max(col[:, 0:1], s[:, :],
                                                  axis=AX))
            P.op("vector", lambda e: e.tensor_scalar(
                eq[:, :], s[:, :], col[:, 0:1], 0.0, OP.subtract, OP.is_ge))
            cumsum_excl()
            P.op("vector", lambda e: e.tensor_scalar(
                f1[:, :], aa[:, :], 0.0, None, OP.is_equal))
            P.op("vector", lambda e: e.tensor_tensor(
                f1[:, :], f1[:, :], eq[:, :], OP.mult))
            P.op("vector", lambda e: e.scalar_tensor_tensor(
                pr[:, :], f1[:, :], NEG, s[:, :], OP.mult, OP.add))
            P.op("vector", lambda e: e.reduce_max(col[:, 1:2], pr[:, :],
                                                  axis=AX))
            P.op("vector", lambda e: e.tensor_scalar(
                eq[:, :], pr[:, :], col[:, 1:2], 0.0, OP.subtract, OP.is_ge))
            cumsum_excl()
            P.op("vector", lambda e: e.tensor_scalar(
                f2[:, :], aa[:, :], 0.0, None, OP.is_equal))
            P.op("vector", lambda e: e.tensor_tensor(
                f2[:, :], f2[:, :], eq[:, :], OP.mult))
            if layer == 0:
                P.op("vector", lambda e: e.tensor_tensor(
                    f1[:, :], f1[:, :], f2[:, :], OP.add))
                P.op("vector", lambda e: e.tensor_tensor(
                    cw[:, :], f1[:, :], s[:, :], OP.mult))
                cl = P.op("vector", lambda e: e.tensor_scalar(
                    cw[:, :], cw[:, :], 0.5, None, OP.mult), inc=True)
            else:
                dd = P.op("vector", lambda e: e.tensor_tensor(
                    col[:, 2:3], col[:, 1:2], col[:, 0:1], OP.subtract),
                    inc=True)
                x1 = P.op("scalar", lambda e: e.activation(
                    col[:, 2:3], col[:, 2:3], AF.Exp), waits=[dd], inc=True)
                a1 = P.op("vector", lambda e: e.tensor_scalar(
                    col[:, 2:3], col[:, 2:3], 1.0, None, OP.add),
                    waits=[x1], inc=True)
                P.op("scalar", lambda e: e.activation(
                    col[:, 2:3], col[:, 2:3], AF.Ln), waits=[a1])
                g1 = P.op("scalar", lambda e: e.activation(
                    col[:, 2:3], col[:, 2:3], AF.Exp, scale=-1.0), inc=True)
                P.op("vector", lambda e: e.tensor_scalar(
                    col[:, 2:3], col[:, 2:3], 0.5, None, OP.mult),
                    waits=[g1])
                P.op("vector", lambda e: e.tensor_scalar(
                    col[:, 3:4], col[:, 2:3], -1.0, 0.5, OP.mult, OP.add))
                P.op("vector", lambda e: e.tensor_scalar(
                    f1[:, :], f1[:, :], col[:, 2:3], None, OP.mult))
                P.op("vector", lambda e: e.tensor_scalar(
                    f2[:, :], f2[:, :], col[:, 3:4], None, OP.mult))
                cl = P.op("vector", lambda e: e.tensor_tensor(
                    cw[:, :], f1[:, :], f2[:, :], OP.add), inc=True)
            P.pool.release(bi, cl)
            cb = P.op("vector", lambda e: e.tensor_copy(cwb[:, :], cw[:, :]),
                      inc=True)
            emit_transpose(P, cT[0:8, tt:tt + 128], cwb[:, :], waits=[cb])
            prev_act = P.now("scalar")
    if DEBUG and getattr(P, "_rt_dumped", False) is False:
        P._rt_dumped = True
        ve_now = P.now("vector")
        for nm, tt_ in (("rs", s), ("rf1", f1), ("rf2", f2), ("rcw", cw),
                        ("raa", aa), ("req", eq), ("rcol", col)):
            dd = nc.dram_tensor("dbg_" + nm, list(tt_.t.shape
                                if hasattr(tt_, 't') else
                                [128, tt_.free_size]), F32,
                                kind="ExternalOutput") \
                if False else nc.dram_tensor(
                "dbg_" + nm, [128, 8 if nm != "rcol" else 4], F32,
                kind="ExternalOutput")
            P.dma("sync", dd[:, :], tt_[:, :], waits=[ve_now])
    cT.evt = P.now("vector")
    P.end_phase()
    les.close()


def emit_ffn(P, dt_in, layer, halves):
    nc = P.nc
    with ExitStack() as les:
        hw = max(sum(tl for _, tl in tg) for tg in halves)
        xhat = Buf(les.enter_context(
            nc.sbuf_tensor(P.uid("ffn_xhat"), [128, DC, hw], BF16)))
        z = Buf(les.enter_context(
            nc.sbuf_tensor(P.uid("ffn_z"), [128, HFC, hw], BF16)))
        acc = les.enter_context(
            nc.sbuf_tensor(P.uid("ffn_acc"), [128, DC, hw], BF16))
        cT = Buf(les.enter_context(nc.sbuf_tensor(P.uid("ffn_cT"), [8, NT], BF16)))
        gBs = les.enter_context(nc.sbuf_tensor(P.uid("ffn_gB"), [128, 3, 512], F32))
        tbf = les.enter_context(nc.sbuf_tensor(P.uid("ffn_tbf"), [128, 512], BF16))
        sres = les.enter_context(nc.sbuf_tensor(P.uid("ffn_sres"), [128, 512], F32))
        tmg = les.enter_context(nc.sbuf_tensor(P.uid("ffn_tmg"), [128, 512], F32))
        eselb = les.enter_context(nc.sbuf_tensor(P.uid("eselb"), [8, 8, 128], BF16))
        wgT = les.enter_context(nc.sbuf_tensor(P.uid("rt_wgT"), [8, D], BF16))
        wgB = les.enter_context(nc.sbuf_tensor(P.uid("rt_wgB"), [128, DC, 8], BF16))
        nslot = 6 if layer == 1 else 5
        slots = [les.enter_context(
            nc.sbuf_tensor(P.uid("wslot"), [128, 4096], BF16))
            for i in range(nslot)]
        slot_free = [None] * nslot
        slot_ready = [None] * nslot
        slot_i = [0]

        ev_es = P.dma("gpsimd", eselb[:, :, :],
                      dt_in["esel"].ap().rearrange("e (g f) -> e g f", f=128))
        ev_wgT = P.dma("gpsimd", wgT[:, :],
                       dt_in["gWg" if layer == 0 else "hWg"][:, :])
        for dc in range(DC):
            emit_transpose(P, sq(wgB[:, dc, :]),
                           wgT[:, dc * 128:(dc + 1) * 128], waits=[ev_wgT])
        ev_wg = P.now("vector")

        if layer == 0:
            w1ap = dt_in["gW1"]
            w2ap = dt_in["gW2"]
            b1sb = [les.enter_context(
                nc.sbuf_tensor(P.uid("b1sb"), [1, HF], BF16)) for i in range(2)]
            b1_ready = [None, None]
            b1_free = [None, None]
            b2sb = les.enter_context(nc.sbuf_tensor(P.uid("b2sb"), [NE, D], BF16))
            ev_b2 = P.dma("gpsimd", b2sb[:, :], dt_in["gb2"][:, :])
            b1ap = dt_in["gb1"].ap().rearrange("o (e h) -> o e h", e=NE)
        else:
            w1ap = dt_in["hW1"]
            w3ap = dt_in["hW3"]
            w2ap = dt_in["hW2"]
            ev_b1 = ev_b2 = None

        def slot_load(src_ap, split):
            i = slot_i[0]
            slot_i[0] = (slot_i[0] + 1) % len(slots)
            ev = P.dma("gpsimd", slots[i][:, :], src_ap,
                       waits=[slot_free[i]])
            slot_ready[i] = ev
            return i

        def w1_slab(w, e, qq):
            return sq(w[e:e + 1, qq, :, :])

        def w2_slab(e, qq):
            return sq(w2ap[e:e + 1, qq, :, :])

        for half, tiles in enumerate(halves):
            t0h = tiles[0][0]
            sched = []
            for e in range(NE):
                for qq in range(4):
                    sched.append(("w1", e, qq))
                    if layer == 1:
                        sched.append(("w3", e, qq))
                for qq in range(4):
                    sched.append(("w2", e, qq))
            slot_of = {}
            nload = [0]

            def load_next():
                if nload[0] >= len(sched):
                    return
                kind, e, qq = sched[nload[0]]
                if kind == "w1":
                    src, split = w1_slab(w1ap, e, qq), False
                elif kind == "w3":
                    src, split = w1_slab(w3ap, e, qq), False
                else:
                    src, split = w2_slab(e, qq), True
                slot_of[(kind, e, qq)] = slot_load(src, split)
                nload[0] += 1

            for _ in range(min(nslot, len(sched))):
                load_next()

            xh_evts = emit_ln(P, tiles, xhat, toff=t0h)
            emit_routing(P, dt_in, layer, tiles, xhat, cT, les, wgB, ev_wg,
                         toff=t0h)
            if DEBUG and half == 0:
                d1 = nc.dram_tensor("dbg_xhat", [128, DC * hw], BF16,
                                    kind="ExternalOutput")
                P.dma("sync", d1[:, :],
                      xhat.t[:, :, :].rearrange("p a b -> p (a b)"),
                      waits=[xhat.evt])
                d2 = nc.dram_tensor("dbg_cT", [8, NT], BF16,
                                    kind="ExternalOutput")
                P.dma("sync", d2[:, :], cT.t[:, :], waits=[cT.evt])

            last_ze = None
            if layer == 0:
                b1_ready[0] = P.dma("gpsimd", b1sb[0][:, :], row(b1ap[:, 0:1, :]))
            for e in range(NE):
                if layer == 0 and e + 1 < NE:
                    b1_ready[(e + 1) % 2] = P.dma(
                        "gpsimd", b1sb[(e + 1) % 2][:, :],
                        row(b1ap[:, e + 1:e + 2, :]), waits=[b1_free[(e + 1) % 2]])
                gev = []
                for ti, (t0, tl) in enumerate(tiles):
                    bank, gate, bi = P.pool.acquire()
                    pe = P.op("tensor", lambda e_, o=bank[:, 0:tl],
                              l=sq(eselb[:, e:e + 1, :]),
                              r=cT[0:8, t0:t0 + tl]:
                              e_.matmul(o, l, r, start=True, stop=True),
                              waits=[cT.evt, ev_es, gate], inc=True)
                    ge = P.op("vector", lambda e_, o=sq(gBs[:, ti, 0:tl]),
                              i=bank[:, 0:tl]: e_.tensor_copy(o, i),
                              waits=[pe], inc=True)
                    gev.append(ge)
                    P.pool.release(bi, ge)

                for qq in range(4):
                    si1 = slot_of[("w1", e, qq)]
                    sl1 = slots[si1][:, :].rearrange("p (a b) -> p a b", a=8)
                    if layer == 1:
                        si3 = slot_of[("w3", e, qq)]
                        sl3 = slots[si3][:, :].rearrange("p (a b) -> p a b",
                                                         a=8)
                    for hci in range(4):
                        hc = qq * 4 + hci
                        nb = len(tiles)
                        bks = [P.pool.acquire() for _ in range(nb)]
                        pe1_t = [None] * len(tiles)
                        for dc in range(DC):
                            for ti, (t0, tl) in enumerate(tiles):
                                bank, gate, bi = bks[ti]
                                lastmm = (dc == DC - 1) and (layer == 1)
                                pe1_t[ti] = P.op(
                                    "tensor",
                                    lambda e_, o=bank[:, 0:tl],
                                    l=sq(sl1[:, dc,
                                             hci * 128:(hci + 1) * 128]),
                                    r=sq(xhat[:, dc, t0 - t0h:t0 - t0h + tl]),
                                    st=(dc == 0), sp=lastmm:
                                    e_.matmul(o, l, r, start=st, stop=sp),
                                    waits=[slot_ready[si1], xh_evts[ti],
                                           gate],
                                    inc=lastmm)
                        if layer == 0:
                            for ti, (t0, tl) in enumerate(tiles):
                                bank, gate, bi = bks[ti]
                                pe = P.op(
                                    "tensor",
                                    lambda e_, o=bank[:, 0:tl],
                                    l=b1sb[e % 2][0:1,
                                                  hc * 128:(hc + 1) * 128],
                                    r=P.ones_r[:, 0:tl]:
                                    e_.matmul(o, l, r, start=False,
                                              stop=True),
                                    waits=[b1_ready[e % 2], P.ev_ones],
                                    inc=True)
                                ze = P.op(
                                    "scalar",
                                    lambda e_,
                                    o=sq(z[:, hc, t0 - t0h:t0 - t0h + tl]),
                                    i=bank[:, 0:tl]:
                                    e_.activation(o, i, AF.Relu),
                                    waits=[pe], inc=True)
                                P.pool.release(bi, ze)
                        else:
                            bks2 = [P.pool.acquire() for _ in range(nb)]
                            pe2_t = [None] * nb
                            for dc in range(DC):
                                for ti, (t0, tl) in enumerate(tiles):
                                    bank2, gate2, bi2 = bks2[ti]
                                    pe2_t[ti] = P.op(
                                        "tensor",
                                        lambda e_, o=bank2[:, 0:tl],
                                        l=sq(sl3[:, dc,
                                                 hci * 128:(hci + 1) * 128]),
                                        r=sq(xhat[:, dc,
                                                  t0 - t0h:t0 - t0h + tl]),
                                        st=(dc == 0), sp=(dc == DC - 1):
                                        e_.matmul(o, l, r, start=st, stop=sp),
                                        waits=[slot_ready[si3], xh_evts[ti],
                                               gate2],
                                        inc=(dc == DC - 1))
                            for ti, (t0, tl) in enumerate(tiles):
                                bank, gate, bi = bks[ti]
                                bank2, gate2, bi2 = bks2[ti]
                                pe = pe1_t[ti]
                                ge_ = P.op(
                                    "scalar",
                                    lambda e_, o=tmg[:, 0:tl],
                                    i=bank[:, 0:tl]:
                                    e_.activation(o, i, AF.Silu),
                                    waits=[pe, last_ze], inc=True)
                                ze = P.op(
                                    "vector",
                                    lambda e_,
                                    o=sq(z[:, hc, t0 - t0h:t0 - t0h + tl]),
                                    a=tmg[:, 0:tl], b=bank2[:, 0:tl]:
                                    e_.tensor_tensor(o, a, b, OP.mult),
                                    waits=[ge_, pe2_t[ti]], inc=True)
                                last_ze = ze
                                P.pool.release(bi, ge_)
                                P.pool.release(bi2, ze)
                    slot_free[si1] = P.now("tensor")
                    if layer == 1:
                        slot_free[si3] = P.now("tensor")
                    load_next()
                    if layer == 1:
                        load_next()
                if layer == 0:
                    b1_free[e % 2] = P.now("tensor")
                z.evt = P.now("vector" if layer == 1 else "scalar")
                if DEBUG and half == 0 and e == 0:
                    d3 = nc.dram_tensor("dbg_z0", [128, HFC * hw], BF16,
                                        kind="ExternalOutput")
                    P.dma("sync", d3[:, :],
                          z.t[:, :, :].rearrange("p a b -> p (a b)"),
                          waits=[z.evt])

                for qq in range(4):
                    si2 = slot_of[("w2", e, qq)]
                    sl2 = slots[si2][:, :].rearrange("p (a b) -> p a b", a=16)
                    for oci in range(2):
                        oc = qq * 2 + oci
                        nb = len(tiles)
                        bks = [P.pool.acquire() for _ in range(nb)]
                        pe_t = [None] * nb
                        for fc in range(HFC):
                            for ti, (t0, tl) in enumerate(tiles):
                                bank, gate, bi = bks[ti]
                                pe_t[ti] = P.op(
                                    "tensor",
                                    lambda e_, o=bank[:, 0:tl],
                                    l=sq(sl2[:, fc,
                                             oci * 128:(oci + 1) * 128]),
                                    r=sq(z[:, fc, t0 - t0h:t0 - t0h + tl]),
                                    st=(fc == 0), sp=(fc == HFC - 1):
                                    e_.matmul(o, l, r, start=st, stop=sp),
                                    waits=[slot_ready[si2], z.evt, gate],
                                    inc=(fc == HFC - 1))
                        for ti, (t0, tl) in enumerate(tiles):
                            bank, gate, bi = bks[ti]
                            accv = sq(acc[:, oc, t0 - t0h:t0 - t0h + tl])
                            if e == 0:
                                ae = P.op("vector",
                                          lambda e_, o=accv, i=bank[:, 0:tl],
                                          g=sq(gBs[:, ti, 0:tl]):
                                          e_.tensor_tensor(o, i, g, OP.mult),
                                          waits=[pe_t[ti], gev[ti]],
                                          inc=True)
                            else:
                                P.op("vector",
                                     lambda e_, o=tbf[:, 0:tl],
                                     i=bank[:, 0:tl], g=sq(gBs[:, ti, 0:tl]):
                                     e_.tensor_tensor(o, i, g, OP.mult),
                                     waits=[pe_t[ti], gev[ti]])
                                ae = P.op("vector",
                                          lambda e_, o=accv, i=tbf[:, 0:tl]:
                                          e_.tensor_tensor(o, o, i, OP.add),
                                          inc=True)
                            P.pool.release(bi, ae)
                    slot_free[si2] = P.now("tensor")
                    load_next()
            acc_evt = P.now("vector")
            if DEBUG and half == 0:
                d4 = nc.dram_tensor("dbg_acc", [128, DC * hw], BF16,
                                    kind="ExternalOutput")
                P.dma("sync", d4[:, :],
                      acc[:, :, :].rearrange("p a b -> p (a b)"),
                      waits=[acc_evt])

            for ti, (t0, tl) in enumerate(tiles):
                for oc in range(DC):
                    accv = sq(acc[:, oc, t0 - t0h:t0 - t0h + tl])
                    xv = sq(P.xT[:, oc, t0:t0 + tl])
                    if layer == 0:
                        bank, gate, bi = P.pool.acquire()
                        pe = P.op("tensor",
                                  lambda e_, o=bank[:, 0:tl],
                                  l=b2sb[:, oc * 128:(oc + 1) * 128],
                                  r=cT[0:8, t0:t0 + tl]:
                                  e_.matmul(o, l, r, start=True, stop=True),
                                  waits=[ev_b2, cT.evt, gate], inc=True)
                        P.op("vector", lambda e_, o=sres[:, 0:tl], a=xv,
                             b=accv: e_.tensor_tensor(o, a, b, OP.add),
                             waits=[acc_evt, P.xT.evt])
                        re = P.op("vector",
                                  lambda e_, o=xv, a=sres[:, 0:tl],
                                  b=bank[:, 0:tl]:
                                  e_.tensor_tensor(o, a, b, OP.add),
                                  waits=[pe], inc=True)
                        P.pool.release(bi, re)
                    else:
                        P.op("vector",
                             lambda e_, o=xv, a=xv, b=accv:
                             e_.tensor_tensor(o, a, b, OP.add),
                             waits=[acc_evt, P.xT.evt], inc=True)
            P.xT.evt = P.now("vector")
            P.xT_half = getattr(P, "xT_half", [])
            P.xT_half.append(P.xT.evt)


def emit_attn_l0(P, dt_in):
    nc = P.nc
    with ExitStack() as les:
        xhat = Buf(les.enter_context(
            nc.sbuf_tensor(P.uid("a0_xhat"), [128, DC, NT], BF16)))
        aoT = Buf(les.enter_context(
            nc.sbuf_tensor(P.uid("a0_aoT"), [128, DC, NT_OWN], BF16)))
        cosT = les.enter_context(nc.sbuf_tensor(P.uid("a0_cos"), [128, NT], F32))
        sinT = les.enter_context(nc.sbuf_tensor(P.uid("a0_sin"), [128, NT], F32))
        pswb = les.enter_context(nc.sbuf_tensor(P.uid("a0_psw"), [128, 128], BF16))
        mask0 = les.enter_context(nc.sbuf_tensor(P.uid("a0_m0"), [128, 256],
                                                 F32))
        maskr = les.enter_context(nc.sbuf_tensor(P.uid("a0_mr"), [128, 256],
                                                 F32))
        wsl = [les.enter_context(
            nc.sbuf_tensor(P.uid("a0_w"), [128, DC, 128], BF16))
            for i in range(3)]
        woT = les.enter_context(nc.sbuf_tensor(P.uid("a0_wo"), [128, DC, D], BF16))
        bqkv = [les.enter_context(nc.sbuf_tensor(P.uid("a0_bias"),
                                                 [1, 3, 128], BF16))
                for i in range(2)]
        bo_sb = les.enter_context(nc.sbuf_tensor(P.uid("a0_bo"), [1, D],
                                                 BF16))
        qT = les.enter_context(nc.sbuf_tensor(P.uid("a0_qT"), [128, NT_OWN], BF16))
        kT = les.enter_context(nc.sbuf_tensor(P.uid("a0_kT"), [128, NT], BF16))
        v2 = les.enter_context(nc.sbuf_tensor(P.uid("a0_v2"), [128, 17, 128], BF16))
        qtmp = les.enter_context(nc.sbuf_tensor(P.uid("a0_qtmp"), [128, 512], BF16))
        rt1 = les.enter_context(nc.sbuf_tensor(P.uid("a0_rt1"), [128, 512], F32))
        rt2 = les.enter_context(nc.sbuf_tensor(P.uid("a0_rt2"), [128, 512], F32))
        NR = 4
        sm = [les.enter_context(nc.sbuf_tensor(P.uid("a0_sm"), [128, 256],
                                               F32)) for _ in range(NR)]
        exs = [les.enter_context(nc.sbuf_tensor(P.uid("a0_exs"), [128, 256],
                                                F32)) for _ in range(NR)]
        prb = [les.enter_context(nc.sbuf_tensor(P.uid("a0_prb"), [128, 256],
                                                BF16)) for _ in range(NR)]
        prT = [les.enter_context(nc.sbuf_tensor(P.uid("a0_prT"), [128, 256],
                                                BF16)) for _ in range(NR)]
        srow = [les.enter_context(nc.sbuf_tensor(P.uid("a0_srow"), [128, 4],
                                                 F32)) for _ in range(NR)]
        tbf = les.enter_context(nc.sbuf_tensor(P.uid("a0_tbf"), [128, 512], BF16))

        ev_cos = P.dma("sync", cosT[:, :], dt_in["ropecos"][:, :])
        ev_sin = P.dma("sync", sinT[:, :], dt_in["ropesin"][:, :])
        ev_psw = P.dma("gpsimd", pswb[:, :], dt_in["pswap"][:, :])
        ev_m0 = P.dma("sync", mask0[:, :], dt_in["mask0"][:, :])
        ev_mr = P.dma("sync", maskr[:, :], dt_in["maskr"][:, :])
        ev_wo = P.dma("gpsimd", woT[:, :, :],
                      dt_in["aWoT"].ap().rearrange("(fc p) o -> p fc o",
                                                   p=128))
        ev_bo = P.dma("gpsimd", bo_sb[:, :], dt_in["abo"][:, :])

        emit_ln(P, [t for tg in TILES_S1 for t in tg], xhat)

        wfree = None
        bias_free = [None, None]
        prev_soft = [None, None]
        for hp in range(8):
            evw = []
            for i, n in enumerate(("aWqT", "aWkT", "aWvT")):
                src = dt_in[n].ap().rearrange("(dc p) f -> p dc f", p=128)[
                    :, :, hp * 128:(hp + 1) * 128]
                evw.append(P.dma("gpsimd", wsl[i][:, :, :], src,
                                 waits=[wfree]))
            bb_ = bqkv[hp % 2]
            ev_b = [P.dma("gpsimd", row(bb_[:, i:i + 1, :]),
                          dt_in[n][0:1, hp * 128:(hp + 1) * 128],
                          waits=[bias_free[hp % 2]])
                    for i, n in enumerate(("abq", "abk", "abv"))]
            for which in (0, 1):
                tiles = [t for tg in (TILES_OWN if which == 0 else TILES_S1)
                         for t in tg]
                dstT = qT if which == 0 else kT
                off = HALO if which == 0 else 0
                scale = 0.125 if which == 0 else 1.0
                for (t0, tl) in tiles:
                    bank, gate, bi = P.pool.acquire()
                    for dc in range(DC):
                        P.op("tensor",
                             lambda e_, o=bank[:, 0:tl],
                             l=sq(wsl[which][:, dc, :]),
                             r=sq(xhat[:, dc, t0:t0 + tl]), st=(dc == 0):
                             e_.matmul(o, l, r, start=st, stop=False),
                             waits=[evw[which], xhat.evt, gate])
                    pe = P.op("tensor",
                              lambda e_, o=bank[:, 0:tl],
                              l=row(bb_[:, which:which + 1, :]),
                              r=P.ones_r[:, 0:tl]:
                              e_.matmul(o, l, r, start=False, stop=True),
                              waits=[ev_b[which], P.ev_ones], inc=True)
                    cb = P.op("vector",
                              lambda e_, o=qtmp[:, 0:tl], i=bank[:, 0:tl]:
                              e_.tensor_copy(o, i), waits=[pe], inc=True)
                    bank2, gate2, bi2 = P.pool.acquire()
                    pes = P.op("tensor",
                               lambda e_, o=bank2[:, 0:tl],
                               r=qtmp[:, 0:tl]:
                               e_.matmul(o, pswb[:, :], r, start=True,
                                         stop=True),
                               waits=[cb, ev_psw, gate2], inc=True)
                    P.op("vector",
                         lambda e_, o=rt1[:, 0:tl], i=bank[:, 0:tl],
                         c=cosT[:, t0:t0 + tl], sc=scale:
                         e_.scalar_tensor_tensor(o, i, sc, c, OP.mult,
                                                 OP.mult),
                         waits=[ev_cos])
                    P.op("vector",
                         lambda e_, o=rt2[:, 0:tl], i=bank2[:, 0:tl],
                         c=sinT[:, t0:t0 + tl], sc=scale:
                         e_.scalar_tensor_tensor(o, i, sc, c, OP.mult,
                                                 OP.mult),
                         waits=[pes, ev_sin])
                    ve2 = P.op("vector",
                               lambda e_, o=dstT[:, t0 - off:t0 - off + tl],
                               a=rt1[:, 0:tl], b=rt2[:, 0:tl]:
                               e_.tensor_tensor(o, a, b, OP.add),
                               inc=True)
                    P.pool.release(bi, ve2)
                    P.pool.release(bi2, ve2)
            ev_q = P.now("vector")
            for tc in range(17):
                bank, gate, bi = P.pool.acquire()
                for dc in range(DC):
                    P.op("tensor",
                         lambda e_, o=bank[:, 0:128],
                         l=sq(xhat[:, dc, tc * 128:(tc + 1) * 128]),
                         r=sq(wsl[2][:, dc, :]), st=(dc == 0):
                         e_.matmul(o, l, r, start=st, stop=False),
                         waits=[evw[2], xhat.evt, gate])
                pe = P.op("tensor",
                          lambda e_, o=bank[:, 0:128],
                          r=row(bb_[:, 2:3, :]):
                          e_.matmul(o, P.ones_r[0:1, 0:128], r, start=False,
                                    stop=True),
                          waits=[ev_b[2], P.ev_ones], inc=True)
                ve = P.op("vector",
                          lambda e_, o=sq(v2[:, tc, :]), i=bank[:, 0:128]:
                          e_.tensor_copy(o, i), waits=[pe], inc=True)
                P.pool.release(bi, ve)
            ev_v = P.now("vector")
            wfree = ev_v
            bias_free[hp % 2] = P.now("tensor")

            iters = [(hh, qb) for hh in range(2) for qb in range(16)]
            LAG = 3
            st_ev = {}
            st_bank = {}

            def stage_a(it):
                hh, qb = iters[it]
                h64 = hh * 64
                r_ = it % NR
                bank, gate, bi = P.pool.acquire()
                pes = P.op("tensor",
                           lambda e_, o=bank[:, 0:256],
                           l=qT[h64:h64 + 64, qb * 128:(qb + 1) * 128],
                           r=kT[h64:h64 + 64, qb * 128:qb * 128 + 256]:
                           e_.matmul(o, l, r, start=True, stop=True),
                           waits=[ev_q, gate], inc=True)
                mk = mask0 if qb == 0 else maskr
                sr = srow[r_]
                e1 = P.op("vector",
                          lambda e_, o=sm[r_], a=bank[:, 0:256], b=mk:
                          e_.tensor_tensor(o[:, :], a, b[:, :], OP.add),
                          waits=[pes, ev_m0, ev_mr], inc=True)
                P.pool.release(bi, e1)
                e2 = P.op("vector",
                          lambda e_, o=sr, i=sm[r_]:
                          e_.reduce_max(o[:, 0:1], i[:, :], axis=AX,
                                        negate=True),
                          inc=True)
                e3 = P.op("scalar",
                          lambda e_, o=exs[r_], i=sm[r_], s=sr:
                          e_.activation(o[:, :], i[:, :], AF.Exp,
                                        bias=s[:, 0:1],
                                        accum_out=s[:, 1:2]),
                          waits=[e1, e2], inc=True)
                P.op("scalar", lambda e_, s=sr: e_.activation(
                    s[:, 2:3], s[:, 1:2], AF.Ln))
                e4 = P.op("scalar", lambda e_, s=sr: e_.activation(
                    s[:, 2:3], s[:, 2:3], AF.Exp, scale=-1.0), inc=True)
                e5 = P.op("vector",
                          lambda e_, o=prb[r_], i=exs[r_], s=sr:
                          e_.tensor_scalar(o[:, :], i[:, :], s[:, 2:3],
                                           None, OP.mult),
                          waits=[e3, e4], inc=True)
                st_ev[it] = e5

            def stage_b(it):
                hh, qb = iters[it]
                h64 = hh * 64
                r_ = it % NR
                e5 = st_ev[it]
                for bb in range(2):
                    emit_transpose(P, prT[r_][:, bb * 128:(bb + 1) * 128],
                                   prb[r_][:, bb * 128:(bb + 1) * 128],
                                   waits=[e5])
                eT = P.now("vector")
                bank2, gate2, bi2 = P.pool.acquire()
                pe2 = None
                for bb in range(2):
                    pe2 = P.op("tensor",
                               lambda e_, o=bank2[0:64, 0:128],
                               l=sq(v2[:, qb + bb, h64:h64 + 64]),
                               r=prT[r_][:, bb * 128:(bb + 1) * 128],
                               st=(bb == 0):
                               e_.matmul(o, l, r, start=st, stop=(bb == 1)),
                               waits=[ev_v, eT, gate2], inc=(bb == 1))
                ve = P.op("vector",
                          lambda e_,
                          o=aoT.t[h64:h64 + 64, hp,
                                  qb * 128:(qb + 1) * 128].squeeze(),
                          i=bank2[0:64, 0:128]:
                          e_.tensor_copy(o, i), waits=[pe2], inc=True)
                P.pool.release(bi2, ve)

            for it in range(len(iters) + LAG):
                if it < len(iters):
                    stage_a(it)
                if it >= LAG:
                    stage_b(it - LAG)
        aoT.evt = P.now("vector")

        for (t0, tl) in [t for tg in TILES_OWN for t in tg]:
            for oc in range(DC):
                bank, gate, bi = P.pool.acquire()
                for fc in range(DC):
                    P.op("tensor",
                         lambda e_, o=bank[:, 0:tl],
                         l=sq(woT[:, fc, oc * 128:(oc + 1) * 128]),
                         r=sq(aoT[:, fc, t0 - HALO:t0 - HALO + tl]),
                         st=(fc == 0):
                         e_.matmul(o, l, r, start=st, stop=False),
                         waits=[ev_wo, aoT.evt, gate])
                pe = P.op("tensor",
                          lambda e_, o=bank[:, 0:tl],
                          l=bo_sb[0:1, oc * 128:(oc + 1) * 128],
                          r=P.ones_r[:, 0:tl]:
                          e_.matmul(o, l, r, start=False, stop=True),
                          waits=[ev_bo, P.ev_ones], inc=True)
                P.op("vector",
                     lambda e_, o=tbf[:, 0:tl], i=bank[:, 0:tl]:
                     e_.tensor_copy(o, i), waits=[pe])
                re = P.op("vector",
                          lambda e_, o=sq(P.xT[:, oc, t0:t0 + tl]),
                          i=tbf[:, 0:tl]:
                          e_.tensor_tensor(o, o, i, OP.add),
                          waits=[P.xT.evt], inc=True)
                P.pool.release(bi, re)
        P.xT.evt = P.now("vector")


def emit_elu(P, bank, tl, t1, t2, out_ap, pe):
    np_ = bank.shape[0]
    mn = P.op("vector", lambda e_, o=t1[0:np_, 0:tl], i=bank[0:np_, 0:tl]:
              e_.tensor_scalar(o, i, 0.0, None, OP.min),
              waits=[pe, getattr(P, "_elu_last", None)], inc=True)
    a1 = P.op("scalar", lambda e_, o=t1[0:np_, 0:tl]:
              e_.activation(o, o, AF.Exp), waits=[mn], inc=True)
    P.op("vector", lambda e_, o=t2[0:np_, 0:tl], i=bank[0:np_, 0:tl]:
         e_.tensor_scalar(o, i, 0.0, None, OP.max), waits=[pe])
    ve = P.op("vector", lambda e_, o=out_ap, a=t1[0:np_, 0:tl],
              b=t2[0:np_, 0:tl]:
              e_.tensor_tensor(o, a, b, OP.add), waits=[a1], inc=True)
    P._elu_last = a1
    return ve


def emit_attn_l1(P, dt_in):
    nc = P.nc
    with ExitStack() as les:
        xhat = Buf(les.enter_context(
            nc.sbuf_tensor(P.uid("a1_xhat"), [128, DC, NT_OWN], BF16)))
        wk = les.enter_context(nc.sbuf_tensor(P.uid("a1_wk"), [128, DC, D], BF16))
        wv = les.enter_context(nc.sbuf_tensor(P.uid("a1_wv"), [128, DC, D], BF16))
        wo = les.enter_context(nc.sbuf_tensor(P.uid("a1_wo"), [128, DC, D], BF16))
        biasr = les.enter_context(nc.sbuf_tensor(P.uid("a1_bias"), [1, 4, D], BF16))
        evk = P.dma("gpsimd", wk[:, :, :],
                    dt_in["lWk"].ap().rearrange("(dc p) f -> p dc f", p=128))
        evv = P.dma("gpsimd", wv[:, :, :],
                    dt_in["lWv"].ap().rearrange("(dc p) f -> p dc f", p=128))
        evo = P.dma("gpsimd", wo[:, :, :],
                    dt_in["lWo"].ap().rearrange("(fc p) o -> p fc o", p=128))
        ev_b = [P.dma("gpsimd", row(biasr[:, i:i + 1, :]), dt_in[n][:, :])
                for i, n in enumerate(("lbq", "lbk", "lbv", "lbo"))]

        emit_ln(P, [t for tg in TILES_OWN for t in tg], xhat, toff=HALO)

        kfc = les.enter_context(nc.sbuf_tensor(P.uid("a1_kfc"), [128, D], BF16))
        vc = les.enter_context(nc.sbuf_tensor(P.uid("a1_vc"), [128, D], BF16))
        t1 = les.enter_context(nc.sbuf_tensor(P.uid("a1_t1"), [128, 512], F32))
        t2 = les.enter_context(nc.sbuf_tensor(P.uid("a1_t2"), [128, 512], F32))
        kvs = les.enter_context(nc.sbuf_tensor(P.uid("a1_kvs"), [64, 1040], F32))
        kvz = les.enter_context(nc.sbuf_tensor(P.uid("a1_kvz"), [64, 1040], F32))
        kvb = les.enter_context(nc.sbuf_tensor(P.uid("a1_kvb"), [64, 1040], BF16))
        smk = les.enter_context(nc.sbuf_tensor(P.uid("a1_smk"), [128, 2], F32))
        dens = [les.enter_context(nc.sbuf_tensor(P.uid("a1_dens"),
                                                  [128, 16], F32))
                for _ in range(2)]
        tbf = les.enter_context(nc.sbuf_tensor(P.uid("a1_tbf"), [128, 512], BF16))
        ev_sm = P.dma("sync", smk[:, :], dt_in["seqmask"][:, :])

        wq = les.enter_context(
            nc.sbuf_tensor(P.uid("a1_wq"), [128, DC, D], BF16))
        evq = P.dma("gpsimd", wq[:, :, :],
                    dt_in["lWq"].ap().rearrange("(dc p) f -> p dc f", p=128))
        qfT = [les.enter_context(
            nc.sbuf_tensor(P.uid("a1_qfT"), [64, NH, 512], BF16))
            for _ in range(2)]
        yz = les.enter_context(nc.sbuf_tensor(P.uid("a1_yz"),
                                              [128, 4, D], BF16))
        yzT = les.enter_context(nc.sbuf_tensor(P.uid("a1_yzT"),
                                               [128, DC, 512], BF16))

        kvbankA, gA, iA = P.pool.acquire()
        kvbankB, gB_, iB = P.pool.acquire()
        kvbankC, gC_, iC = P.pool.acquire()
        started = [False, False, False]
        pe_kv = None
        for tc in range(16):
            trow = tc * 128
            for half in range(2):
                bank, gate, bi = P.pool.acquire()
                peh = None
                for dc in range(DC):
                    peh = P.op("tensor",
                               lambda e_, o=bank[:, 0:512],
                               l=sq(xhat[:, dc, trow:trow + 128]),
                               r=sq(wk[:, dc,
                                       half * 512:half * 512 + 512]),
                               st=(dc == 0):
                               e_.matmul(o, l, r, start=st, stop=False),
                               waits=[evk, xhat.evt, gate])
                peh = P.op("tensor",
                           lambda e_, o=bank[:, 0:512],
                           r=row(biasr[0:1, 1:2,
                                       half * 512:half * 512 + 512]):
                           e_.matmul(o, P.ones_r[0:1, 0:128], r, start=False,
                                     stop=True),
                           waits=[ev_b[1], P.ev_ones], inc=True)
                ve = emit_elu(P, bank, 512, t1, t2,
                              kfc[:, half * 512:half * 512 + 512], peh)
                P.pool.release(bi, ve)
                ev_kf = P.now("vector")
                bank2, gate2, bi2 = P.pool.acquire()
                peh = None
                for dc in range(DC):
                    peh = P.op("tensor",
                               lambda e_, o=bank2[:, 0:512],
                               l=sq(xhat[:, dc, trow:trow + 128]),
                               r=sq(wv[:, dc,
                                       half * 512:half * 512 + 512]),
                               st=(dc == 0):
                               e_.matmul(o, l, r, start=st, stop=False),
                               waits=[evv, xhat.evt, gate2])
                peh = P.op("tensor",
                           lambda e_, o=bank2[:, 0:512],
                           r=row(biasr[0:1, 2:3,
                                       half * 512:half * 512 + 512]):
                           e_.matmul(o, P.ones_r[0:1, 0:128], r, start=False,
                                     stop=True),
                           waits=[ev_b[2], P.ev_ones], inc=True)
                ve = P.op("vector",
                          lambda e_, o=vc[:, half * 512:half * 512 + 512],
                          i=bank2[:, 0:512]: e_.tensor_copy(o, i),
                          waits=[peh], inc=True)
                P.pool.release(bi2, ve)
                ev_vc = P.now("vector")
                for hh in range(8):
                    h = half * 8 + hh
                    bk = kvbankA if half == 0 else kvbankB
                    bix = half
                    st = not started[bix]
                    started[bix] = True
                    P.op("tensor",
                         lambda e_, o=bk[0:64, hh * 64:(hh + 1) * 64],
                         l=kfc[:, h * 64:(h + 1) * 64],
                         r=vc[:, h * 64:(h + 1) * 64], s=st:
                         e_.matmul(o, l, r, start=s, stop=False,
                                   skip_group_check=True),
                         waits=[ev_kf, ev_vc,
                                (gA if half == 0 else gB_) if st else None])
                    stC = not started[2]
                    started[2] = True
                    pe_kv = P.op("tensor",
                                 lambda e_, o=kvbankC[0:64, h:h + 1],
                                 l=kfc[:, h * 64:(h + 1) * 64], s=stC:
                                 e_.matmul(o, l, P.ones_c[:, 0:1], start=s,
                                           stop=(tc == 15 and hh == 7),
                                           skip_group_check=True),
                                 waits=[P.ev_ones, gC_ if stC else None],
                                 inc=(tc == 15 and h == NH - 1))
        kvs3 = kvs[:, :].rearrange("p (h c) -> p h c", c=65)
        ev1 = P.op("vector", lambda e_,
                   o=kvs3[:, 0:8, 0:64],
                   i=kvbankA[0:64, 0:512].rearrange("p (h c) -> p h c", c=64):
                   e_.tensor_copy(o, i), waits=[pe_kv], inc=True)
        ev2 = P.op("vector", lambda e_,
                   o=kvs3[:, 8:16, 0:64],
                   i=kvbankB[0:64, 0:512].rearrange("p (h c) -> p h c", c=64):
                   e_.tensor_copy(o, i), waits=[pe_kv], inc=True)
        ev3 = P.op("vector", lambda e_,
                   o=kvs3[:, 0:16, 64:65],
                   i=kvbankC[0:64, 0:16].rearrange("p (h c) -> p h c", c=1):
                   e_.tensor_copy(o, i), waits=[pe_kv], inc=True)
        if DEBUG:
            dkv = nc.dram_tensor("dbg_kvs", [64, 1040], F32,
                                 kind="ExternalOutput")
            P.dma("sync", dkv[:, :], kvs[:, :], waits=[ev1, ev2, ev3])
        P.pool.release(iA, ev1)
        P.pool.release(iB, ev2)
        P.pool.release(iC, ev3)
        z1 = P.op("vector", lambda e_: e_.tensor_scalar(
            kvz[:, :], kvs[:, :], smk[0:64, 0:1], None, OP.mult),
            waits=[ev1, ev2, ev3, ev_sm], inc=True)
        d1 = P.dma("sync", P.cc_in[0:64, :], kvz[:, :], waits=[z1])
        z2 = P.op("vector", lambda e_: e_.tensor_scalar(
            kvz[:, :], kvs[:, :], smk[0:64, 1:2], None, OP.mult),
            waits=[d1], inc=True)
        d2 = P.dma("sync", P.cc_in[64:128, :], kvz[:, :], waits=[z2])
        ccs = P.sems["cc"]
        ccs.n += 1
        P.q["gpsimd"].append((
            ((P.sems["dsy"], P.sems["dsy"].n),),
            lambda e: e.collective_compute(
                "AllReduce", OP.add,
                replica_groups=[list(range(NCORES))],
                ins=[P.cc_in.ap().opt()],
                outs=[P.cc_out.ap().opt()]),
            ((ccs, 1),), None))
        ev_cc = (ccs, ccs.n)
        l1 = P.dma("sync", kvz[:, :], P.cc_out[0:64, :], waits=[ev_cc])
        m1 = P.op("vector", lambda e_: e_.tensor_scalar(
            kvs[:, :], kvz[:, :], smk[0:64, 0:1], None, OP.mult),
            waits=[l1], inc=True)
        l2 = P.dma("sync", kvz[:, :], P.cc_out[64:128, :], waits=[m1])
        P.op("vector", lambda e_: e_.tensor_scalar(
            kvz[:, :], kvz[:, :], smk[0:64, 1:2], None, OP.mult), waits=[l2])
        P.op("vector", lambda e_: e_.tensor_tensor(
            kvs[:, :], kvs[:, :], kvz[:, :], OP.add))
        ev_kvb = P.op("vector", lambda e_: e_.tensor_copy(
            kvb[:, :], kvs[:, :]), inc=True)
        if DEBUG:
            dkb = nc.dram_tensor("dbg_kvb", [64, 1040], BF16,
                                 kind="ExternalOutput")
            P.dma("sync", dkb[:, :], kvb[:, :], waits=[ev_kvb])

        all_tiles = [t for tg in TILES_OWN for t in tg]
        qev = [None, None]

        def emit_qproj(ti):
            t0, tl = all_tiles[ti]
            sl = ti % 2
            for h in range(NH):
                bank, gate, bi = P.pool.acquire()
                for dc in range(DC):
                    P.op("tensor",
                         lambda e_, o=bank[0:64, 0:tl],
                         l=sq(wq[:, dc, h * 64:(h + 1) * 64]),
                         r=sq(xhat[:, dc, t0 - HALO:t0 - HALO + tl]),
                         st=(dc == 0):
                         e_.matmul(o, l, r, start=st, stop=False),
                         waits=[evq, xhat.evt, gate])
                pe = P.op("tensor",
                          lambda e_, o=bank[0:64, 0:tl],
                          l=row(biasr[0:1, 0:1, h * 64:(h + 1) * 64]),
                          r=P.ones_r[:, 0:tl]:
                          e_.matmul(o, l, r, start=False, stop=True),
                          waits=[ev_b[0], P.ev_ones], inc=True)
                ve = emit_elu(P, bank[0:64, :], tl, t1, t2,
                              sq(qfT[sl][0:64, h, 0:tl]), pe)
                P.pool.release(bi, ve)
            qev[sl] = P.now("vector")

        prev_z = None
        emit_qproj(0)
        emit_qproj(1)
        for ti, (t0, tl) in enumerate(all_tiles):
            sl = ti % 2
            ev_qf = qev[sl]
            for k in range(tl // 128):
                par = k % 2
                dn = dens[par]
                bY = [P.pool.acquire() for _ in range(2)]
                bD = P.pool.acquire()
                peY = [None, None]
                peD = None
                for h in range(NH):
                    bk, gk, _ = bY[h // 8]
                    cb = (h % 8) * 64
                    st = (h % 8 == 0)
                    peY[h // 8] = P.op(
                        "tensor",
                        lambda e_, o=bk[:, cb:cb + 64],
                        l=sq(qfT[sl][0:64, h, k * 128:(k + 1) * 128]),
                        r=kvb[0:64, h * 65:h * 65 + 64], s=st:
                        e_.matmul(o, l, r, start=s, stop=False,
                                  skip_group_check=True),
                        waits=[ev_qf, ev_kvb, gk if st else None],
                        inc=(h % 8 == 7))
                    peD = P.op(
                        "tensor",
                        lambda e_, o=bD[0][:, h:h + 1],
                        l=sq(qfT[sl][0:64, h, k * 128:(k + 1) * 128]),
                        r=kvb[0:64, h * 65 + 64:h * 65 + 65], s=(h == 0):
                        e_.matmul(o, l, r, start=s, stop=(h == NH - 1),
                                  skip_group_check=True),
                        waits=[bD[1] if h == 0 else None],
                        inc=(h == NH - 1))
                zc = P.op("vector", lambda e_, o=dn, i=bD[0][:, 0:16]:
                          e_.tensor_scalar(o[:, :], i, EPS_LA, None, OP.add),
                          waits=[peD], inc=True)
                P.pool.release(bD[2], zc)
                P.op("scalar", lambda e_, o=dn: e_.activation(
                    o[:, :], o[:, :], AF.Ln), waits=[zc])
                za = P.op("scalar", lambda e_, o=dn: e_.activation(
                    o[:, :], o[:, :], AF.Exp, scale=-1.0), inc=True)
                last = [None, None]
                for h in range(NH):
                    bk, gk, bik = bY[h // 8]
                    cb = (h % 8) * 64
                    last[h // 8] = P.op(
                        "vector",
                        lambda e_, o=sq(yz[:, k, h * 64:(h + 1) * 64]),
                        i=bk[:, cb:cb + 64], s=dn[:, h:h + 1]:
                        e_.tensor_scalar(o, i, s, None, OP.mult),
                        waits=[za, peY[h // 8]], inc=True)
                for j in range(2):
                    P.pool.release(bY[j][2], last[j])
            if ti + 2 < len(all_tiles):
                emit_qproj(ti + 2)
            for k in range(tl // 128):
                for fc in range(DC):
                    emit_transpose(P, sq(yzT[:, fc, k * 128:(k + 1) * 128]),
                                   sq(yz[:, k, fc * 128:(fc + 1) * 128]))
            ev_yzT = P.now("vector")
            for oc in range(DC):
                bank, gate, bi = P.pool.acquire()
                for fc in range(DC):
                    P.op("tensor",
                         lambda e_, o=bank[:, 0:tl],
                         l=sq(wo[:, fc, oc * 128:(oc + 1) * 128]),
                         r=sq(yzT[:, fc, 0:tl]), st=(fc == 0):
                         e_.matmul(o, l, r, start=st, stop=False),
                         waits=[evo, ev_yzT, gate])
                pe = P.op("tensor",
                          lambda e_, o=bank[:, 0:tl],
                          l=row(biasr[0:1, 3:4, oc * 128:(oc + 1) * 128]),
                          r=P.ones_r[:, 0:tl]:
                          e_.matmul(o, l, r, start=False, stop=True),
                          waits=[ev_b[3], P.ev_ones], inc=True)
                P.op("vector",
                     lambda e_, o=tbf[:, 0:tl], i=bank[:, 0:tl]:
                     e_.tensor_copy(o, i), waits=[pe])
                re = P.op("vector",
                          lambda e_, o=sq(P.xT[:, oc, t0:t0 + tl]),
                          i=tbf[:, 0:tl]:
                          e_.tensor_tensor(o, o, i, OP.add),
                          waits=[P.xT.evt], inc=True)
                P.pool.release(bi, re)
        P.xT.evt = P.now("vector")


_LAYCACHE = {}


def _lay1(w):
    key = id(w)
    if key in _LAYCACHE:
        return _LAYCACHE[key]
    w = np.asarray(w)
    r = np.ascontiguousarray(
        w.reshape(NE, DC, 128, 4, 512).transpose(0, 3, 2, 1, 4)
    ).reshape(NE, 4, 128, 4096)
    _LAYCACHE[key] = r
    return r


def _lay2(w):
    key = id(w)
    if key in _LAYCACHE:
        return _LAYCACHE[key]
    w = np.asarray(w)
    r = np.ascontiguousarray(
        w.reshape(NE, HFC, 128, 4, 256).transpose(0, 3, 2, 1, 4)
    ).reshape(NE, 4, 128, 4096)
    _LAYCACHE[key] = r
    return r


def _swap_mat():
    p = np.zeros((128, 128), dtype=np.float32)
    for i in range(128):
        d = i % 64
        j = (i - d) + (d + 32) % 64
        p[j, i] = 1.0
    return p


def _rope_tables(core):
    cs = core % 4
    rows = np.arange(NT)
    pos = np.maximum(cs * NT_OWN - HALO + rows, 0).astype(np.float64)
    pp = np.arange(128)
    j = (pp % 64) % 32
    inv = 10000.0 ** (-j / 32.0)
    ang = pos[None, :] * inv[:, None]
    cos = np.cos(ang).astype(np.float32)
    sin = np.sin(ang).astype(np.float32)
    sgn = np.where((pp % 64) < 32, -1.0, 1.0).astype(np.float32)
    return cos, (sin * sgn[:, None]).astype(np.float32)


def _masks(core):
    ii = np.arange(128)[:, None]
    jj = np.arange(256)[None, :]
    prev_ok = (jj < 128) & (jj >= ii)
    cur_ok = (jj >= 128) & ((jj - 128) <= ii)
    maskr = np.where(prev_ok | cur_ok, 0.0, -1e30).astype(np.float32)
    if core % 4 == 0:
        mask0 = np.where(cur_ok, 0.0, -1e30).astype(np.float32)
    else:
        mask0 = maskr
    return mask0, maskr


def _host_core_inputs(core, need, x_rows, inputs):
    m = {"xin": x_rows, "ident": np.eye(128, dtype=np.float32)}
    ffn0 = bool(need & {"s1", "s3"})
    ffn1 = bool(need & {"s4", "s6"})
    if ffn0 or ffn1:
        sel = np.zeros((8, 8, 128), dtype=np.float32)
        for e in range(8):
            sel[e, e, :] = 1.0
        m["esel"] = sel.reshape(8, 8 * 128)
    if ffn0:
        m["gWg"] = np.ascontiguousarray(inputs["L0_ffn_Wg"].T)
        m["gW1"] = _lay1(inputs["L0_ffn_W1"])
        m["gb1"] = np.asarray(inputs["L0_ffn_b1"]).reshape(1, -1)
        m["gW2"] = _lay2(inputs["L0_ffn_W2"])
        m["gb2"] = inputs["L0_ffn_b2"]
    if ffn1:
        m["hWg"] = np.ascontiguousarray(inputs["L1_ffn_router_w"].T)
        m["hW1"] = _lay1(inputs["L1_ffn_w1"])
        m["hW3"] = _lay1(inputs["L1_ffn_w3"])
        m["hW2"] = _lay2(inputs["L1_ffn_w2"])
    if "s2" in need:
        m["pswap"] = _swap_mat()
        for k, n in (("aWqT", "L0_attn_Wq"), ("aWkT", "L0_attn_Wk"),
                     ("aWvT", "L0_attn_Wv"), ("aWoT", "L0_attn_Wo")):
            m[k] = np.ascontiguousarray(inputs[n].T)
        for k, n in (("abq", "L0_attn_bq"), ("abk", "L0_attn_bk"),
                     ("abv", "L0_attn_bv"), ("abo", "L0_attn_bo")):
            m[k] = np.ascontiguousarray(inputs[n]).reshape(1, -1)
        m["mask0"], m["maskr"] = _masks(core)
        m["ropecos"], m["ropesin"] = _rope_tables(core)
    if "s5" in need:
        for k, n in (("lWq", "L1_attn_Wq"), ("lWk", "L1_attn_Wk"),
                     ("lWv", "L1_attn_Wv")):
            m[k] = np.ascontiguousarray(np.asarray(inputs[n]).reshape(D, D))
        m["lWo"] = np.ascontiguousarray(
            np.asarray(inputs["L1_attn_Wo"]).reshape(D, D))
        for k, n in (("lbq", "L1_attn_bq"), ("lbk", "L1_attn_bk"),
                     ("lbv", "L1_attn_bv"), ("lbo", "L1_attn_bo")):
            m[k] = np.ascontiguousarray(np.asarray(inputs[n])).reshape(1, -1)
        sm = np.zeros((128, 2), dtype=np.float32)
        sm[:, 0 if core < 4 else 1] = 1.0
        m["seqmask"] = sm
    return m


def _shard_rows(xfull, dtype):
    shards = []
    for c in range(NCORES):
        lo = c * NT_OWN - HALO
        if c % 4 == 0:
            pad = np.zeros((HALO, D), dtype=dtype)
            rows = np.concatenate(
                [pad, xfull[c * NT_OWN:(c + 1) * NT_OWN].astype(dtype,
                                                                copy=False)],
                axis=0)
        else:
            rows = xfull[lo:(c + 1) * NT_OWN].astype(dtype, copy=False)
        shards.append(np.ascontiguousarray(rows))
    return shards


def _run(phases, xfull, inputs, x_in_f32):
    need = set(phases)
    nc = build_program(phases, x_in_f32)
    dt = np.float32 if x_in_f32 else BF
    shards = _shard_rows(xfull, dt)
    in_maps = [_host_core_inputs(c, need, shards[c], inputs)
               for c in range(NCORES)]
    res = run_bass_kernel_spmd(nc, in_maps, core_ids=list(range(NCORES)))
    outs = [np.asarray(r["xout"]) for r in res.results]
    return np.concatenate(outs, axis=0)


def kernel(**inputs):
    x = np.asarray(inputs["L0_attn_x"], dtype=np.float32)
    xfull = np.ascontiguousarray(x.reshape(NT_OWN * NCORES, D))
    if FUSED:
        out = _run(ALL_PHASES, xfull, inputs, x_in_f32=True)
    else:
        cur = xfull
        f32 = True
        for ph in ALL_PHASES:
            out = _run((ph,), cur, inputs, x_in_f32=f32)
            cur = out
            f32 = False
    return np.ascontiguousarray(
        out.reshape(2, 8192, D).astype(np.float32, copy=False))


# version 06 (final)
# speedup vs baseline: 1.0185x; 1.0159x over previous
import numpy as np
import ml_dtypes
from contextlib import ExitStack

import concourse.bass as bass
import concourse.mybir as mybir
from concourse.bass_utils import run_bass_kernel_spmd

F32 = mybir.dt.float32
BF16 = mybir.dt.bfloat16
AX = mybir.AxisListType.X
OP = mybir.AluOpType
AF = mybir.ActivationFunctionType
BF = ml_dtypes.bfloat16

NCORES = 8
D = 1024
DC = 8
NT_OWN = 2048
HALO = 128
NT = NT_OWN + HALO
NH, DHD = 16, 64
NE, HF = 8, 2048
HFC = HF // 128
EPS_LN = 1e-5
EPS_LA = 1e-6
NEG = -1e9

FUSED = True
DEBUG = False

TILES_S1 = [[(0, 128), (128, 512), (640, 512)], [(1152, 512), (1664, 512)]]
TILES_OWN = [[(128, 512), (640, 512)], [(1152, 512), (1664, 512)]]
ALL_PHASES = ("s1", "s2", "s3", "s4", "s5", "s6")

ENGINES = ("sync", "tensor", "vector", "scalar", "gpsimd")


class Sem:
    def __init__(self, h):
        self.h = h
        self.n = 0


class Prog:
    def __init__(self, nc, es):
        self.nc = nc
        self.es = es
        self.q = {e: [] for e in ENGINES}
        self.sems = {}
        for name in ("pe", "ve", "ac", "po", "dpo", "dsy", "cc"):
            self.sems[name] = Sem(es.enter_context(nc.semaphore(name)))
        self.eng_sem = {"tensor": "pe", "vector": "ve", "scalar": "ac",
                        "gpsimd": "po"}

    def op(self, eng, fn, waits=(), inc=False):
        own = self.sems.get(self.eng_sem.get(eng))
        wl = []
        for w in waits:
            if w is None:
                continue
            s, v = w
            if s is own:
                continue
            assert v <= s.n, "wait on future event"
            wl.append((s, v))
        evt = None
        incs = ()
        selfwait = None
        if eng in ("vector", "scalar", "gpsimd"):
            selfwait = own.n
            own.n += 1
            incs = ((own, 1),)
            evt = (own, own.n)
        elif inc:
            own.n += 1
            incs = ((own, 1),)
            evt = (own, own.n)
        self.q[eng].append((tuple(wl), fn, incs, selfwait))
        return evt

    def now(self, eng):
        s = self.sems[self.eng_sem[eng]]
        return (s, s.n)

    def uid(self, base):
        self._uid = getattr(self, "_uid", 0) + 1
        return f"{base}{self._uid}"

    def end_phase(self):
        self.mem_gate = tuple(self.now(e) for e in
                              ("tensor", "vector", "scalar", "gpsimd"))

    def dma(self, eng, out_ap, in_ap, waits=()):
        s = self.sems["dpo" if eng == "gpsimd" else "dsy"]
        wl = []
        for w in tuple(waits) + tuple(getattr(self, "mem_gate", ())):
            if w is None:
                continue
            ss, v = w
            assert v <= ss.n
            wl.append((ss, v))
        s.n += 16
        self.q[eng].append(
            (tuple(wl), lambda e, o=out_ap, i=in_ap: e.dma_start(o, i),
             ((s, 16),), None))
        return (s, s.n)

    def replay(self, name, eng):
        seen = {}
        own = self.sems.get(self.eng_sem.get(name))
        for waits, fn, incs, selfwait in self.q[name]:
            if selfwait is not None and selfwait > 0:
                eng.wait_ge(own.h, selfwait)
            for (s, v) in waits:
                if seen.get(id(s), -1) >= v:
                    continue
                eng.wait_ge(s.h, v)
                seen[id(s)] = v
            inst = fn(eng)
            for (s, n) in incs:
                inst.then_inc(s.h, n)


class Pool:

    def __init__(self, P, n=6):
        self.P = P
        self.banks = [P.es.enter_context(
            P.nc.psum_tensor(f"pool{i}", [128, 512], F32)) for i in range(n)]
        self.gate = [None] * n
        self.used = [False] * n
        self.i = 0

    def acquire(self):
        for _ in range(len(self.banks)):
            i = self.i
            self.i = (self.i + 1) % len(self.banks)
            if not self.used[i]:
                self.used[i] = True
                return self.banks[i], self.gate[i], i
        raise RuntimeError("psum pool exhausted")

    def release(self, idx, evt):
        self.used[idx] = False
        self.gate[idx] = evt


class Buf:
    def __init__(self, t):
        self.t = t
        self.evt = None

    def __getitem__(self, idx):
        return self.t[idx]


def sq(ap):
    return ap.squeeze()


def row(ap):
    return ap.rearrange("p a b -> p (a b)")


def build_program(phases, x_in_f32):
    nc = bass.Bass("TRN2", target_bir_lowering=False)
    es = ExitStack()
    P = Prog(nc, es)
    P.pool = Pool(P, 6)
    P.tp = [es.enter_context(nc.psum_tensor(f"tp{i}", [128, 1024], BF16))
            for i in range(2)]
    P.tp_gate = [None, None]
    P.tp_i = 0

    need = set(phases)
    ffn0 = bool(need & {"s1", "s3"})
    ffn1 = bool(need & {"s4", "s6"})

    dt_in = {}

    def din(name, shape, dtype=F32):
        dt_in[name] = nc.dram_tensor(name, list(shape), dtype,
                                     kind="ExternalInput")
        return dt_in[name]

    xin = din("xin", [NT, D], F32 if x_in_f32 else BF16)
    store_f32 = "s6" in need
    xout = nc.dram_tensor("xout", [NT_OWN, D],
                          F32 if store_f32 else BF16, kind="ExternalOutput")
    din("ident", [128, 128])
    if ffn0 or ffn1:
        din("esel", [8, 8 * 128])
    if ffn0:
        din("gWg", [8, D])
        din("gW1", [NE, 4, 128, 4096])
        din("gb1", [1, NE * HF])
        din("gW2", [NE, 4, 128, 4096])
        din("gb2", [NE, D])
    if ffn1:
        din("hWg", [8, D])
        din("hW1", [NE, 4, 128, 4096])
        din("hW3", [NE, 4, 128, 4096])
        din("hW2", [NE, 4, 128, 4096])
    if "s2" in need:
        din("pswap", [128, 128])
        din("aWqT", [D, D]); din("aWkT", [D, D])
        din("aWvT", [D, D]); din("aWoT", [D, D])
        din("abq", [1, D]); din("abk", [1, D])
        din("abv", [1, D]); din("abo", [1, D])
        din("mask0", [128, 256]); din("maskr", [128, 256])
        din("ropecos", [128, NT]); din("ropesin", [128, NT])
    if "s5" in need:
        din("lWq", [D, D]); din("lWk", [D, D]); din("lWv", [D, D])
        din("lWo", [D, D])
        din("lbq", [1, D]); din("lbk", [1, D]); din("lbv", [1, D])
        din("lbo", [1, D])
        din("seqmask", [128, 2])
        P.cc_in = nc.dram_tensor("cc_in", [128, 1040], F32)
        P.cc_out = nc.dram_tensor("cc_out", [128, 1040], F32)

    xT = Buf(es.enter_context(nc.sbuf_tensor("xT", [128, DC, NT], BF16)))
    identb = es.enter_context(nc.sbuf_tensor("identb", [128, 128], BF16))
    ones_r = es.enter_context(nc.sbuf_tensor("ones_r", [1, 512], BF16))
    ones_c = es.enter_context(nc.sbuf_tensor("ones_c", [128, 1], BF16))
    ones_f = es.enter_context(nc.sbuf_tensor("ones_f", [1, 128], F32))
    P.xT, P.identb = xT, identb
    P.ones_r, P.ones_c, P.ones_f = ones_r, ones_c, ones_f

    P.ld_id = P.dma("gpsimd", identb[:, :], dt_in["ident"][:, :])
    P.op("gpsimd", lambda e: e.memset(ones_r[:, :], 1.0))
    P.op("gpsimd", lambda e: e.memset(ones_c[:, :], 1.0))
    P.ev_ones = P.op("gpsimd", lambda e: e.memset(ones_f[:, :], 1.0),
                     inc=True)

    emit_load_x(P, xin, cast=x_in_f32)

    for ph in ALL_PHASES:
        if ph not in need:
            continue
        if ph == "s1":
            tiles = TILES_S1 if len(need) > 1 else TILES_OWN
            emit_ffn(P, dt_in, layer=0, halves=tiles)
        elif ph == "s3":
            emit_ffn(P, dt_in, layer=0, halves=TILES_OWN)
        elif ph in ("s4", "s6"):
            emit_ffn(P, dt_in, layer=1, halves=TILES_OWN)
        elif ph == "s2":
            emit_attn_l0(P, dt_in)
        elif ph == "s5":
            emit_attn_l1(P, dt_in)
        P.end_phase()

    emit_store_x(P, xout, as_f32=store_f32)

    fin = [(P.sems["dsy"], P.sems["dsy"].n), (P.sems["dpo"], P.sems["dpo"].n)]
    P.q["gpsimd"].append((tuple(fin), lambda e: e.engine_nop(), (), None))

    with nc.Block() as block:
        @block.sync
        def _(eng):
            P.replay("sync", eng)

        @block.tensor
        def _(eng):
            P.replay("tensor", eng)

        @block.vector
        def _(eng):
            P.replay("vector", eng)

        @block.scalar
        def _(eng):
            P.replay("scalar", eng)

        @block.gpsimd
        def _(eng):
            P.replay("gpsimd", eng)

    es.close()
    return nc


def emit_transpose(P, dst_ap, src_ap, waits=()):
    i = P.tp_i
    P.tp_i ^= 1
    t = P.tp[i]
    npart = src_ap.shape[0]
    nf = src_ap.shape[-1]
    pe = P.op("tensor",
              lambda e, o=t, s=src_ap, np_=npart, nf_=nf:
              e.transpose(o[0:nf_, 0:np_], s, P.identb[0:np_, 0:np_]),
              waits=list(waits) + [P.tp_gate[i], P.ld_id], inc=True)
    ve = P.op("vector",
              lambda e, o=dst_ap, i_=t, np_=npart, nf_=nf:
              e.tensor_copy(o, i_[0:nf_, 0:np_]),
              waits=[pe], inc=True)
    P.tp_gate[i] = ve
    return ve


def emit_load_x(P, xin, cast):
    nc = P.nc
    with ExitStack() as les:
        xtok = les.enter_context(nc.sbuf_tensor(P.uid("xtok"), [128, 17, D], BF16))
        src = xin.ap().rearrange("(tc p) f -> p tc f", p=128)
        eng = "gpsimd" if cast else "sync"
        ev = P.dma(eng, xtok[:, :, :], src)
        for tc in range(17):
            for fc in range(DC):
                emit_transpose(P, sq(P.xT[:, fc, tc * 128:(tc + 1) * 128]),
                               sq(xtok[:, tc, fc * 128:(fc + 1) * 128]),
                               waits=[ev])
        P.xT.evt = P.now("vector")
    P.end_phase()


def emit_store_x(P, xout, as_f32):
    nc = P.nc
    es = P.es
    dtype = F32 if as_f32 else BF16
    NB = 4
    obuf = [es.enter_context(nc.sbuf_tensor(f"obuf{i}", [128, D], dtype))
            for i in range(NB)]
    ogate = [None] * NB
    halves = getattr(P, "xT_half", [])
    hev = [halves[-2] if len(halves) >= 2 else P.xT.evt,
           halves[-1] if halves else P.xT.evt]
    dst = xout.ap().rearrange("(tc p) f -> p tc f", p=128)
    for tc in range(16):
        ob = obuf[tc % NB]
        gate_ev = hev[0] if tc < 8 else hev[1]
        last = None
        for fc in range(DC):
            last = emit_transpose(
                P, ob[:, fc * 128:(fc + 1) * 128],
                sq(P.xT[:, fc, HALO + tc * 128:HALO + (tc + 1) * 128]),
                waits=[gate_ev, ogate[tc % NB]])
        ogate[tc % NB] = P.dma("sync", sq(dst[:, tc, :]), ob[:, :],
                               waits=[last])


def emit_ln(P, tiles, out_xhat, les=None, toff=0):
    nc = P.nc
    les = ExitStack()
    xsq = les.enter_context(nc.sbuf_tensor(P.uid("ln_xsq"), [128, 512], BF16))
    rows = les.enter_context(nc.sbuf_tensor(P.uid("ln_rows"), [1, 2048], F32))
    nmB = les.enter_context(nc.sbuf_tensor(P.uid("ln_nmB"), [128, 512], BF16))
    rsB = les.enter_context(nc.sbuf_tensor(P.uid("ln_rsB"), [128, 512], BF16))
    tb = les.enter_context(nc.sbuf_tensor(P.uid("ln_tb"), [128, 512], BF16))
    tile_evts = []
    for (t0, tl) in tiles:
        bank1, g1, i1 = P.pool.acquire()
        bank2, g2, i2 = P.pool.acquire()
        sx = bank1[0:1, 0:tl]
        sx2 = bank2[0:1, 0:tl]
        pe2 = None
        pe = None
        for c in range(DC):
            xc = sq(P.xT[:, c, t0:t0 + tl])
            ev_sq = P.op("scalar",
                         lambda e, o=xsq[:, 0:tl], i=xc:
                         e.activation(o, i, AF.Square),
                         waits=[P.xT.evt, g1, g2, pe2], inc=True)
            pe = P.op("tensor",
                      lambda e, o=sx, i=xc, st=(c == 0), sp=(c == DC - 1):
                      e.matmul(o, P.ones_c[:, :], i, start=st, stop=sp),
                      waits=[P.xT.evt, g1, P.ev_ones],
                      inc=(c == DC - 1))
            pe2 = P.op("tensor",
                       lambda e, o=sx2, i=xsq[:, 0:tl], st=(c == 0),
                       sp=(c == DC - 1):
                       e.matmul(o, P.ones_c[:, :], i, start=st, stop=sp),
                       waits=[ev_sq, g2], inc=True)
        mrow = rows[:, 0:tl]
        vrow = rows[:, 512:512 + tl]
        rrow = rows[:, 1024:1024 + tl]
        nrow = rows[:, 1536:1536 + tl]
        e1 = P.op("vector", lambda e, o=mrow, i=sx: e.tensor_scalar(
            o, i, 1.0 / D, None, OP.mult), waits=[pe], inc=True)
        P.op("vector", lambda e, o=vrow, i=sx2: e.tensor_scalar(
            o, i, 1.0 / D, None, OP.mult), waits=[pe2])
        P.op("vector", lambda e, o=nrow, i=mrow: e.scalar_tensor_tensor(
            o, i, -1.0, i, OP.mult, OP.mult))
        P.op("vector", lambda e, o=vrow, i=nrow: e.tensor_tensor(
            o, o, i, OP.add))
        e3 = P.op("vector", lambda e, o=vrow: e.tensor_scalar(
            o, o, EPS_LN, None, OP.add), inc=True)
        P.op("scalar", lambda e, o=vrow: e.activation(o, o, AF.Ln),
             waits=[e3])
        e5 = P.op("scalar", lambda e, o=rrow, i=vrow: e.activation(
            o, i, AF.Exp, scale=-0.5), inc=True)
        e6 = P.op("vector", lambda e, o=nrow, i=mrow: e.tensor_scalar(
            o, i, -1.0, None, OP.mult), waits=[e1], inc=True)
        P.pool.release(i1, e6)
        bank3, g3, i3 = P.pool.acquire()
        bank4, g4, i4 = P.pool.acquire()
        pb1 = P.op("tensor", lambda e, o=bank3[:, 0:tl], r=nrow: e.matmul(
            o, P.ones_f[:, :], r, start=True, stop=True),
            waits=[e6, g3, P.ev_ones], inc=True)
        pb2 = P.op("tensor", lambda e, o=bank4[:, 0:tl], r=rrow: e.matmul(
            o, P.ones_f[:, :], r, start=True, stop=True),
            waits=[e5, g4], inc=True)
        P.pool.release(i2, pb2)
        c1 = P.op("vector", lambda e, o=nmB[:, 0:tl], i=bank3[:, 0:tl]:
                  e.tensor_copy(o, i), waits=[pb1], inc=True)
        c2 = P.op("vector", lambda e, o=rsB[:, 0:tl], i=bank4[:, 0:tl]:
                  e.tensor_copy(o, i), waits=[pb2], inc=True)
        P.pool.release(i3, c1)
        P.pool.release(i4, c2)
        last = None
        for c in range(DC):
            xc = sq(P.xT[:, c, t0:t0 + tl])
            oc_ = sq(out_xhat[:, c, t0 - toff:t0 - toff + tl])
            P.op("vector", lambda e, o=tb[:, 0:tl], i=xc, m=nmB[:, 0:tl]:
                 e.tensor_tensor(o, i, m, OP.add), waits=[P.xT.evt, c1])
            last = P.op("vector", lambda e, o=oc_, i=tb[:, 0:tl],
                        r=rsB[:, 0:tl]:
                        e.tensor_tensor(o, i, r, OP.mult),
                        waits=[c2], inc=True)
        tile_evts.append(last)
    out_xhat.evt = P.now("vector")
    P.end_phase()
    les.close()
    return tile_evts


def emit_routing(P, dt_in, layer, tiles, xhat, cT, les_, wgB, ev_wg,
                 toff=0):
    nc = P.nc
    les = ExitStack()
    s = les.enter_context(nc.sbuf_tensor(P.uid("rt_s"), [128, 8], F32))
    pr = les.enter_context(nc.sbuf_tensor(P.uid("rt_pr"), [128, 8], F32))
    eq = les.enter_context(nc.sbuf_tensor(P.uid("rt_eq"), [128, 8], F32))
    aa = les.enter_context(nc.sbuf_tensor(P.uid("rt_aa"), [128, 8], F32))
    bb = les.enter_context(nc.sbuf_tensor(P.uid("rt_bb"), [128, 8], F32))
    f1 = les.enter_context(nc.sbuf_tensor(P.uid("rt_f1"), [128, 8], F32))
    f2 = les.enter_context(nc.sbuf_tensor(P.uid("rt_f2"), [128, 8], F32))
    cw = les.enter_context(nc.sbuf_tensor(P.uid("rt_cw"), [128, 8], F32))
    cwb = les.enter_context(nc.sbuf_tensor(P.uid("rt_cwb"), [128, 8], BF16))
    col = les.enter_context(nc.sbuf_tensor(P.uid("rt_col"), [128, 4], F32))

    def cumsum_excl():
        P.op("vector", lambda e: e.memset(aa[:, 0:1], 0.0))
        P.op("vector", lambda e: e.tensor_copy(aa[:, 1:8], eq[:, 0:7]))
        P.op("vector", lambda e: e.tensor_copy(bb[:, :], aa[:, :]))
        P.op("vector", lambda e: e.tensor_tensor(
            bb[:, 2:8], aa[:, 2:8], aa[:, 0:6], OP.add))
        P.op("vector", lambda e: e.tensor_copy(aa[:, :], bb[:, :]))
        P.op("vector", lambda e: e.tensor_tensor(
            aa[:, 4:8], bb[:, 4:8], bb[:, 0:4], OP.add))

    prev_act = None
    for (t0, tl) in tiles:
        for k in range(tl // 128):
            tt = t0 + k * 128
            bank, gate, bi = P.pool.acquire()
            lg = bank[:, 0:8]
            pe = None
            for dc in range(DC):
                pe = P.op("tensor",
                          lambda e, o=lg,
                          l=sq(xhat[:, dc, tt - toff:tt - toff + 128]),
                          r=sq(wgB[:, dc, :]), st=(dc == 0),
                          sp=(dc == DC - 1):
                          e.matmul(o, l, r, start=st, stop=sp),
                          waits=[xhat.evt, ev_wg, gate],
                          inc=(dc == DC - 1))
            if layer == 0:
                m = P.op("vector", lambda e, i=lg: e.reduce_max(
                    col[:, 0:1], i, axis=AX, negate=True),
                    waits=[pe, prev_act], inc=True)
                ex = P.op("scalar", lambda e, i=lg: e.activation(
                    pr[:, :], i, AF.Exp, bias=col[:, 0:1],
                    accum_out=col[:, 1:2]), waits=[pe, m], inc=True)
                P.op("scalar", lambda e: e.activation(
                    col[:, 2:3], col[:, 1:2], AF.Ln))
                iv = P.op("scalar", lambda e: e.activation(
                    col[:, 2:3], col[:, 2:3], AF.Exp, scale=-1.0), inc=True)
                P.op("vector", lambda e: e.tensor_scalar(
                    s[:, :], pr[:, :], col[:, 2:3], None, OP.mult),
                    waits=[ex, iv])
            else:
                P.op("vector", lambda e, i=lg: e.tensor_copy(s[:, :], i),
                     waits=[pe, prev_act])
            P.op("vector", lambda e: e.reduce_max(col[:, 0:1], s[:, :],
                                                  axis=AX))
            P.op("vector", lambda e: e.tensor_scalar(
                eq[:, :], s[:, :], col[:, 0:1], 0.0, OP.subtract, OP.is_ge))
            cumsum_excl()
            P.op("vector", lambda e: e.tensor_scalar(
                f1[:, :], aa[:, :], 0.0, None, OP.is_equal))
            P.op("vector", lambda e: e.tensor_tensor(
                f1[:, :], f1[:, :], eq[:, :], OP.mult))
            P.op("vector", lambda e: e.scalar_tensor_tensor(
                pr[:, :], f1[:, :], NEG, s[:, :], OP.mult, OP.add))
            P.op("vector", lambda e: e.reduce_max(col[:, 1:2], pr[:, :],
                                                  axis=AX))
            P.op("vector", lambda e: e.tensor_scalar(
                eq[:, :], pr[:, :], col[:, 1:2], 0.0, OP.subtract, OP.is_ge))
            cumsum_excl()
            P.op("vector", lambda e: e.tensor_scalar(
                f2[:, :], aa[:, :], 0.0, None, OP.is_equal))
            P.op("vector", lambda e: e.tensor_tensor(
                f2[:, :], f2[:, :], eq[:, :], OP.mult))
            if layer == 0:
                P.op("vector", lambda e: e.tensor_tensor(
                    f1[:, :], f1[:, :], f2[:, :], OP.add))
                P.op("vector", lambda e: e.tensor_tensor(
                    cw[:, :], f1[:, :], s[:, :], OP.mult))
                cl = P.op("vector", lambda e: e.tensor_scalar(
                    cw[:, :], cw[:, :], 0.5, None, OP.mult), inc=True)
            else:
                dd = P.op("vector", lambda e: e.tensor_tensor(
                    col[:, 2:3], col[:, 1:2], col[:, 0:1], OP.subtract),
                    inc=True)
                x1 = P.op("scalar", lambda e: e.activation(
                    col[:, 2:3], col[:, 2:3], AF.Exp), waits=[dd], inc=True)
                a1 = P.op("vector", lambda e: e.tensor_scalar(
                    col[:, 2:3], col[:, 2:3], 1.0, None, OP.add),
                    waits=[x1], inc=True)
                P.op("scalar", lambda e: e.activation(
                    col[:, 2:3], col[:, 2:3], AF.Ln), waits=[a1])
                g1 = P.op("scalar", lambda e: e.activation(
                    col[:, 2:3], col[:, 2:3], AF.Exp, scale=-1.0), inc=True)
                P.op("vector", lambda e: e.tensor_scalar(
                    col[:, 2:3], col[:, 2:3], 0.5, None, OP.mult),
                    waits=[g1])
                P.op("vector", lambda e: e.tensor_scalar(
                    col[:, 3:4], col[:, 2:3], -1.0, 0.5, OP.mult, OP.add))
                P.op("vector", lambda e: e.tensor_scalar(
                    f1[:, :], f1[:, :], col[:, 2:3], None, OP.mult))
                P.op("vector", lambda e: e.tensor_scalar(
                    f2[:, :], f2[:, :], col[:, 3:4], None, OP.mult))
                cl = P.op("vector", lambda e: e.tensor_tensor(
                    cw[:, :], f1[:, :], f2[:, :], OP.add), inc=True)
            P.pool.release(bi, cl)
            cb = P.op("vector", lambda e: e.tensor_copy(cwb[:, :], cw[:, :]),
                      inc=True)
            emit_transpose(P, cT[0:8, tt:tt + 128], cwb[:, :], waits=[cb])
            prev_act = P.now("scalar")
    if DEBUG and getattr(P, "_rt_dumped", False) is False:
        P._rt_dumped = True
        ve_now = P.now("vector")
        for nm, tt_ in (("rs", s), ("rf1", f1), ("rf2", f2), ("rcw", cw),
                        ("raa", aa), ("req", eq), ("rcol", col)):
            dd = nc.dram_tensor("dbg_" + nm, list(tt_.t.shape
                                if hasattr(tt_, 't') else
                                [128, tt_.free_size]), F32,
                                kind="ExternalOutput") \
                if False else nc.dram_tensor(
                "dbg_" + nm, [128, 8 if nm != "rcol" else 4], F32,
                kind="ExternalOutput")
            P.dma("sync", dd[:, :], tt_[:, :], waits=[ve_now])
    cT.evt = P.now("vector")
    P.end_phase()
    les.close()


def emit_ffn(P, dt_in, layer, halves):
    nc = P.nc
    with ExitStack() as les:
        hw = max(sum(tl for _, tl in tg) for tg in halves)
        xhat = Buf(les.enter_context(
            nc.sbuf_tensor(P.uid("ffn_xhat"), [128, DC, hw], BF16)))
        z = Buf(les.enter_context(
            nc.sbuf_tensor(P.uid("ffn_z"), [128, HFC, hw], BF16)))
        acc = les.enter_context(
            nc.sbuf_tensor(P.uid("ffn_acc"), [128, DC, hw], BF16))
        cT = Buf(les.enter_context(nc.sbuf_tensor(P.uid("ffn_cT"), [8, NT], BF16)))
        gBs = les.enter_context(nc.sbuf_tensor(P.uid("ffn_gB"), [128, 3, 512], F32))
        tbf = les.enter_context(nc.sbuf_tensor(P.uid("ffn_tbf"), [128, 512], BF16))
        sres = les.enter_context(nc.sbuf_tensor(P.uid("ffn_sres"), [128, 512], F32))
        tmg = les.enter_context(nc.sbuf_tensor(P.uid("ffn_tmg"), [128, 512], F32))
        eselb = les.enter_context(nc.sbuf_tensor(P.uid("eselb"), [8, 8, 128], BF16))
        wgT = les.enter_context(nc.sbuf_tensor(P.uid("rt_wgT"), [8, D], BF16))
        wgB = les.enter_context(nc.sbuf_tensor(P.uid("rt_wgB"), [128, DC, 8], BF16))
        nslot = 6 if layer == 1 else 5
        slots = [les.enter_context(
            nc.sbuf_tensor(P.uid("wslot"), [128, 4096], BF16))
            for i in range(nslot)]
        slot_free = [None] * nslot
        slot_ready = [None] * nslot
        slot_i = [0]

        ev_es = P.dma("gpsimd", eselb[:, :, :],
                      dt_in["esel"].ap().rearrange("e (g f) -> e g f", f=128))
        ev_wgT = P.dma("gpsimd", wgT[:, :],
                       dt_in["gWg" if layer == 0 else "hWg"][:, :])
        for dc in range(DC):
            emit_transpose(P, sq(wgB[:, dc, :]),
                           wgT[:, dc * 128:(dc + 1) * 128], waits=[ev_wgT])
        ev_wg = P.now("vector")

        if layer == 0:
            w1ap = dt_in["gW1"]
            w2ap = dt_in["gW2"]
            b1sb = [les.enter_context(
                nc.sbuf_tensor(P.uid("b1sb"), [1, HF], BF16)) for i in range(2)]
            b1_ready = [None, None]
            b1_free = [None, None]
            b2sb = les.enter_context(nc.sbuf_tensor(P.uid("b2sb"), [NE, D], BF16))
            ev_b2 = P.dma("gpsimd", b2sb[:, :], dt_in["gb2"][:, :])
            b1ap = dt_in["gb1"].ap().rearrange("o (e h) -> o e h", e=NE)
        else:
            w1ap = dt_in["hW1"]
            w3ap = dt_in["hW3"]
            w2ap = dt_in["hW2"]
            ev_b1 = ev_b2 = None

        def slot_load(src_ap, split):
            i = slot_i[0]
            slot_i[0] = (slot_i[0] + 1) % len(slots)
            ev = P.dma("gpsimd", slots[i][:, :], src_ap,
                       waits=[slot_free[i]])
            slot_ready[i] = ev
            return i

        def w1_slab(w, e, qq):
            return sq(w[e:e + 1, qq, :, :])

        def w2_slab(e, qq):
            return sq(w2ap[e:e + 1, qq, :, :])

        for half, tiles in enumerate(halves):
            t0h = tiles[0][0]
            sched = []
            for e in range(NE):
                for qq in range(4):
                    sched.append(("w1", e, qq))
                    if layer == 1:
                        sched.append(("w3", e, qq))
                for qq in range(4):
                    sched.append(("w2", e, qq))
            slot_of = {}
            nload = [0]

            def load_next():
                if nload[0] >= len(sched):
                    return
                kind, e, qq = sched[nload[0]]
                if kind == "w1":
                    src, split = w1_slab(w1ap, e, qq), False
                elif kind == "w3":
                    src, split = w1_slab(w3ap, e, qq), False
                else:
                    src, split = w2_slab(e, qq), True
                slot_of[(kind, e, qq)] = slot_load(src, split)
                nload[0] += 1

            for _ in range(min(nslot, len(sched))):
                load_next()

            xh_evts = emit_ln(P, tiles, xhat, toff=t0h)
            emit_routing(P, dt_in, layer, tiles, xhat, cT, les, wgB, ev_wg,
                         toff=t0h)
            if DEBUG and half == 0:
                d1 = nc.dram_tensor("dbg_xhat", [128, DC * hw], BF16,
                                    kind="ExternalOutput")
                P.dma("sync", d1[:, :],
                      xhat.t[:, :, :].rearrange("p a b -> p (a b)"),
                      waits=[xhat.evt])
                d2 = nc.dram_tensor("dbg_cT", [8, NT], BF16,
                                    kind="ExternalOutput")
                P.dma("sync", d2[:, :], cT.t[:, :], waits=[cT.evt])

            last_ze = None
            if layer == 0:
                b1_ready[0] = P.dma("gpsimd", b1sb[0][:, :], row(b1ap[:, 0:1, :]))
            for e in range(NE):
                if layer == 0 and e + 1 < NE:
                    b1_ready[(e + 1) % 2] = P.dma(
                        "gpsimd", b1sb[(e + 1) % 2][:, :],
                        row(b1ap[:, e + 1:e + 2, :]), waits=[b1_free[(e + 1) % 2]])
                gev = []
                for ti, (t0, tl) in enumerate(tiles):
                    bank, gate, bi = P.pool.acquire()
                    pe = P.op("tensor", lambda e_, o=bank[:, 0:tl],
                              l=sq(eselb[:, e:e + 1, :]),
                              r=cT[0:8, t0:t0 + tl]:
                              e_.matmul(o, l, r, start=True, stop=True),
                              waits=[cT.evt, ev_es, gate], inc=True)
                    ge = P.op("vector", lambda e_, o=sq(gBs[:, ti, 0:tl]),
                              i=bank[:, 0:tl]: e_.tensor_copy(o, i),
                              waits=[pe], inc=True)
                    gev.append(ge)
                    P.pool.release(bi, ge)

                for qq in range(4):
                    si1 = slot_of[("w1", e, qq)]
                    sl1 = slots[si1][:, :].rearrange("p (a b) -> p a b", a=8)
                    if layer == 1:
                        si3 = slot_of[("w3", e, qq)]
                        sl3 = slots[si3][:, :].rearrange("p (a b) -> p a b",
                                                         a=8)
                    for hci in range(4):
                        hc = qq * 4 + hci
                        nb = len(tiles)
                        bks = [P.pool.acquire() for _ in range(nb)]
                        pe1_t = [None] * len(tiles)
                        for dc in range(DC):
                            for ti, (t0, tl) in enumerate(tiles):
                                bank, gate, bi = bks[ti]
                                lastmm = (dc == DC - 1) and (layer == 1)
                                pe1_t[ti] = P.op(
                                    "tensor",
                                    lambda e_, o=bank[:, 0:tl],
                                    l=sq(sl1[:, dc,
                                             hci * 128:(hci + 1) * 128]),
                                    r=sq(xhat[:, dc, t0 - t0h:t0 - t0h + tl]),
                                    st=(dc == 0), sp=lastmm:
                                    e_.matmul(o, l, r, start=st, stop=sp),
                                    waits=[slot_ready[si1], xh_evts[ti],
                                           gate],
                                    inc=lastmm)
                        if layer == 0:
                            for ti, (t0, tl) in enumerate(tiles):
                                bank, gate, bi = bks[ti]
                                pe = P.op(
                                    "tensor",
                                    lambda e_, o=bank[:, 0:tl],
                                    l=b1sb[e % 2][0:1,
                                                  hc * 128:(hc + 1) * 128],
                                    r=P.ones_r[:, 0:tl]:
                                    e_.matmul(o, l, r, start=False,
                                              stop=True),
                                    waits=[b1_ready[e % 2], P.ev_ones],
                                    inc=True)
                                ze = P.op(
                                    "scalar",
                                    lambda e_,
                                    o=sq(z[:, hc, t0 - t0h:t0 - t0h + tl]),
                                    i=bank[:, 0:tl]:
                                    e_.activation(o, i, AF.Relu),
                                    waits=[pe], inc=True)
                                P.pool.release(bi, ze)
                        else:
                            bks2 = [P.pool.acquire() for _ in range(nb)]
                            pe2_t = [None] * nb
                            for dc in range(DC):
                                for ti, (t0, tl) in enumerate(tiles):
                                    bank2, gate2, bi2 = bks2[ti]
                                    pe2_t[ti] = P.op(
                                        "tensor",
                                        lambda e_, o=bank2[:, 0:tl],
                                        l=sq(sl3[:, dc,
                                                 hci * 128:(hci + 1) * 128]),
                                        r=sq(xhat[:, dc,
                                                  t0 - t0h:t0 - t0h + tl]),
                                        st=(dc == 0), sp=(dc == DC - 1):
                                        e_.matmul(o, l, r, start=st, stop=sp),
                                        waits=[slot_ready[si3], xh_evts[ti],
                                               gate2],
                                        inc=(dc == DC - 1))
                            for ti, (t0, tl) in enumerate(tiles):
                                bank, gate, bi = bks[ti]
                                bank2, gate2, bi2 = bks2[ti]
                                pe = pe1_t[ti]
                                ge_ = P.op(
                                    "scalar",
                                    lambda e_, o=tmg[:, 0:tl],
                                    i=bank[:, 0:tl]:
                                    e_.activation(o, i, AF.Silu),
                                    waits=[pe, last_ze], inc=True)
                                ze = P.op(
                                    "vector",
                                    lambda e_,
                                    o=sq(z[:, hc, t0 - t0h:t0 - t0h + tl]),
                                    a=tmg[:, 0:tl], b=bank2[:, 0:tl]:
                                    e_.tensor_tensor(o, a, b, OP.mult),
                                    waits=[ge_, pe2_t[ti]], inc=True)
                                last_ze = ze
                                P.pool.release(bi, ge_)
                                P.pool.release(bi2, ze)
                    slot_free[si1] = P.now("tensor")
                    if layer == 1:
                        slot_free[si3] = P.now("tensor")
                    load_next()
                    if layer == 1:
                        load_next()
                if layer == 0:
                    b1_free[e % 2] = P.now("tensor")
                z.evt = P.now("vector" if layer == 1 else "scalar")
                if DEBUG and half == 0 and e == 0:
                    d3 = nc.dram_tensor("dbg_z0", [128, HFC * hw], BF16,
                                        kind="ExternalOutput")
                    P.dma("sync", d3[:, :],
                          z.t[:, :, :].rearrange("p a b -> p (a b)"),
                          waits=[z.evt])

                for qq in range(4):
                    si2 = slot_of[("w2", e, qq)]
                    sl2 = slots[si2][:, :].rearrange("p (a b) -> p a b", a=16)
                    for oci in range(2):
                        oc = qq * 2 + oci
                        nb = len(tiles)
                        bks = [P.pool.acquire() for _ in range(nb)]
                        pe_t = [None] * nb
                        for fc in range(HFC):
                            for ti, (t0, tl) in enumerate(tiles):
                                bank, gate, bi = bks[ti]
                                pe_t[ti] = P.op(
                                    "tensor",
                                    lambda e_, o=bank[:, 0:tl],
                                    l=sq(sl2[:, fc,
                                             oci * 128:(oci + 1) * 128]),
                                    r=sq(z[:, fc, t0 - t0h:t0 - t0h + tl]),
                                    st=(fc == 0), sp=(fc == HFC - 1):
                                    e_.matmul(o, l, r, start=st, stop=sp),
                                    waits=[slot_ready[si2], z.evt, gate],
                                    inc=(fc == HFC - 1))
                        for ti, (t0, tl) in enumerate(tiles):
                            bank, gate, bi = bks[ti]
                            accv = sq(acc[:, oc, t0 - t0h:t0 - t0h + tl])
                            if e == 0:
                                ae = P.op("vector",
                                          lambda e_, o=accv, i=bank[:, 0:tl],
                                          g=sq(gBs[:, ti, 0:tl]):
                                          e_.tensor_tensor(o, i, g, OP.mult),
                                          waits=[pe_t[ti], gev[ti]],
                                          inc=True)
                            else:
                                P.op("vector",
                                     lambda e_, o=tbf[:, 0:tl],
                                     i=bank[:, 0:tl], g=sq(gBs[:, ti, 0:tl]):
                                     e_.tensor_tensor(o, i, g, OP.mult),
                                     waits=[pe_t[ti], gev[ti]])
                                ae = P.op("vector",
                                          lambda e_, o=accv, i=tbf[:, 0:tl]:
                                          e_.tensor_tensor(o, o, i, OP.add),
                                          inc=True)
                            P.pool.release(bi, ae)
                    slot_free[si2] = P.now("tensor")
                    load_next()
            acc_evt = P.now("vector")
            if DEBUG and half == 0:
                d4 = nc.dram_tensor("dbg_acc", [128, DC * hw], BF16,
                                    kind="ExternalOutput")
                P.dma("sync", d4[:, :],
                      acc[:, :, :].rearrange("p a b -> p (a b)"),
                      waits=[acc_evt])

            for ti, (t0, tl) in enumerate(tiles):
                for oc in range(DC):
                    accv = sq(acc[:, oc, t0 - t0h:t0 - t0h + tl])
                    xv = sq(P.xT[:, oc, t0:t0 + tl])
                    if layer == 0:
                        bank, gate, bi = P.pool.acquire()
                        pe = P.op("tensor",
                                  lambda e_, o=bank[:, 0:tl],
                                  l=b2sb[:, oc * 128:(oc + 1) * 128],
                                  r=cT[0:8, t0:t0 + tl]:
                                  e_.matmul(o, l, r, start=True, stop=True),
                                  waits=[ev_b2, cT.evt, gate], inc=True)
                        P.op("vector", lambda e_, o=sres[:, 0:tl], a=xv,
                             b=accv: e_.tensor_tensor(o, a, b, OP.add),
                             waits=[acc_evt, P.xT.evt])
                        re = P.op("vector",
                                  lambda e_, o=xv, a=sres[:, 0:tl],
                                  b=bank[:, 0:tl]:
                                  e_.tensor_tensor(o, a, b, OP.add),
                                  waits=[pe], inc=True)
                        P.pool.release(bi, re)
                    else:
                        P.op("vector",
                             lambda e_, o=xv, a=xv, b=accv:
                             e_.tensor_tensor(o, a, b, OP.add),
                             waits=[acc_evt, P.xT.evt], inc=True)
            P.xT.evt = P.now("vector")
            P.xT_half = getattr(P, "xT_half", [])
            P.xT_half.append(P.xT.evt)


def emit_attn_l0(P, dt_in):
    nc = P.nc
    with ExitStack() as les:
        xhat = Buf(les.enter_context(
            nc.sbuf_tensor(P.uid("a0_xhat"), [128, DC, NT], BF16)))
        aoT = Buf(les.enter_context(
            nc.sbuf_tensor(P.uid("a0_aoT"), [128, DC, NT_OWN], BF16)))
        cosT = les.enter_context(nc.sbuf_tensor(P.uid("a0_cos"), [128, NT], F32))
        sinT = les.enter_context(nc.sbuf_tensor(P.uid("a0_sin"), [128, NT], F32))
        pswb = les.enter_context(nc.sbuf_tensor(P.uid("a0_psw"), [128, 128], BF16))
        mask0 = les.enter_context(nc.sbuf_tensor(P.uid("a0_m0"), [128, 256],
                                                 F32))
        maskr = les.enter_context(nc.sbuf_tensor(P.uid("a0_mr"), [128, 256],
                                                 F32))
        wsl = [les.enter_context(
            nc.sbuf_tensor(P.uid("a0_w"), [128, DC, 128], BF16))
            for i in range(3)]
        woT = les.enter_context(nc.sbuf_tensor(P.uid("a0_wo"), [128, DC, D], BF16))
        bqkv = [les.enter_context(nc.sbuf_tensor(P.uid("a0_bias"),
                                                 [1, 3, 128], BF16))
                for i in range(2)]
        bo_sb = les.enter_context(nc.sbuf_tensor(P.uid("a0_bo"), [1, D],
                                                 BF16))
        qT = les.enter_context(nc.sbuf_tensor(P.uid("a0_qT"), [128, NT_OWN], BF16))
        kT = les.enter_context(nc.sbuf_tensor(P.uid("a0_kT"), [128, NT], BF16))
        v2 = les.enter_context(nc.sbuf_tensor(P.uid("a0_v2"), [128, 17, 128], BF16))
        qtmp = les.enter_context(nc.sbuf_tensor(P.uid("a0_qtmp"), [128, 512], BF16))
        rt1 = les.enter_context(nc.sbuf_tensor(P.uid("a0_rt1"), [128, 512], F32))
        rt2 = les.enter_context(nc.sbuf_tensor(P.uid("a0_rt2"), [128, 512], F32))
        NR = 4
        sm = [les.enter_context(nc.sbuf_tensor(P.uid("a0_sm"), [128, 256],
                                               F32)) for _ in range(NR)]
        exs = [les.enter_context(nc.sbuf_tensor(P.uid("a0_exs"), [128, 256],
                                                F32)) for _ in range(NR)]
        prb = [les.enter_context(nc.sbuf_tensor(P.uid("a0_prb"), [128, 256],
                                                BF16)) for _ in range(NR)]
        prT = [les.enter_context(nc.sbuf_tensor(P.uid("a0_prT"), [128, 256],
                                                BF16)) for _ in range(NR)]
        srow = [les.enter_context(nc.sbuf_tensor(P.uid("a0_srow"), [128, 4],
                                                 F32)) for _ in range(NR)]
        tbf = les.enter_context(nc.sbuf_tensor(P.uid("a0_tbf"), [128, 512], BF16))

        ev_cos = P.dma("sync", cosT[:, :], dt_in["ropecos"][:, :])
        ev_sin = P.dma("sync", sinT[:, :], dt_in["ropesin"][:, :])
        ev_psw = P.dma("gpsimd", pswb[:, :], dt_in["pswap"][:, :])
        ev_m0 = P.dma("sync", mask0[:, :], dt_in["mask0"][:, :])
        ev_mr = P.dma("sync", maskr[:, :], dt_in["maskr"][:, :])
        ev_wo = P.dma("gpsimd", woT[:, :, :],
                      dt_in["aWoT"].ap().rearrange("(fc p) o -> p fc o",
                                                   p=128))
        ev_bo = P.dma("gpsimd", bo_sb[:, :], dt_in["abo"][:, :])

        emit_ln(P, [t for tg in TILES_S1 for t in tg], xhat)

        wfree = None
        bias_free = [None, None]
        prev_soft = [None, None]
        for hp in range(8):
            evw = []
            for i, n in enumerate(("aWqT", "aWkT", "aWvT")):
                src = dt_in[n].ap().rearrange("(dc p) f -> p dc f", p=128)[
                    :, :, hp * 128:(hp + 1) * 128]
                evw.append(P.dma("gpsimd", wsl[i][:, :, :], src,
                                 waits=[wfree]))
            bb_ = bqkv[hp % 2]
            ev_b = [P.dma("gpsimd", row(bb_[:, i:i + 1, :]),
                          dt_in[n][0:1, hp * 128:(hp + 1) * 128],
                          waits=[bias_free[hp % 2]])
                    for i, n in enumerate(("abq", "abk", "abv"))]
            for which in (0, 1):
                tiles = [t for tg in (TILES_OWN if which == 0 else TILES_S1)
                         for t in tg]
                dstT = qT if which == 0 else kT
                off = HALO if which == 0 else 0
                scale = 0.125 if which == 0 else 1.0
                for (t0, tl) in tiles:
                    bank, gate, bi = P.pool.acquire()
                    for dc in range(DC):
                        P.op("tensor",
                             lambda e_, o=bank[:, 0:tl],
                             l=sq(wsl[which][:, dc, :]),
                             r=sq(xhat[:, dc, t0:t0 + tl]), st=(dc == 0):
                             e_.matmul(o, l, r, start=st, stop=False),
                             waits=[evw[which], xhat.evt, gate])
                    pe = P.op("tensor",
                              lambda e_, o=bank[:, 0:tl],
                              l=row(bb_[:, which:which + 1, :]),
                              r=P.ones_r[:, 0:tl]:
                              e_.matmul(o, l, r, start=False, stop=True),
                              waits=[ev_b[which], P.ev_ones], inc=True)
                    cb = P.op("vector",
                              lambda e_, o=qtmp[:, 0:tl], i=bank[:, 0:tl]:
                              e_.tensor_copy(o, i), waits=[pe], inc=True)
                    bank2, gate2, bi2 = P.pool.acquire()
                    pes = P.op("tensor",
                               lambda e_, o=bank2[:, 0:tl],
                               r=qtmp[:, 0:tl]:
                               e_.matmul(o, pswb[:, :], r, start=True,
                                         stop=True),
                               waits=[cb, ev_psw, gate2], inc=True)
                    P.op("vector",
                         lambda e_, o=rt1[:, 0:tl], i=bank[:, 0:tl],
                         c=cosT[:, t0:t0 + tl], sc=scale:
                         e_.scalar_tensor_tensor(o, i, sc, c, OP.mult,
                                                 OP.mult),
                         waits=[ev_cos])
                    P.op("vector",
                         lambda e_, o=rt2[:, 0:tl], i=bank2[:, 0:tl],
                         c=sinT[:, t0:t0 + tl], sc=scale:
                         e_.scalar_tensor_tensor(o, i, sc, c, OP.mult,
                                                 OP.mult),
                         waits=[pes, ev_sin])
                    ve2 = P.op("vector",
                               lambda e_, o=dstT[:, t0 - off:t0 - off + tl],
                               a=rt1[:, 0:tl], b=rt2[:, 0:tl]:
                               e_.tensor_tensor(o, a, b, OP.add),
                               inc=True)
                    P.pool.release(bi, ve2)
                    P.pool.release(bi2, ve2)
            ev_q = P.now("vector")
            for tc in range(17):
                bank, gate, bi = P.pool.acquire()
                for dc in range(DC):
                    P.op("tensor",
                         lambda e_, o=bank[:, 0:128],
                         l=sq(xhat[:, dc, tc * 128:(tc + 1) * 128]),
                         r=sq(wsl[2][:, dc, :]), st=(dc == 0):
                         e_.matmul(o, l, r, start=st, stop=False),
                         waits=[evw[2], xhat.evt, gate])
                pe = P.op("tensor",
                          lambda e_, o=bank[:, 0:128],
                          r=row(bb_[:, 2:3, :]):
                          e_.matmul(o, P.ones_r[0:1, 0:128], r, start=False,
                                    stop=True),
                          waits=[ev_b[2], P.ev_ones], inc=True)
                ve = P.op("vector",
                          lambda e_, o=sq(v2[:, tc, :]), i=bank[:, 0:128]:
                          e_.tensor_copy(o, i), waits=[pe], inc=True)
                P.pool.release(bi, ve)
            ev_v = P.now("vector")
            wfree = ev_v
            bias_free[hp % 2] = P.now("tensor")

            iters = [(hh, qb) for hh in range(2) for qb in range(16)]
            LAG = 3
            st_ev = {}
            st_bank = {}

            def stage_a(it):
                hh, qb = iters[it]
                h64 = hh * 64
                r_ = it % NR
                bank, gate, bi = P.pool.acquire()
                pes = P.op("tensor",
                           lambda e_, o=bank[:, 0:256],
                           l=qT[h64:h64 + 64, qb * 128:(qb + 1) * 128],
                           r=kT[h64:h64 + 64, qb * 128:qb * 128 + 256]:
                           e_.matmul(o, l, r, start=True, stop=True),
                           waits=[ev_q, gate], inc=True)
                mk = mask0 if qb == 0 else maskr
                sr = srow[r_]
                e1 = P.op("vector",
                          lambda e_, o=sm[r_], a=bank[:, 0:256], b=mk:
                          e_.tensor_tensor(o[:, :], a, b[:, :], OP.add),
                          waits=[pes, ev_m0, ev_mr], inc=True)
                P.pool.release(bi, e1)
                e2 = P.op("vector",
                          lambda e_, o=sr, i=sm[r_]:
                          e_.reduce_max(o[:, 0:1], i[:, :], axis=AX,
                                        negate=True),
                          inc=True)
                e3 = P.op("scalar",
                          lambda e_, o=exs[r_], i=sm[r_], s=sr:
                          e_.activation(o[:, :], i[:, :], AF.Exp,
                                        bias=s[:, 0:1],
                                        accum_out=s[:, 1:2]),
                          waits=[e1, e2], inc=True)
                P.op("scalar", lambda e_, s=sr: e_.activation(
                    s[:, 2:3], s[:, 1:2], AF.Ln))
                e4 = P.op("scalar", lambda e_, s=sr: e_.activation(
                    s[:, 2:3], s[:, 2:3], AF.Exp, scale=-1.0), inc=True)
                e5 = P.op("gpsimd",
                          lambda e_, o=prb[r_], i=exs[r_], s=sr:
                          e_.tensor_scalar(o[:, :], i[:, :], s[:, 2:3],
                                           None, OP.mult),
                          waits=[e3, e4], inc=True)
                st_ev[it] = e5

            def stage_b(it):
                hh, qb = iters[it]
                h64 = hh * 64
                r_ = it % NR
                e5 = st_ev[it]
                for bb in range(2):
                    emit_transpose(P, prT[r_][:, bb * 128:(bb + 1) * 128],
                                   prb[r_][:, bb * 128:(bb + 1) * 128],
                                   waits=[e5])
                eT = P.now("vector")
                bank2, gate2, bi2 = P.pool.acquire()
                pe2 = None
                for bb in range(2):
                    pe2 = P.op("tensor",
                               lambda e_, o=bank2[0:64, 0:128],
                               l=sq(v2[:, qb + bb, h64:h64 + 64]),
                               r=prT[r_][:, bb * 128:(bb + 1) * 128],
                               st=(bb == 0):
                               e_.matmul(o, l, r, start=st, stop=(bb == 1)),
                               waits=[ev_v, eT, gate2], inc=(bb == 1))
                ve = P.op("vector",
                          lambda e_,
                          o=aoT.t[h64:h64 + 64, hp,
                                  qb * 128:(qb + 1) * 128].squeeze(),
                          i=bank2[0:64, 0:128]:
                          e_.tensor_copy(o, i), waits=[pe2], inc=True)
                P.pool.release(bi2, ve)

            for it in range(len(iters) + LAG):
                if it >= LAG:
                    stage_b(it - LAG)
                if it < len(iters):
                    stage_a(it)
        aoT.evt = P.now("vector")

        for (t0, tl) in [t for tg in TILES_OWN for t in tg]:
            for oc in range(DC):
                bank, gate, bi = P.pool.acquire()
                for fc in range(DC):
                    P.op("tensor",
                         lambda e_, o=bank[:, 0:tl],
                         l=sq(woT[:, fc, oc * 128:(oc + 1) * 128]),
                         r=sq(aoT[:, fc, t0 - HALO:t0 - HALO + tl]),
                         st=(fc == 0):
                         e_.matmul(o, l, r, start=st, stop=False),
                         waits=[ev_wo, aoT.evt, gate])
                pe = P.op("tensor",
                          lambda e_, o=bank[:, 0:tl],
                          l=bo_sb[0:1, oc * 128:(oc + 1) * 128],
                          r=P.ones_r[:, 0:tl]:
                          e_.matmul(o, l, r, start=False, stop=True),
                          waits=[ev_bo, P.ev_ones], inc=True)
                P.op("vector",
                     lambda e_, o=tbf[:, 0:tl], i=bank[:, 0:tl]:
                     e_.tensor_copy(o, i), waits=[pe])
                re = P.op("vector",
                          lambda e_, o=sq(P.xT[:, oc, t0:t0 + tl]),
                          i=tbf[:, 0:tl]:
                          e_.tensor_tensor(o, o, i, OP.add),
                          waits=[P.xT.evt], inc=True)
                P.pool.release(bi, re)
        P.xT.evt = P.now("vector")


def emit_elu(P, bank, tl, t1, t2, out_ap, pe):
    np_ = bank.shape[0]
    mn = P.op("vector", lambda e_, o=t1[0:np_, 0:tl], i=bank[0:np_, 0:tl]:
              e_.tensor_scalar(o, i, 0.0, None, OP.min),
              waits=[pe, getattr(P, "_elu_last", None)], inc=True)
    a1 = P.op("scalar", lambda e_, o=t1[0:np_, 0:tl]:
              e_.activation(o, o, AF.Exp), waits=[mn], inc=True)
    P.op("vector", lambda e_, o=t2[0:np_, 0:tl], i=bank[0:np_, 0:tl]:
         e_.tensor_scalar(o, i, 0.0, None, OP.max), waits=[pe])
    ve = P.op("vector", lambda e_, o=out_ap, a=t1[0:np_, 0:tl],
              b=t2[0:np_, 0:tl]:
              e_.tensor_tensor(o, a, b, OP.add), waits=[a1], inc=True)
    P._elu_last = a1
    return ve


def emit_attn_l1(P, dt_in):
    nc = P.nc
    with ExitStack() as les:
        xhat = Buf(les.enter_context(
            nc.sbuf_tensor(P.uid("a1_xhat"), [128, DC, NT_OWN], BF16)))
        wk = les.enter_context(nc.sbuf_tensor(P.uid("a1_wk"), [128, DC, D], BF16))
        wv = les.enter_context(nc.sbuf_tensor(P.uid("a1_wv"), [128, DC, D], BF16))
        wo = les.enter_context(nc.sbuf_tensor(P.uid("a1_wo"), [128, DC, D], BF16))
        biasr = les.enter_context(nc.sbuf_tensor(P.uid("a1_bias"), [1, 4, D], BF16))
        evk = P.dma("gpsimd", wk[:, :, :],
                    dt_in["lWk"].ap().rearrange("(dc p) f -> p dc f", p=128))
        evv = P.dma("gpsimd", wv[:, :, :],
                    dt_in["lWv"].ap().rearrange("(dc p) f -> p dc f", p=128))
        evo = P.dma("gpsimd", wo[:, :, :],
                    dt_in["lWo"].ap().rearrange("(fc p) o -> p fc o", p=128))
        ev_b = [P.dma("gpsimd", row(biasr[:, i:i + 1, :]), dt_in[n][:, :])
                for i, n in enumerate(("lbq", "lbk", "lbv", "lbo"))]

        emit_ln(P, [t for tg in TILES_OWN for t in tg], xhat, toff=HALO)

        kfc = les.enter_context(nc.sbuf_tensor(P.uid("a1_kfc"), [128, D], BF16))
        vc = les.enter_context(nc.sbuf_tensor(P.uid("a1_vc"), [128, D], BF16))
        t1 = les.enter_context(nc.sbuf_tensor(P.uid("a1_t1"), [128, 512], F32))
        t2 = les.enter_context(nc.sbuf_tensor(P.uid("a1_t2"), [128, 512], F32))
        kvs = les.enter_context(nc.sbuf_tensor(P.uid("a1_kvs"), [64, 1040], F32))
        kvz = les.enter_context(nc.sbuf_tensor(P.uid("a1_kvz"), [64, 1040], F32))
        kvb = les.enter_context(nc.sbuf_tensor(P.uid("a1_kvb"), [64, 1040], BF16))
        smk = les.enter_context(nc.sbuf_tensor(P.uid("a1_smk"), [128, 2], F32))
        dens = [les.enter_context(nc.sbuf_tensor(P.uid("a1_dens"),
                                                  [128, 16], F32))
                for _ in range(2)]
        tbf = les.enter_context(nc.sbuf_tensor(P.uid("a1_tbf"), [128, 512], BF16))
        ev_sm = P.dma("sync", smk[:, :], dt_in["seqmask"][:, :])

        wq = les.enter_context(
            nc.sbuf_tensor(P.uid("a1_wq"), [128, DC, D], BF16))
        evq = P.dma("gpsimd", wq[:, :, :],
                    dt_in["lWq"].ap().rearrange("(dc p) f -> p dc f", p=128))
        qfT = [les.enter_context(
            nc.sbuf_tensor(P.uid("a1_qfT"), [64, NH, 512], BF16))
            for _ in range(2)]
        yz = les.enter_context(nc.sbuf_tensor(P.uid("a1_yz"),
                                              [128, 4, D], BF16))
        yzT = les.enter_context(nc.sbuf_tensor(P.uid("a1_yzT"),
                                               [128, DC, 512], BF16))

        kvbankA, gA, iA = P.pool.acquire()
        kvbankB, gB_, iB = P.pool.acquire()
        kvbankC, gC_, iC = P.pool.acquire()
        started = [False, False, False]
        pe_kv = None
        units = [(tc, half) for tc in range(16) for half in range(2)]
        u_kf = {}
        u_vc = {}

        def stage_kf(u):
            tc, half = units[u]
            trow = tc * 128
            bank, gate, bi = P.pool.acquire()
            peh = None
            for dc in range(DC):
                peh = P.op("tensor",
                           lambda e_, o=bank[:, 0:512],
                           l=sq(xhat[:, dc, trow:trow + 128]),
                           r=sq(wk[:, dc, half * 512:half * 512 + 512]),
                           st=(dc == 0):
                           e_.matmul(o, l, r, start=st, stop=False),
                           waits=[evk, xhat.evt, gate])
            peh = P.op("tensor",
                       lambda e_, o=bank[:, 0:512],
                       r=row(biasr[0:1, 1:2, half * 512:half * 512 + 512]):
                       e_.matmul(o, P.ones_r[0:1, 0:128], r, start=False,
                                 stop=True),
                       waits=[ev_b[1], P.ev_ones], inc=True)
            ve = emit_elu(P, bank, 512, t1, t2,
                          kfc[:, half * 512:half * 512 + 512], peh)
            P.pool.release(bi, ve)
            u_kf[u] = ve

        def stage_v(u):
            tc, half = units[u]
            trow = tc * 128
            bank, gate, bi = P.pool.acquire()
            peh = None
            for dc in range(DC):
                peh = P.op("tensor",
                           lambda e_, o=bank[:, 0:512],
                           l=sq(xhat[:, dc, trow:trow + 128]),
                           r=sq(wv[:, dc, half * 512:half * 512 + 512]),
                           st=(dc == 0):
                           e_.matmul(o, l, r, start=st, stop=False),
                           waits=[evv, xhat.evt, gate])
            peh = P.op("tensor",
                       lambda e_, o=bank[:, 0:512],
                       r=row(biasr[0:1, 2:3, half * 512:half * 512 + 512]):
                       e_.matmul(o, P.ones_r[0:1, 0:128], r, start=False,
                                 stop=True),
                       waits=[ev_b[2], P.ev_ones], inc=True)
            ve = P.op("vector",
                      lambda e_, o=vc[:, half * 512:half * 512 + 512],
                      i=bank[:, 0:512]: e_.tensor_copy(o, i),
                      waits=[peh], inc=True)
            P.pool.release(bi, ve)
            u_vc[u] = ve

        def stage_kv(u):
            nonlocal pe_kv
            tc, half = units[u]
            for hh in range(8):
                h = half * 8 + hh
                bk = kvbankA if half == 0 else kvbankB
                bix = half
                st = not started[bix]
                started[bix] = True
                P.op("tensor",
                     lambda e_, o=bk[0:64, hh * 64:(hh + 1) * 64],
                     l=kfc[:, h * 64:(h + 1) * 64],
                     r=vc[:, h * 64:(h + 1) * 64], s=st:
                     e_.matmul(o, l, r, start=s, stop=False,
                               skip_group_check=True),
                     waits=[u_kf[u], u_vc[u],
                            (gA if half == 0 else gB_) if st else None])
                stC = not started[2]
                started[2] = True
                pe_kv = P.op("tensor",
                             lambda e_, o=kvbankC[0:64, h:h + 1],
                             l=kfc[:, h * 64:(h + 1) * 64], s=stC:
                             e_.matmul(o, l, P.ones_c[:, 0:1], start=s,
                                       stop=(u == len(units) - 1
                                             and hh == 7),
                                       skip_group_check=True),
                             waits=[P.ev_ones, gC_ if stC else None],
                             inc=(u == len(units) - 1 and hh == 7))

        stage_kf(0)
        stage_v(0)
        for u in range(len(units)):
            if u + 1 < len(units):
                stage_kf(u + 1)
            stage_kv(u)
            if u + 1 < len(units):
                stage_v(u + 1)
        kvs3 = kvs[:, :].rearrange("p (h c) -> p h c", c=65)
        ev1 = P.op("vector", lambda e_,
                   o=kvs3[:, 0:8, 0:64],
                   i=kvbankA[0:64, 0:512].rearrange("p (h c) -> p h c", c=64):
                   e_.tensor_copy(o, i), waits=[pe_kv], inc=True)
        ev2 = P.op("vector", lambda e_,
                   o=kvs3[:, 8:16, 0:64],
                   i=kvbankB[0:64, 0:512].rearrange("p (h c) -> p h c", c=64):
                   e_.tensor_copy(o, i), waits=[pe_kv], inc=True)
        ev3 = P.op("vector", lambda e_,
                   o=kvs3[:, 0:16, 64:65],
                   i=kvbankC[0:64, 0:16].rearrange("p (h c) -> p h c", c=1):
                   e_.tensor_copy(o, i), waits=[pe_kv], inc=True)
        if DEBUG:
            dkv = nc.dram_tensor("dbg_kvs", [64, 1040], F32,
                                 kind="ExternalOutput")
            P.dma("sync", dkv[:, :], kvs[:, :], waits=[ev1, ev2, ev3])
        P.pool.release(iA, ev1)
        P.pool.release(iB, ev2)
        P.pool.release(iC, ev3)
        z1 = P.op("vector", lambda e_: e_.tensor_scalar(
            kvz[:, :], kvs[:, :], smk[0:64, 0:1], None, OP.mult),
            waits=[ev1, ev2, ev3, ev_sm], inc=True)
        d1 = P.dma("sync", P.cc_in[0:64, :], kvz[:, :], waits=[z1])
        z2 = P.op("vector", lambda e_: e_.tensor_scalar(
            kvz[:, :], kvs[:, :], smk[0:64, 1:2], None, OP.mult),
            waits=[d1], inc=True)
        d2 = P.dma("sync", P.cc_in[64:128, :], kvz[:, :], waits=[z2])
        ccs = P.sems["cc"]
        ccs.n += 1
        P.q["gpsimd"].append((
            ((P.sems["dsy"], P.sems["dsy"].n),),
            lambda e: e.collective_compute(
                "AllReduce", OP.add,
                replica_groups=[list(range(NCORES))],
                ins=[P.cc_in.ap().opt()],
                outs=[P.cc_out.ap().opt()]),
            ((ccs, 1),), None))
        ev_cc = (ccs, ccs.n)
        l1 = P.dma("sync", kvz[:, :], P.cc_out[0:64, :], waits=[ev_cc])
        m1 = P.op("vector", lambda e_: e_.tensor_scalar(
            kvs[:, :], kvz[:, :], smk[0:64, 0:1], None, OP.mult),
            waits=[l1], inc=True)
        l2 = P.dma("sync", kvz[:, :], P.cc_out[64:128, :], waits=[m1])
        P.op("vector", lambda e_: e_.tensor_scalar(
            kvz[:, :], kvz[:, :], smk[0:64, 1:2], None, OP.mult), waits=[l2])
        P.op("vector", lambda e_: e_.tensor_tensor(
            kvs[:, :], kvs[:, :], kvz[:, :], OP.add))
        ev_kvb = P.op("vector", lambda e_: e_.tensor_copy(
            kvb[:, :], kvs[:, :]), inc=True)
        if DEBUG:
            dkb = nc.dram_tensor("dbg_kvb", [64, 1040], BF16,
                                 kind="ExternalOutput")
            P.dma("sync", dkb[:, :], kvb[:, :], waits=[ev_kvb])

        all_tiles = [t for tg in TILES_OWN for t in tg]
        qev = [None, None]

        def emit_qproj(ti):
            t0, tl = all_tiles[ti]
            sl = ti % 2
            for h in range(NH):
                bank, gate, bi = P.pool.acquire()
                for dc in range(DC):
                    P.op("tensor",
                         lambda e_, o=bank[0:64, 0:tl],
                         l=sq(wq[:, dc, h * 64:(h + 1) * 64]),
                         r=sq(xhat[:, dc, t0 - HALO:t0 - HALO + tl]),
                         st=(dc == 0):
                         e_.matmul(o, l, r, start=st, stop=False),
                         waits=[evq, xhat.evt, gate])
                pe = P.op("tensor",
                          lambda e_, o=bank[0:64, 0:tl],
                          l=row(biasr[0:1, 0:1, h * 64:(h + 1) * 64]),
                          r=P.ones_r[:, 0:tl]:
                          e_.matmul(o, l, r, start=False, stop=True),
                          waits=[ev_b[0], P.ev_ones], inc=True)
                ve = emit_elu(P, bank[0:64, :], tl, t1, t2,
                              sq(qfT[sl][0:64, h, 0:tl]), pe)
                P.pool.release(bi, ve)
            qev[sl] = P.now("vector")

        prev_z = None
        emit_qproj(0)
        emit_qproj(1)
        for ti, (t0, tl) in enumerate(all_tiles):
            sl = ti % 2
            ev_qf = qev[sl]
            for k in range(tl // 128):
                par = k % 2
                dn = dens[par]
                bY = [P.pool.acquire() for _ in range(2)]
                bD = P.pool.acquire()
                peY = [None, None]
                peD = None
                for h in range(NH):
                    bk, gk, _ = bY[h // 8]
                    cb = (h % 8) * 64
                    st = (h % 8 == 0)
                    peY[h // 8] = P.op(
                        "tensor",
                        lambda e_, o=bk[:, cb:cb + 64],
                        l=sq(qfT[sl][0:64, h, k * 128:(k + 1) * 128]),
                        r=kvb[0:64, h * 65:h * 65 + 64], s=st:
                        e_.matmul(o, l, r, start=s, stop=False,
                                  skip_group_check=True),
                        waits=[ev_qf, ev_kvb, gk if st else None],
                        inc=(h % 8 == 7))
                    peD = P.op(
                        "tensor",
                        lambda e_, o=bD[0][:, h:h + 1],
                        l=sq(qfT[sl][0:64, h, k * 128:(k + 1) * 128]),
                        r=kvb[0:64, h * 65 + 64:h * 65 + 65], s=(h == 0):
                        e_.matmul(o, l, r, start=s, stop=(h == NH - 1),
                                  skip_group_check=True),
                        waits=[bD[1] if h == 0 else None],
                        inc=(h == NH - 1))
                zc = P.op("vector", lambda e_, o=dn, i=bD[0][:, 0:16]:
                          e_.tensor_scalar(o[:, :], i, EPS_LA, None, OP.add),
                          waits=[peD], inc=True)
                P.pool.release(bD[2], zc)
                P.op("scalar", lambda e_, o=dn: e_.activation(
                    o[:, :], o[:, :], AF.Ln), waits=[zc])
                za = P.op("scalar", lambda e_, o=dn: e_.activation(
                    o[:, :], o[:, :], AF.Exp, scale=-1.0), inc=True)
                last = [None, None]
                for h in range(NH):
                    bk, gk, bik = bY[h // 8]
                    cb = (h % 8) * 64
                    last[h // 8] = P.op(
                        "vector",
                        lambda e_, o=sq(yz[:, k, h * 64:(h + 1) * 64]),
                        i=bk[:, cb:cb + 64], s=dn[:, h:h + 1]:
                        e_.tensor_scalar(o, i, s, None, OP.mult),
                        waits=[za, peY[h // 8]], inc=True)
                for j in range(2):
                    P.pool.release(bY[j][2], last[j])
            if ti + 2 < len(all_tiles):
                emit_qproj(ti + 2)
            for k in range(tl // 128):
                for fc in range(DC):
                    emit_transpose(P, sq(yzT[:, fc, k * 128:(k + 1) * 128]),
                                   sq(yz[:, k, fc * 128:(fc + 1) * 128]))
            ev_yzT = P.now("vector")
            for oc in range(DC):
                bank, gate, bi = P.pool.acquire()
                for fc in range(DC):
                    P.op("tensor",
                         lambda e_, o=bank[:, 0:tl],
                         l=sq(wo[:, fc, oc * 128:(oc + 1) * 128]),
                         r=sq(yzT[:, fc, 0:tl]), st=(fc == 0):
                         e_.matmul(o, l, r, start=st, stop=False),
                         waits=[evo, ev_yzT, gate])
                pe = P.op("tensor",
                          lambda e_, o=bank[:, 0:tl],
                          l=row(biasr[0:1, 3:4, oc * 128:(oc + 1) * 128]),
                          r=P.ones_r[:, 0:tl]:
                          e_.matmul(o, l, r, start=False, stop=True),
                          waits=[ev_b[3], P.ev_ones], inc=True)
                P.op("vector",
                     lambda e_, o=tbf[:, 0:tl], i=bank[:, 0:tl]:
                     e_.tensor_copy(o, i), waits=[pe])
                re = P.op("vector",
                          lambda e_, o=sq(P.xT[:, oc, t0:t0 + tl]),
                          i=tbf[:, 0:tl]:
                          e_.tensor_tensor(o, o, i, OP.add),
                          waits=[P.xT.evt], inc=True)
                P.pool.release(bi, re)
        P.xT.evt = P.now("vector")


_LAYCACHE = {}


def _lay1(w):
    key = id(w)
    if key in _LAYCACHE:
        return _LAYCACHE[key]
    w = np.asarray(w)
    r = np.ascontiguousarray(
        w.reshape(NE, DC, 128, 4, 512).transpose(0, 3, 2, 1, 4)
    ).reshape(NE, 4, 128, 4096)
    _LAYCACHE[key] = r
    return r


def _lay2(w):
    key = id(w)
    if key in _LAYCACHE:
        return _LAYCACHE[key]
    w = np.asarray(w)
    r = np.ascontiguousarray(
        w.reshape(NE, HFC, 128, 4, 256).transpose(0, 3, 2, 1, 4)
    ).reshape(NE, 4, 128, 4096)
    _LAYCACHE[key] = r
    return r


def _swap_mat():
    p = np.zeros((128, 128), dtype=np.float32)
    for i in range(128):
        d = i % 64
        j = (i - d) + (d + 32) % 64
        p[j, i] = 1.0
    return p


def _rope_tables(core):
    cs = core % 4
    rows = np.arange(NT)
    pos = np.maximum(cs * NT_OWN - HALO + rows, 0).astype(np.float64)
    pp = np.arange(128)
    j = (pp % 64) % 32
    inv = 10000.0 ** (-j / 32.0)
    ang = pos[None, :] * inv[:, None]
    cos = np.cos(ang).astype(np.float32)
    sin = np.sin(ang).astype(np.float32)
    sgn = np.where((pp % 64) < 32, -1.0, 1.0).astype(np.float32)
    return cos, (sin * sgn[:, None]).astype(np.float32)


def _masks(core):
    ii = np.arange(128)[:, None]
    jj = np.arange(256)[None, :]
    prev_ok = (jj < 128) & (jj >= ii)
    cur_ok = (jj >= 128) & ((jj - 128) <= ii)
    maskr = np.where(prev_ok | cur_ok, 0.0, -1e30).astype(np.float32)
    if core % 4 == 0:
        mask0 = np.where(cur_ok, 0.0, -1e30).astype(np.float32)
    else:
        mask0 = maskr
    return mask0, maskr


def _host_core_inputs(core, need, x_rows, inputs):
    m = {"xin": x_rows, "ident": np.eye(128, dtype=np.float32)}
    ffn0 = bool(need & {"s1", "s3"})
    ffn1 = bool(need & {"s4", "s6"})
    if ffn0 or ffn1:
        sel = np.zeros((8, 8, 128), dtype=np.float32)
        for e in range(8):
            sel[e, e, :] = 1.0
        m["esel"] = sel.reshape(8, 8 * 128)
    if ffn0:
        m["gWg"] = np.ascontiguousarray(inputs["L0_ffn_Wg"].T)
        m["gW1"] = _lay1(inputs["L0_ffn_W1"])
        m["gb1"] = np.asarray(inputs["L0_ffn_b1"]).reshape(1, -1)
        m["gW2"] = _lay2(inputs["L0_ffn_W2"])
        m["gb2"] = inputs["L0_ffn_b2"]
    if ffn1:
        m["hWg"] = np.ascontiguousarray(inputs["L1_ffn_router_w"].T)
        m["hW1"] = _lay1(inputs["L1_ffn_w1"])
        m["hW3"] = _lay1(inputs["L1_ffn_w3"])
        m["hW2"] = _lay2(inputs["L1_ffn_w2"])
    if "s2" in need:
        m["pswap"] = _swap_mat()
        for k, n in (("aWqT", "L0_attn_Wq"), ("aWkT", "L0_attn_Wk"),
                     ("aWvT", "L0_attn_Wv"), ("aWoT", "L0_attn_Wo")):
            m[k] = np.ascontiguousarray(inputs[n].T)
        for k, n in (("abq", "L0_attn_bq"), ("abk", "L0_attn_bk"),
                     ("abv", "L0_attn_bv"), ("abo", "L0_attn_bo")):
            m[k] = np.ascontiguousarray(inputs[n]).reshape(1, -1)
        m["mask0"], m["maskr"] = _masks(core)
        m["ropecos"], m["ropesin"] = _rope_tables(core)
    if "s5" in need:
        for k, n in (("lWq", "L1_attn_Wq"), ("lWk", "L1_attn_Wk"),
                     ("lWv", "L1_attn_Wv")):
            m[k] = np.ascontiguousarray(np.asarray(inputs[n]).reshape(D, D))
        m["lWo"] = np.ascontiguousarray(
            np.asarray(inputs["L1_attn_Wo"]).reshape(D, D))
        for k, n in (("lbq", "L1_attn_bq"), ("lbk", "L1_attn_bk"),
                     ("lbv", "L1_attn_bv"), ("lbo", "L1_attn_bo")):
            m[k] = np.ascontiguousarray(np.asarray(inputs[n])).reshape(1, -1)
        sm = np.zeros((128, 2), dtype=np.float32)
        sm[:, 0 if core < 4 else 1] = 1.0
        m["seqmask"] = sm
    return m


def _shard_rows(xfull, dtype):
    shards = []
    for c in range(NCORES):
        lo = c * NT_OWN - HALO
        if c % 4 == 0:
            pad = np.zeros((HALO, D), dtype=dtype)
            rows = np.concatenate(
                [pad, xfull[c * NT_OWN:(c + 1) * NT_OWN].astype(dtype,
                                                                copy=False)],
                axis=0)
        else:
            rows = xfull[lo:(c + 1) * NT_OWN].astype(dtype, copy=False)
        shards.append(np.ascontiguousarray(rows))
    return shards


def _run(phases, xfull, inputs, x_in_f32):
    need = set(phases)
    nc = build_program(phases, x_in_f32)
    dt = np.float32 if x_in_f32 else BF
    shards = _shard_rows(xfull, dt)
    in_maps = [_host_core_inputs(c, need, shards[c], inputs)
               for c in range(NCORES)]
    res = run_bass_kernel_spmd(nc, in_maps, core_ids=list(range(NCORES)))
    outs = [np.asarray(r["xout"]) for r in res.results]
    return np.concatenate(outs, axis=0)


def kernel(**inputs):
    x = np.asarray(inputs["L0_attn_x"], dtype=np.float32)
    xfull = np.ascontiguousarray(x.reshape(NT_OWN * NCORES, D))
    if FUSED:
        out = _run(ALL_PHASES, xfull, inputs, x_in_f32=True)
    else:
        cur = xfull
        f32 = True
        for ph in ALL_PHASES:
            out = _run((ph,), cur, inputs, x_in_f32=f32)
            cur = out
            f32 = False
    return np.ascontiguousarray(
        out.reshape(2, 8192, D).astype(np.float32, copy=False))


# version 07
# speedup vs baseline: 1.0242x; 1.0056x over previous
import numpy as np
import ml_dtypes
from contextlib import ExitStack

import concourse.bass as bass
import concourse.mybir as mybir
from concourse.bass_utils import run_bass_kernel_spmd

F32 = mybir.dt.float32
BF16 = mybir.dt.bfloat16
AX = mybir.AxisListType.X
OP = mybir.AluOpType
AF = mybir.ActivationFunctionType
BF = ml_dtypes.bfloat16

NCORES = 8
D = 1024
DC = 8
NT_OWN = 2048
HALO = 128
NT = NT_OWN + HALO
NH, DHD = 16, 64
NE, HF = 8, 2048
HFC = HF // 128
EPS_LN = 1e-5
EPS_LA = 1e-6
NEG = -1e9

FUSED = True
DEBUG = False

TILES_S1 = [[(0, 128), (128, 512), (640, 512)], [(1152, 512), (1664, 512)]]
TILES_OWN = [[(128, 512), (640, 512)], [(1152, 512), (1664, 512)]]
ALL_PHASES = ("s1", "s2", "s3", "s4", "s5", "s6")

ENGINES = ("sync", "tensor", "vector", "scalar", "gpsimd")


class Sem:
    def __init__(self, h):
        self.h = h
        self.n = 0


class Prog:
    def __init__(self, nc, es):
        self.nc = nc
        self.es = es
        self.q = {e: [] for e in ENGINES}
        self.sems = {}
        for name in ("pe", "ve", "ac", "po", "dpo", "dsy", "cc"):
            self.sems[name] = Sem(es.enter_context(nc.semaphore(name)))
        self.eng_sem = {"tensor": "pe", "vector": "ve", "scalar": "ac",
                        "gpsimd": "po"}

    def op(self, eng, fn, waits=(), inc=False):
        own = self.sems.get(self.eng_sem.get(eng))
        wl = []
        for w in waits:
            if w is None:
                continue
            s, v = w
            if s is own:
                continue
            assert v <= s.n, "wait on future event"
            wl.append((s, v))
        evt = None
        incs = ()
        selfwait = None
        if eng in ("vector", "scalar", "gpsimd"):
            selfwait = own.n
            own.n += 1
            incs = ((own, 1),)
            evt = (own, own.n)
        elif inc:
            own.n += 1
            incs = ((own, 1),)
            evt = (own, own.n)
        self.q[eng].append((tuple(wl), fn, incs, selfwait))
        return evt

    def now(self, eng):
        s = self.sems[self.eng_sem[eng]]
        return (s, s.n)

    def uid(self, base):
        self._uid = getattr(self, "_uid", 0) + 1
        return f"{base}{self._uid}"

    def end_phase(self):
        self.mem_gate = tuple(self.now(e) for e in
                              ("tensor", "vector", "scalar", "gpsimd"))

    def dma(self, eng, out_ap, in_ap, waits=()):
        s = self.sems["dpo" if eng == "gpsimd" else "dsy"]
        wl = []
        for w in tuple(waits) + tuple(getattr(self, "mem_gate", ())):
            if w is None:
                continue
            ss, v = w
            assert v <= ss.n
            wl.append((ss, v))
        s.n += 16
        self.q[eng].append(
            (tuple(wl), lambda e, o=out_ap, i=in_ap: e.dma_start(o, i),
             ((s, 16),), None))
        return (s, s.n)

    def replay(self, name, eng):
        seen = {}
        own = self.sems.get(self.eng_sem.get(name))
        for waits, fn, incs, selfwait in self.q[name]:
            if selfwait is not None and selfwait > 0:
                eng.wait_ge(own.h, selfwait)
            for (s, v) in waits:
                if seen.get(id(s), -1) >= v:
                    continue
                eng.wait_ge(s.h, v)
                seen[id(s)] = v
            inst = fn(eng)
            for (s, n) in incs:
                inst.then_inc(s.h, n)


class Pool:

    def __init__(self, P, n=6):
        self.P = P
        self.banks = [P.es.enter_context(
            P.nc.psum_tensor(f"pool{i}", [128, 512], F32)) for i in range(n)]
        self.gate = [None] * n
        self.used = [False] * n
        self.i = 0

    def acquire(self):
        for _ in range(len(self.banks)):
            i = self.i
            self.i = (self.i + 1) % len(self.banks)
            if not self.used[i]:
                self.used[i] = True
                return self.banks[i], self.gate[i], i
        raise RuntimeError("psum pool exhausted")

    def release(self, idx, evt):
        self.used[idx] = False
        self.gate[idx] = evt


class Buf:
    def __init__(self, t):
        self.t = t
        self.evt = None

    def __getitem__(self, idx):
        return self.t[idx]


def sq(ap):
    return ap.squeeze()


def row(ap):
    return ap.rearrange("p a b -> p (a b)")


def build_program(phases, x_in_f32):
    nc = bass.Bass("TRN2", target_bir_lowering=False)
    es = ExitStack()
    P = Prog(nc, es)
    P.pool = Pool(P, 6)
    P.tp = [es.enter_context(nc.psum_tensor(f"tp{i}", [128, 1024], BF16))
            for i in range(2)]
    P.tp_gate = [None, None]
    P.tp_i = 0

    need = set(phases)
    ffn0 = bool(need & {"s1", "s3"})
    ffn1 = bool(need & {"s4", "s6"})

    dt_in = {}

    def din(name, shape, dtype=F32):
        dt_in[name] = nc.dram_tensor(name, list(shape), dtype,
                                     kind="ExternalInput")
        return dt_in[name]

    xin = din("xin", [NT, D], F32 if x_in_f32 else BF16)
    store_f32 = "s6" in need
    xout = nc.dram_tensor("xout", [NT_OWN, D],
                          F32 if store_f32 else BF16, kind="ExternalOutput")
    din("ident", [128, 128])
    if ffn0 or ffn1:
        din("esel", [8, 8 * 128])
    if ffn0:
        din("gWg", [8, D])
        din("gW1", [NE, 4, 128, 4096])
        din("gb1", [1, NE * HF])
        din("gW2", [NE, 4, 128, 4096])
        din("gb2", [NE, D])
    if ffn1:
        din("hWg", [8, D])
        din("hW1", [NE, 4, 128, 4096])
        din("hW3", [NE, 4, 128, 4096])
        din("hW2", [NE, 4, 128, 4096])
    if "s2" in need:
        din("pswap", [128, 128])
        din("aWqT", [D, D]); din("aWkT", [D, D])
        din("aWvT", [D, D]); din("aWoT", [D, D])
        din("abq", [1, D]); din("abk", [1, D])
        din("abv", [1, D]); din("abo", [1, D])
        din("mask0", [128, 256]); din("maskr", [128, 256])
        din("ropecos", [128, NT]); din("ropesin", [128, NT])
    if "s5" in need:
        din("lWq", [D, D]); din("lWk", [D, D]); din("lWv", [D, D])
        din("lWo", [D, D])
        din("lbq", [1, D]); din("lbk", [1, D]); din("lbv", [1, D])
        din("lbo", [1, D])
        din("seqmask", [128, 2])
        P.cc_in = nc.dram_tensor("cc_in", [128, 1040], F32)
        P.cc_out = nc.dram_tensor("cc_out", [128, 1040], F32)

    xT = Buf(es.enter_context(nc.sbuf_tensor("xT", [128, DC, NT], BF16)))
    identb = es.enter_context(nc.sbuf_tensor("identb", [128, 128], BF16))
    ones_r = es.enter_context(nc.sbuf_tensor("ones_r", [1, 512], BF16))
    ones_c = es.enter_context(nc.sbuf_tensor("ones_c", [128, 1], BF16))
    ones_f = es.enter_context(nc.sbuf_tensor("ones_f", [1, 128], F32))
    P.xT, P.identb = xT, identb
    P.ones_r, P.ones_c, P.ones_f = ones_r, ones_c, ones_f

    P.ld_id = P.dma("gpsimd", identb[:, :], dt_in["ident"][:, :])
    P.op("gpsimd", lambda e: e.memset(ones_r[:, :], 1.0))
    P.op("gpsimd", lambda e: e.memset(ones_c[:, :], 1.0))
    P.ev_ones = P.op("gpsimd", lambda e: e.memset(ones_f[:, :], 1.0),
                     inc=True)

    emit_load_x(P, xin, cast=x_in_f32)

    for ph in ALL_PHASES:
        if ph not in need:
            continue
        if ph == "s1":
            tiles = TILES_S1 if len(need) > 1 else TILES_OWN
            emit_ffn(P, dt_in, layer=0, halves=tiles)
        elif ph == "s3":
            emit_ffn(P, dt_in, layer=0, halves=TILES_OWN)
        elif ph in ("s4", "s6"):
            emit_ffn(P, dt_in, layer=1, halves=TILES_OWN)
        elif ph == "s2":
            emit_attn_l0(P, dt_in)
        elif ph == "s5":
            emit_attn_l1(P, dt_in)
        P.end_phase()

    emit_store_x(P, xout, as_f32=store_f32)

    fin = [(P.sems["dsy"], P.sems["dsy"].n), (P.sems["dpo"], P.sems["dpo"].n)]
    P.q["gpsimd"].append((tuple(fin), lambda e: e.engine_nop(), (), None))

    with nc.Block() as block:
        @block.sync
        def _(eng):
            P.replay("sync", eng)

        @block.tensor
        def _(eng):
            P.replay("tensor", eng)

        @block.vector
        def _(eng):
            P.replay("vector", eng)

        @block.scalar
        def _(eng):
            P.replay("scalar", eng)

        @block.gpsimd
        def _(eng):
            P.replay("gpsimd", eng)

    es.close()
    return nc


def emit_transpose(P, dst_ap, src_ap, waits=()):
    i = P.tp_i
    P.tp_i ^= 1
    t = P.tp[i]
    npart = src_ap.shape[0]
    nf = src_ap.shape[-1]
    pe = P.op("tensor",
              lambda e, o=t, s=src_ap, np_=npart, nf_=nf:
              e.transpose(o[0:nf_, 0:np_], s, P.identb[0:np_, 0:np_]),
              waits=list(waits) + [P.tp_gate[i], P.ld_id], inc=True)
    ve = P.op("vector",
              lambda e, o=dst_ap, i_=t, np_=npart, nf_=nf:
              e.tensor_copy(o, i_[0:nf_, 0:np_]),
              waits=[pe], inc=True)
    P.tp_gate[i] = ve
    return ve


def emit_load_x(P, xin, cast):
    nc = P.nc
    with ExitStack() as les:
        xtok = les.enter_context(nc.sbuf_tensor(P.uid("xtok"), [128, 17, D], BF16))
        src = xin.ap().rearrange("(tc p) f -> p tc f", p=128)
        eng = "gpsimd" if cast else "sync"
        ev = P.dma(eng, xtok[:, :, :], src)
        for tc in range(17):
            for fc in range(DC):
                emit_transpose(P, sq(P.xT[:, fc, tc * 128:(tc + 1) * 128]),
                               sq(xtok[:, tc, fc * 128:(fc + 1) * 128]),
                               waits=[ev])
        P.xT.evt = P.now("vector")
    P.end_phase()


def emit_store_x(P, xout, as_f32):
    nc = P.nc
    es = P.es
    dtype = F32 if as_f32 else BF16
    NB = 4
    obuf = [es.enter_context(nc.sbuf_tensor(f"obuf{i}", [128, D], dtype))
            for i in range(NB)]
    ogate = [None] * NB
    halves = getattr(P, "xT_half", [])
    hev = [halves[-2] if len(halves) >= 2 else P.xT.evt,
           halves[-1] if halves else P.xT.evt]
    dst = xout.ap().rearrange("(tc p) f -> p tc f", p=128)
    for tc in range(16):
        ob = obuf[tc % NB]
        gate_ev = hev[0] if tc < 8 else hev[1]
        last = None
        for fc in range(DC):
            last = emit_transpose(
                P, ob[:, fc * 128:(fc + 1) * 128],
                sq(P.xT[:, fc, HALO + tc * 128:HALO + (tc + 1) * 128]),
                waits=[gate_ev, ogate[tc % NB]])
        ogate[tc % NB] = P.dma("sync", sq(dst[:, tc, :]), ob[:, :],
                               waits=[last])


def emit_ln(P, tiles, out_xhat, les=None, toff=0):
    nc = P.nc
    les = ExitStack()
    xsq = les.enter_context(nc.sbuf_tensor(P.uid("ln_xsq"), [128, 512], BF16))
    rows = les.enter_context(nc.sbuf_tensor(P.uid("ln_rows"), [1, 2048], F32))
    nmB = les.enter_context(nc.sbuf_tensor(P.uid("ln_nmB"), [128, 512], BF16))
    rsB = les.enter_context(nc.sbuf_tensor(P.uid("ln_rsB"), [128, 512], BF16))
    tb = les.enter_context(nc.sbuf_tensor(P.uid("ln_tb"), [128, 512], BF16))
    tile_evts = []
    for (t0, tl) in tiles:
        bank1, g1, i1 = P.pool.acquire()
        bank2, g2, i2 = P.pool.acquire()
        sx = bank1[0:1, 0:tl]
        sx2 = bank2[0:1, 0:tl]
        pe2 = None
        pe = None
        for c in range(DC):
            xc = sq(P.xT[:, c, t0:t0 + tl])
            ev_sq = P.op("scalar",
                         lambda e, o=xsq[:, 0:tl], i=xc:
                         e.activation(o, i, AF.Square),
                         waits=[P.xT.evt, g1, g2, pe2], inc=True)
            pe = P.op("tensor",
                      lambda e, o=sx, i=xc, st=(c == 0), sp=(c == DC - 1):
                      e.matmul(o, P.ones_c[:, :], i, start=st, stop=sp),
                      waits=[P.xT.evt, g1, P.ev_ones],
                      inc=(c == DC - 1))
            pe2 = P.op("tensor",
                       lambda e, o=sx2, i=xsq[:, 0:tl], st=(c == 0),
                       sp=(c == DC - 1):
                       e.matmul(o, P.ones_c[:, :], i, start=st, stop=sp),
                       waits=[ev_sq, g2], inc=True)
        mrow = rows[:, 0:tl]
        vrow = rows[:, 512:512 + tl]
        rrow = rows[:, 1024:1024 + tl]
        nrow = rows[:, 1536:1536 + tl]
        e1 = P.op("vector", lambda e, o=mrow, i=sx: e.tensor_scalar(
            o, i, 1.0 / D, None, OP.mult), waits=[pe], inc=True)
        P.op("vector", lambda e, o=vrow, i=sx2: e.tensor_scalar(
            o, i, 1.0 / D, None, OP.mult), waits=[pe2])
        P.op("vector", lambda e, o=nrow, i=mrow: e.scalar_tensor_tensor(
            o, i, -1.0, i, OP.mult, OP.mult))
        P.op("vector", lambda e, o=vrow, i=nrow: e.tensor_tensor(
            o, o, i, OP.add))
        e3 = P.op("vector", lambda e, o=vrow: e.tensor_scalar(
            o, o, EPS_LN, None, OP.add), inc=True)
        P.op("scalar", lambda e, o=vrow: e.activation(o, o, AF.Ln),
             waits=[e3])
        e5 = P.op("scalar", lambda e, o=rrow, i=vrow: e.activation(
            o, i, AF.Exp, scale=-0.5), inc=True)
        e6 = P.op("vector", lambda e, o=nrow, i=mrow: e.tensor_scalar(
            o, i, -1.0, None, OP.mult), waits=[e1], inc=True)
        P.pool.release(i1, e6)
        bank3, g3, i3 = P.pool.acquire()
        bank4, g4, i4 = P.pool.acquire()
        pb1 = P.op("tensor", lambda e, o=bank3[:, 0:tl], r=nrow: e.matmul(
            o, P.ones_f[:, :], r, start=True, stop=True),
            waits=[e6, g3, P.ev_ones], inc=True)
        pb2 = P.op("tensor", lambda e, o=bank4[:, 0:tl], r=rrow: e.matmul(
            o, P.ones_f[:, :], r, start=True, stop=True),
            waits=[e5, g4], inc=True)
        P.pool.release(i2, pb2)
        c1 = P.op("vector", lambda e, o=nmB[:, 0:tl], i=bank3[:, 0:tl]:
                  e.tensor_copy(o, i), waits=[pb1], inc=True)
        c2 = P.op("vector", lambda e, o=rsB[:, 0:tl], i=bank4[:, 0:tl]:
                  e.tensor_copy(o, i), waits=[pb2], inc=True)
        P.pool.release(i3, c1)
        P.pool.release(i4, c2)
        last = None
        for c in range(DC):
            xc = sq(P.xT[:, c, t0:t0 + tl])
            oc_ = sq(out_xhat[:, c, t0 - toff:t0 - toff + tl])
            P.op("vector", lambda e, o=tb[:, 0:tl], i=xc, m=nmB[:, 0:tl]:
                 e.tensor_tensor(o, i, m, OP.add), waits=[P.xT.evt, c1])
            last = P.op("vector", lambda e, o=oc_, i=tb[:, 0:tl],
                        r=rsB[:, 0:tl]:
                        e.tensor_tensor(o, i, r, OP.mult),
                        waits=[c2], inc=True)
        tile_evts.append(last)
    out_xhat.evt = P.now("vector")
    P.end_phase()
    les.close()
    return tile_evts


def emit_routing(P, dt_in, layer, tiles, xhat, cT, les_, wgB, ev_wg,
                 toff=0):
    nc = P.nc
    les = ExitStack()
    s = les.enter_context(nc.sbuf_tensor(P.uid("rt_s"), [128, 8], F32))
    pr = les.enter_context(nc.sbuf_tensor(P.uid("rt_pr"), [128, 8], F32))
    eq = les.enter_context(nc.sbuf_tensor(P.uid("rt_eq"), [128, 8], F32))
    aa = les.enter_context(nc.sbuf_tensor(P.uid("rt_aa"), [128, 8], F32))
    bb = les.enter_context(nc.sbuf_tensor(P.uid("rt_bb"), [128, 8], F32))
    f1 = les.enter_context(nc.sbuf_tensor(P.uid("rt_f1"), [128, 8], F32))
    f2 = les.enter_context(nc.sbuf_tensor(P.uid("rt_f2"), [128, 8], F32))
    cw = les.enter_context(nc.sbuf_tensor(P.uid("rt_cw"), [128, 8], F32))
    cwb = les.enter_context(nc.sbuf_tensor(P.uid("rt_cwb"), [128, 8], BF16))
    col = les.enter_context(nc.sbuf_tensor(P.uid("rt_col"), [128, 4], F32))

    def cumsum_excl():
        P.op("vector", lambda e: e.memset(aa[:, 0:1], 0.0))
        P.op("vector", lambda e: e.tensor_copy(aa[:, 1:8], eq[:, 0:7]))
        P.op("vector", lambda e: e.tensor_copy(bb[:, :], aa[:, :]))
        P.op("vector", lambda e: e.tensor_tensor(
            bb[:, 2:8], aa[:, 2:8], aa[:, 0:6], OP.add))
        P.op("vector", lambda e: e.tensor_copy(aa[:, :], bb[:, :]))
        P.op("vector", lambda e: e.tensor_tensor(
            aa[:, 4:8], bb[:, 4:8], bb[:, 0:4], OP.add))

    prev_act = None
    for (t0, tl) in tiles:
        for k in range(tl // 128):
            tt = t0 + k * 128
            bank, gate, bi = P.pool.acquire()
            lg = bank[:, 0:8]
            pe = None
            for dc in range(DC):
                pe = P.op("tensor",
                          lambda e, o=lg,
                          l=sq(xhat[:, dc, tt - toff:tt - toff + 128]),
                          r=sq(wgB[:, dc, :]), st=(dc == 0),
                          sp=(dc == DC - 1):
                          e.matmul(o, l, r, start=st, stop=sp),
                          waits=[xhat.evt, ev_wg, gate],
                          inc=(dc == DC - 1))
            if layer == 0:
                m = P.op("vector", lambda e, i=lg: e.reduce_max(
                    col[:, 0:1], i, axis=AX, negate=True),
                    waits=[pe, prev_act], inc=True)
                ex = P.op("scalar", lambda e, i=lg: e.activation(
                    pr[:, :], i, AF.Exp, bias=col[:, 0:1],
                    accum_out=col[:, 1:2]), waits=[pe, m], inc=True)
                P.op("scalar", lambda e: e.activation(
                    col[:, 2:3], col[:, 1:2], AF.Ln))
                iv = P.op("scalar", lambda e: e.activation(
                    col[:, 2:3], col[:, 2:3], AF.Exp, scale=-1.0), inc=True)
                P.op("vector", lambda e: e.tensor_scalar(
                    s[:, :], pr[:, :], col[:, 2:3], None, OP.mult),
                    waits=[ex, iv])
            else:
                P.op("vector", lambda e, i=lg: e.tensor_copy(s[:, :], i),
                     waits=[pe, prev_act])
            P.op("vector", lambda e: e.reduce_max(col[:, 0:1], s[:, :],
                                                  axis=AX))
            P.op("vector", lambda e: e.tensor_scalar(
                eq[:, :], s[:, :], col[:, 0:1], 0.0, OP.subtract, OP.is_ge))
            cumsum_excl()
            P.op("vector", lambda e: e.tensor_scalar(
                f1[:, :], aa[:, :], 0.0, None, OP.is_equal))
            P.op("vector", lambda e: e.tensor_tensor(
                f1[:, :], f1[:, :], eq[:, :], OP.mult))
            P.op("vector", lambda e: e.scalar_tensor_tensor(
                pr[:, :], f1[:, :], NEG, s[:, :], OP.mult, OP.add))
            P.op("vector", lambda e: e.reduce_max(col[:, 1:2], pr[:, :],
                                                  axis=AX))
            P.op("vector", lambda e: e.tensor_scalar(
                eq[:, :], pr[:, :], col[:, 1:2], 0.0, OP.subtract, OP.is_ge))
            cumsum_excl()
            P.op("vector", lambda e: e.tensor_scalar(
                f2[:, :], aa[:, :], 0.0, None, OP.is_equal))
            P.op("vector", lambda e: e.tensor_tensor(
                f2[:, :], f2[:, :], eq[:, :], OP.mult))
            if layer == 0:
                P.op("vector", lambda e: e.tensor_tensor(
                    f1[:, :], f1[:, :], f2[:, :], OP.add))
                P.op("vector", lambda e: e.tensor_tensor(
                    cw[:, :], f1[:, :], s[:, :], OP.mult))
                cl = P.op("vector", lambda e: e.tensor_scalar(
                    cw[:, :], cw[:, :], 0.5, None, OP.mult), inc=True)
            else:
                dd = P.op("vector", lambda e: e.tensor_tensor(
                    col[:, 2:3], col[:, 1:2], col[:, 0:1], OP.subtract),
                    inc=True)
                x1 = P.op("scalar", lambda e: e.activation(
                    col[:, 2:3], col[:, 2:3], AF.Exp), waits=[dd], inc=True)
                a1 = P.op("vector", lambda e: e.tensor_scalar(
                    col[:, 2:3], col[:, 2:3], 1.0, None, OP.add),
                    waits=[x1], inc=True)
                P.op("scalar", lambda e: e.activation(
                    col[:, 2:3], col[:, 2:3], AF.Ln), waits=[a1])
                g1 = P.op("scalar", lambda e: e.activation(
                    col[:, 2:3], col[:, 2:3], AF.Exp, scale=-1.0), inc=True)
                P.op("vector", lambda e: e.tensor_scalar(
                    col[:, 2:3], col[:, 2:3], 0.5, None, OP.mult),
                    waits=[g1])
                P.op("vector", lambda e: e.tensor_scalar(
                    col[:, 3:4], col[:, 2:3], -1.0, 0.5, OP.mult, OP.add))
                P.op("vector", lambda e: e.tensor_scalar(
                    f1[:, :], f1[:, :], col[:, 2:3], None, OP.mult))
                P.op("vector", lambda e: e.tensor_scalar(
                    f2[:, :], f2[:, :], col[:, 3:4], None, OP.mult))
                cl = P.op("vector", lambda e: e.tensor_tensor(
                    cw[:, :], f1[:, :], f2[:, :], OP.add), inc=True)
            P.pool.release(bi, cl)
            cb = P.op("vector", lambda e: e.tensor_copy(cwb[:, :], cw[:, :]),
                      inc=True)
            emit_transpose(P, cT[0:8, tt:tt + 128], cwb[:, :], waits=[cb])
            prev_act = P.now("scalar")
    if DEBUG and getattr(P, "_rt_dumped", False) is False:
        P._rt_dumped = True
        ve_now = P.now("vector")
        for nm, tt_ in (("rs", s), ("rf1", f1), ("rf2", f2), ("rcw", cw),
                        ("raa", aa), ("req", eq), ("rcol", col)):
            dd = nc.dram_tensor("dbg_" + nm, list(tt_.t.shape
                                if hasattr(tt_, 't') else
                                [128, tt_.free_size]), F32,
                                kind="ExternalOutput") \
                if False else nc.dram_tensor(
                "dbg_" + nm, [128, 8 if nm != "rcol" else 4], F32,
                kind="ExternalOutput")
            P.dma("sync", dd[:, :], tt_[:, :], waits=[ve_now])
    cT.evt = P.now("vector")
    P.end_phase()
    les.close()


def emit_ffn(P, dt_in, layer, halves):
    nc = P.nc
    with ExitStack() as les:
        hw = max(sum(tl for _, tl in tg) for tg in halves)
        xhat = Buf(les.enter_context(
            nc.sbuf_tensor(P.uid("ffn_xhat"), [128, DC, hw], BF16)))
        z = Buf(les.enter_context(
            nc.sbuf_tensor(P.uid("ffn_z"), [128, HFC, hw], BF16)))
        acc = les.enter_context(
            nc.sbuf_tensor(P.uid("ffn_acc"), [128, DC, hw], BF16))
        cT = Buf(les.enter_context(nc.sbuf_tensor(P.uid("ffn_cT"), [8, NT], BF16)))
        gBs = les.enter_context(nc.sbuf_tensor(P.uid("ffn_gB"), [128, 3, 512], F32))
        tbf = les.enter_context(nc.sbuf_tensor(P.uid("ffn_tbf"), [128, 512], BF16))
        sres = les.enter_context(nc.sbuf_tensor(P.uid("ffn_sres"), [128, 512], F32))
        tmg = les.enter_context(nc.sbuf_tensor(P.uid("ffn_tmg"), [128, 512], F32))
        eselb = les.enter_context(nc.sbuf_tensor(P.uid("eselb"), [8, 8, 128], BF16))
        wgT = les.enter_context(nc.sbuf_tensor(P.uid("rt_wgT"), [8, D], BF16))
        wgB = les.enter_context(nc.sbuf_tensor(P.uid("rt_wgB"), [128, DC, 8], BF16))
        nslot = 6 if layer == 1 else 5
        slots = [les.enter_context(
            nc.sbuf_tensor(P.uid("wslot"), [128, 4096], BF16))
            for i in range(nslot)]
        slot_free = [None] * nslot
        slot_ready = [None] * nslot
        slot_i = [0]

        ev_es = P.dma("gpsimd", eselb[:, :, :],
                      dt_in["esel"].ap().rearrange("e (g f) -> e g f", f=128))
        ev_wgT = P.dma("gpsimd", wgT[:, :],
                       dt_in["gWg" if layer == 0 else "hWg"][:, :])
        for dc in range(DC):
            emit_transpose(P, sq(wgB[:, dc, :]),
                           wgT[:, dc * 128:(dc + 1) * 128], waits=[ev_wgT])
        ev_wg = P.now("vector")

        if layer == 0:
            w1ap = dt_in["gW1"]
            w2ap = dt_in["gW2"]
            b1sb = [les.enter_context(
                nc.sbuf_tensor(P.uid("b1sb"), [1, HF], BF16)) for i in range(2)]
            b1_ready = [None, None]
            b1_free = [None, None]
            b2sb = les.enter_context(nc.sbuf_tensor(P.uid("b2sb"), [NE, D], BF16))
            ev_b2 = P.dma("gpsimd", b2sb[:, :], dt_in["gb2"][:, :])
            b1ap = dt_in["gb1"].ap().rearrange("o (e h) -> o e h", e=NE)
        else:
            w1ap = dt_in["hW1"]
            w3ap = dt_in["hW3"]
            w2ap = dt_in["hW2"]
            ev_b1 = ev_b2 = None

        def slot_load(src_ap, split):
            i = slot_i[0]
            slot_i[0] = (slot_i[0] + 1) % len(slots)
            ev = P.dma("gpsimd", slots[i][:, :], src_ap,
                       waits=[slot_free[i]])
            slot_ready[i] = ev
            return i

        def w1_slab(w, e, qq):
            return sq(w[e:e + 1, qq, :, :])

        def w2_slab(e, qq):
            return sq(w2ap[e:e + 1, qq, :, :])

        for half, tiles in enumerate(halves):
            t0h = tiles[0][0]
            sched = []
            for e in range(NE):
                for qq in range(4):
                    sched.append(("w1", e, qq))
                    if layer == 1:
                        sched.append(("w3", e, qq))
                for qq in range(4):
                    sched.append(("w2", e, qq))
            slot_of = {}
            nload = [0]

            def load_next():
                if nload[0] >= len(sched):
                    return
                kind, e, qq = sched[nload[0]]
                if kind == "w1":
                    src, split = w1_slab(w1ap, e, qq), False
                elif kind == "w3":
                    src, split = w1_slab(w3ap, e, qq), False
                else:
                    src, split = w2_slab(e, qq), True
                slot_of[(kind, e, qq)] = slot_load(src, split)
                nload[0] += 1

            for _ in range(min(nslot, len(sched))):
                load_next()

            xh_evts = emit_ln(P, tiles, xhat, toff=t0h)
            emit_routing(P, dt_in, layer, tiles, xhat, cT, les, wgB, ev_wg,
                         toff=t0h)
            if DEBUG and half == 0:
                d1 = nc.dram_tensor("dbg_xhat", [128, DC * hw], BF16,
                                    kind="ExternalOutput")
                P.dma("sync", d1[:, :],
                      xhat.t[:, :, :].rearrange("p a b -> p (a b)"),
                      waits=[xhat.evt])
                d2 = nc.dram_tensor("dbg_cT", [8, NT], BF16,
                                    kind="ExternalOutput")
                P.dma("sync", d2[:, :], cT.t[:, :], waits=[cT.evt])

            last_ze = None
            if layer == 0:
                b1_ready[0] = P.dma("gpsimd", b1sb[0][:, :], row(b1ap[:, 0:1, :]))
            for e in range(NE):
                if layer == 0 and e + 1 < NE:
                    b1_ready[(e + 1) % 2] = P.dma(
                        "gpsimd", b1sb[(e + 1) % 2][:, :],
                        row(b1ap[:, e + 1:e + 2, :]), waits=[b1_free[(e + 1) % 2]])
                gev = []
                for ti, (t0, tl) in enumerate(tiles):
                    bank, gate, bi = P.pool.acquire()
                    pe = P.op("tensor", lambda e_, o=bank[:, 0:tl],
                              l=sq(eselb[:, e:e + 1, :]),
                              r=cT[0:8, t0:t0 + tl]:
                              e_.matmul(o, l, r, start=True, stop=True),
                              waits=[cT.evt, ev_es, gate], inc=True)
                    ge = P.op("vector", lambda e_, o=sq(gBs[:, ti, 0:tl]),
                              i=bank[:, 0:tl]: e_.tensor_copy(o, i),
                              waits=[pe], inc=True)
                    gev.append(ge)
                    P.pool.release(bi, ge)

                for qq in range(4):
                    si1 = slot_of[("w1", e, qq)]
                    sl1 = slots[si1][:, :].rearrange("p (a b) -> p a b", a=8)
                    if layer == 1:
                        si3 = slot_of[("w3", e, qq)]
                        sl3 = slots[si3][:, :].rearrange("p (a b) -> p a b",
                                                         a=8)
                    for hci in range(4):
                        hc = qq * 4 + hci
                        nb = len(tiles)
                        bks = [P.pool.acquire() for _ in range(nb)]
                        if layer == 0:
                            pe1_t = [None] * len(tiles)
                            for dc in range(DC):
                                for ti, (t0, tl) in enumerate(tiles):
                                    bank, gate, bi = bks[ti]
                                    pe1_t[ti] = P.op(
                                        "tensor",
                                        lambda e_, o=bank[:, 0:tl],
                                        l=sq(sl1[:, dc,
                                                 hci * 128:(hci + 1) * 128]),
                                        r=sq(xhat[:, dc,
                                                  t0 - t0h:t0 - t0h + tl]),
                                        st=(dc == 0):
                                        e_.matmul(o, l, r, start=st,
                                                  stop=False),
                                        waits=[slot_ready[si1], xh_evts[ti],
                                               gate])
                            for ti, (t0, tl) in enumerate(tiles):
                                bank, gate, bi = bks[ti]
                                pe = P.op(
                                    "tensor",
                                    lambda e_, o=bank[:, 0:tl],
                                    l=b1sb[e % 2][0:1,
                                                  hc * 128:(hc + 1) * 128],
                                    r=P.ones_r[:, 0:tl]:
                                    e_.matmul(o, l, r, start=False,
                                              stop=True),
                                    waits=[b1_ready[e % 2], P.ev_ones],
                                    inc=True)
                                ze = P.op(
                                    "scalar",
                                    lambda e_,
                                    o=sq(z[:, hc, t0 - t0h:t0 - t0h + tl]),
                                    i=bank[:, 0:tl]:
                                    e_.activation(o, i, AF.Relu),
                                    waits=[pe], inc=True)
                                P.pool.release(bi, ze)
                        else:
                            for ti, (t0, tl) in enumerate(tiles):
                                bank, gate, bi = bks[ti]
                                pe = None
                                for dc in range(DC):
                                    pe = P.op(
                                        "tensor",
                                        lambda e_, o=bank[:, 0:tl],
                                        l=sq(sl1[:, dc,
                                                 hci * 128:(hci + 1) * 128]),
                                        r=sq(xhat[:, dc,
                                                  t0 - t0h:t0 - t0h + tl]),
                                        st=(dc == 0), sp=(dc == DC - 1):
                                        e_.matmul(o, l, r, start=st, stop=sp),
                                        waits=[slot_ready[si1], xh_evts[ti],
                                               gate],
                                        inc=(dc == DC - 1))
                                bank2, gate2, bi2 = P.pool.acquire()
                                pe2 = None
                                for dc in range(DC):
                                    pe2 = P.op(
                                        "tensor",
                                        lambda e_, o=bank2[:, 0:tl],
                                        l=sq(sl3[:, dc,
                                                 hci * 128:(hci + 1) * 128]),
                                        r=sq(xhat[:, dc,
                                                  t0 - t0h:t0 - t0h + tl]),
                                        st=(dc == 0), sp=(dc == DC - 1):
                                        e_.matmul(o, l, r, start=st, stop=sp),
                                        waits=[slot_ready[si3], xh_evts[ti],
                                               gate2],
                                        inc=(dc == DC - 1))
                                ge_ = P.op(
                                    "scalar",
                                    lambda e_, o=tmg[:, 0:tl],
                                    i=bank[:, 0:tl]:
                                    e_.activation(o, i, AF.Silu),
                                    waits=[pe, last_ze], inc=True)
                                ze = P.op(
                                    "vector",
                                    lambda e_,
                                    o=sq(z[:, hc, t0 - t0h:t0 - t0h + tl]),
                                    a=tmg[:, 0:tl], b=bank2[:, 0:tl]:
                                    e_.tensor_tensor(o, a, b, OP.mult),
                                    waits=[ge_, pe2], inc=True)
                                last_ze = ze
                                P.pool.release(bi, ge_)
                                P.pool.release(bi2, ze)
                    slot_free[si1] = P.now("tensor")
                    if layer == 1:
                        slot_free[si3] = P.now("tensor")
                    load_next()
                    if layer == 1:
                        load_next()
                if layer == 0:
                    b1_free[e % 2] = P.now("tensor")
                z.evt = P.now("vector" if layer == 1 else "scalar")
                if DEBUG and half == 0 and e == 0:
                    d3 = nc.dram_tensor("dbg_z0", [128, HFC * hw], BF16,
                                        kind="ExternalOutput")
                    P.dma("sync", d3[:, :],
                          z.t[:, :, :].rearrange("p a b -> p (a b)"),
                          waits=[z.evt])

                for qq in range(4):
                    si2 = slot_of[("w2", e, qq)]
                    sl2 = slots[si2][:, :].rearrange("p (a b) -> p a b", a=16)
                    for oci in range(2):
                        oc = qq * 2 + oci
                        nb = len(tiles)
                        bks = [P.pool.acquire() for _ in range(nb)]
                        pe_t = [None] * nb
                        for fc in range(HFC):
                            for ti, (t0, tl) in enumerate(tiles):
                                bank, gate, bi = bks[ti]
                                pe_t[ti] = P.op(
                                    "tensor",
                                    lambda e_, o=bank[:, 0:tl],
                                    l=sq(sl2[:, fc,
                                             oci * 128:(oci + 1) * 128]),
                                    r=sq(z[:, fc, t0 - t0h:t0 - t0h + tl]),
                                    st=(fc == 0), sp=(fc == HFC - 1):
                                    e_.matmul(o, l, r, start=st, stop=sp),
                                    waits=[slot_ready[si2], z.evt, gate],
                                    inc=(fc == HFC - 1))
                        for ti, (t0, tl) in enumerate(tiles):
                            bank, gate, bi = bks[ti]
                            accv = sq(acc[:, oc, t0 - t0h:t0 - t0h + tl])
                            if e == 0:
                                ae = P.op("vector",
                                          lambda e_, o=accv, i=bank[:, 0:tl],
                                          g=sq(gBs[:, ti, 0:tl]):
                                          e_.tensor_tensor(o, i, g, OP.mult),
                                          waits=[pe_t[ti], gev[ti]],
                                          inc=True)
                            else:
                                P.op("vector",
                                     lambda e_, o=tbf[:, 0:tl],
                                     i=bank[:, 0:tl], g=sq(gBs[:, ti, 0:tl]):
                                     e_.tensor_tensor(o, i, g, OP.mult),
                                     waits=[pe_t[ti], gev[ti]])
                                ae = P.op("vector",
                                          lambda e_, o=accv, i=tbf[:, 0:tl]:
                                          e_.tensor_tensor(o, o, i, OP.add),
                                          inc=True)
                            P.pool.release(bi, ae)
                    slot_free[si2] = P.now("tensor")
                    load_next()
            acc_evt = P.now("vector")
            if DEBUG and half == 0:
                d4 = nc.dram_tensor("dbg_acc", [128, DC * hw], BF16,
                                    kind="ExternalOutput")
                P.dma("sync", d4[:, :],
                      acc[:, :, :].rearrange("p a b -> p (a b)"),
                      waits=[acc_evt])

            for ti, (t0, tl) in enumerate(tiles):
                for oc in range(DC):
                    accv = sq(acc[:, oc, t0 - t0h:t0 - t0h + tl])
                    xv = sq(P.xT[:, oc, t0:t0 + tl])
                    if layer == 0:
                        bank, gate, bi = P.pool.acquire()
                        pe = P.op("tensor",
                                  lambda e_, o=bank[:, 0:tl],
                                  l=b2sb[:, oc * 128:(oc + 1) * 128],
                                  r=cT[0:8, t0:t0 + tl]:
                                  e_.matmul(o, l, r, start=True, stop=True),
                                  waits=[ev_b2, cT.evt, gate], inc=True)
                        P.op("vector", lambda e_, o=sres[:, 0:tl], a=xv,
                             b=accv: e_.tensor_tensor(o, a, b, OP.add),
                             waits=[acc_evt, P.xT.evt])
                        re = P.op("vector",
                                  lambda e_, o=xv, a=sres[:, 0:tl],
                                  b=bank[:, 0:tl]:
                                  e_.tensor_tensor(o, a, b, OP.add),
                                  waits=[pe], inc=True)
                        P.pool.release(bi, re)
                    else:
                        P.op("vector",
                             lambda e_, o=xv, a=xv, b=accv:
                             e_.tensor_tensor(o, a, b, OP.add),
                             waits=[acc_evt, P.xT.evt], inc=True)
            P.xT.evt = P.now("vector")
            P.xT_half = getattr(P, "xT_half", [])
            P.xT_half.append(P.xT.evt)


def emit_attn_l0(P, dt_in):
    nc = P.nc
    with ExitStack() as les:
        xhat = Buf(les.enter_context(
            nc.sbuf_tensor(P.uid("a0_xhat"), [128, DC, NT], BF16)))
        aoT = Buf(les.enter_context(
            nc.sbuf_tensor(P.uid("a0_aoT"), [128, DC, NT_OWN], BF16)))
        cosT = les.enter_context(nc.sbuf_tensor(P.uid("a0_cos"), [128, NT], F32))
        sinT = les.enter_context(nc.sbuf_tensor(P.uid("a0_sin"), [128, NT], F32))
        pswb = les.enter_context(nc.sbuf_tensor(P.uid("a0_psw"), [128, 128], BF16))
        mask0 = les.enter_context(nc.sbuf_tensor(P.uid("a0_m0"), [128, 256],
                                                 F32))
        maskr = les.enter_context(nc.sbuf_tensor(P.uid("a0_mr"), [128, 256],
                                                 F32))
        wsl = [les.enter_context(
            nc.sbuf_tensor(P.uid("a0_w"), [128, DC, 128], BF16))
            for i in range(3)]
        woT = les.enter_context(nc.sbuf_tensor(P.uid("a0_wo"), [128, DC, D], BF16))
        bqkv = [les.enter_context(nc.sbuf_tensor(P.uid("a0_bias"),
                                                 [1, 3, 128], BF16))
                for i in range(2)]
        bo_sb = les.enter_context(nc.sbuf_tensor(P.uid("a0_bo"), [1, D],
                                                 BF16))
        qT = les.enter_context(nc.sbuf_tensor(P.uid("a0_qT"), [128, NT_OWN], BF16))
        kT = les.enter_context(nc.sbuf_tensor(P.uid("a0_kT"), [128, NT], BF16))
        v2 = les.enter_context(nc.sbuf_tensor(P.uid("a0_v2"), [128, 17, 128], BF16))
        qtmp = les.enter_context(nc.sbuf_tensor(P.uid("a0_qtmp"), [128, 512], BF16))
        rt1 = les.enter_context(nc.sbuf_tensor(P.uid("a0_rt1"), [128, 512], F32))
        rt2 = les.enter_context(nc.sbuf_tensor(P.uid("a0_rt2"), [128, 512], F32))
        NR = 6
        sm = [les.enter_context(nc.sbuf_tensor(P.uid("a0_sm"), [128, 256],
                                               F32)) for _ in range(NR)]
        exs = [les.enter_context(nc.sbuf_tensor(P.uid("a0_exs"), [128, 256],
                                                F32)) for _ in range(NR)]
        prb = [les.enter_context(nc.sbuf_tensor(P.uid("a0_prb"), [128, 256],
                                                BF16)) for _ in range(NR)]
        prT = [les.enter_context(nc.sbuf_tensor(P.uid("a0_prT"), [128, 256],
                                                BF16)) for _ in range(NR)]
        srow = [les.enter_context(nc.sbuf_tensor(P.uid("a0_srow"), [128, 4],
                                                 F32)) for _ in range(NR)]
        tbf = les.enter_context(nc.sbuf_tensor(P.uid("a0_tbf"), [128, 512], BF16))

        ev_cos = P.dma("sync", cosT[:, :], dt_in["ropecos"][:, :])
        ev_sin = P.dma("sync", sinT[:, :], dt_in["ropesin"][:, :])
        ev_psw = P.dma("gpsimd", pswb[:, :], dt_in["pswap"][:, :])
        ev_m0 = P.dma("sync", mask0[:, :], dt_in["mask0"][:, :])
        ev_mr = P.dma("sync", maskr[:, :], dt_in["maskr"][:, :])
        ev_wo = P.dma("gpsimd", woT[:, :, :],
                      dt_in["aWoT"].ap().rearrange("(fc p) o -> p fc o",
                                                   p=128))
        ev_bo = P.dma("gpsimd", bo_sb[:, :], dt_in["abo"][:, :])

        emit_ln(P, [t for tg in TILES_S1 for t in tg], xhat)

        wfree = None
        bias_free = [None, None]
        prev_soft = [None, None]
        for hp in range(8):
            evw = []
            for i, n in enumerate(("aWqT", "aWkT", "aWvT")):
                src = dt_in[n].ap().rearrange("(dc p) f -> p dc f", p=128)[
                    :, :, hp * 128:(hp + 1) * 128]
                evw.append(P.dma("gpsimd", wsl[i][:, :, :], src,
                                 waits=[wfree]))
            bb_ = bqkv[hp % 2]
            ev_b = [P.dma("gpsimd", row(bb_[:, i:i + 1, :]),
                          dt_in[n][0:1, hp * 128:(hp + 1) * 128],
                          waits=[bias_free[hp % 2]])
                    for i, n in enumerate(("abq", "abk", "abv"))]
            for which in (0, 1):
                tiles = [t for tg in (TILES_OWN if which == 0 else TILES_S1)
                         for t in tg]
                dstT = qT if which == 0 else kT
                off = HALO if which == 0 else 0
                scale = 0.125 if which == 0 else 1.0
                for (t0, tl) in tiles:
                    bank, gate, bi = P.pool.acquire()
                    for dc in range(DC):
                        P.op("tensor",
                             lambda e_, o=bank[:, 0:tl],
                             l=sq(wsl[which][:, dc, :]),
                             r=sq(xhat[:, dc, t0:t0 + tl]), st=(dc == 0):
                             e_.matmul(o, l, r, start=st, stop=False),
                             waits=[evw[which], xhat.evt, gate])
                    pe = P.op("tensor",
                              lambda e_, o=bank[:, 0:tl],
                              l=row(bb_[:, which:which + 1, :]),
                              r=P.ones_r[:, 0:tl]:
                              e_.matmul(o, l, r, start=False, stop=True),
                              waits=[ev_b[which], P.ev_ones], inc=True)
                    cb = P.op("vector",
                              lambda e_, o=qtmp[:, 0:tl], i=bank[:, 0:tl]:
                              e_.tensor_copy(o, i), waits=[pe], inc=True)
                    bank2, gate2, bi2 = P.pool.acquire()
                    pes = P.op("tensor",
                               lambda e_, o=bank2[:, 0:tl],
                               r=qtmp[:, 0:tl]:
                               e_.matmul(o, pswb[:, :], r, start=True,
                                         stop=True),
                               waits=[cb, ev_psw, gate2], inc=True)
                    P.op("vector",
                         lambda e_, o=rt1[:, 0:tl], i=bank[:, 0:tl],
                         c=cosT[:, t0:t0 + tl], sc=scale:
                         e_.scalar_tensor_tensor(o, i, sc, c, OP.mult,
                                                 OP.mult),
                         waits=[ev_cos])
                    P.op("vector",
                         lambda e_, o=rt2[:, 0:tl], i=bank2[:, 0:tl],
                         c=sinT[:, t0:t0 + tl], sc=scale:
                         e_.scalar_tensor_tensor(o, i, sc, c, OP.mult,
                                                 OP.mult),
                         waits=[pes, ev_sin])
                    ve2 = P.op("vector",
                               lambda e_, o=dstT[:, t0 - off:t0 - off + tl],
                               a=rt1[:, 0:tl], b=rt2[:, 0:tl]:
                               e_.tensor_tensor(o, a, b, OP.add),
                               inc=True)
                    P.pool.release(bi, ve2)
                    P.pool.release(bi2, ve2)
            ev_q = P.now("vector")
            for tc in range(17):
                bank, gate, bi = P.pool.acquire()
                for dc in range(DC):
                    P.op("tensor",
                         lambda e_, o=bank[:, 0:128],
                         l=sq(xhat[:, dc, tc * 128:(tc + 1) * 128]),
                         r=sq(wsl[2][:, dc, :]), st=(dc == 0):
                         e_.matmul(o, l, r, start=st, stop=False),
                         waits=[evw[2], xhat.evt, gate])
                pe = P.op("tensor",
                          lambda e_, o=bank[:, 0:128],
                          r=row(bb_[:, 2:3, :]):
                          e_.matmul(o, P.ones_r[0:1, 0:128], r, start=False,
                                    stop=True),
                          waits=[ev_b[2], P.ev_ones], inc=True)
                ve = P.op("vector",
                          lambda e_, o=sq(v2[:, tc, :]), i=bank[:, 0:128]:
                          e_.tensor_copy(o, i), waits=[pe], inc=True)
                P.pool.release(bi, ve)
            ev_v = P.now("vector")
            wfree = ev_v
            bias_free[hp % 2] = P.now("tensor")

            iters = [(hh, qb) for hh in range(2) for qb in range(16)]
            LAG = 5
            st_ev = {}
            st_bank = {}

            def stage_a(it):
                hh, qb = iters[it]
                h64 = hh * 64
                r_ = it % NR
                bank, gate, bi = P.pool.acquire()
                pes = P.op("tensor",
                           lambda e_, o=bank[:, 0:256],
                           l=qT[h64:h64 + 64, qb * 128:(qb + 1) * 128],
                           r=kT[h64:h64 + 64, qb * 128:qb * 128 + 256]:
                           e_.matmul(o, l, r, start=True, stop=True),
                           waits=[ev_q, gate], inc=True)
                mk = mask0 if qb == 0 else maskr
                sr = srow[r_]
                e1 = P.op("vector",
                          lambda e_, o=sm[r_], a=bank[:, 0:256], b=mk:
                          e_.tensor_tensor(o[:, :], a, b[:, :], OP.add),
                          waits=[pes, ev_m0, ev_mr], inc=True)
                P.pool.release(bi, e1)
                e2 = P.op("vector",
                          lambda e_, o=sr, i=sm[r_]:
                          e_.reduce_max(o[:, 0:1], i[:, :], axis=AX,
                                        negate=True),
                          inc=True)
                e3 = P.op("scalar",
                          lambda e_, o=exs[r_], i=sm[r_], s=sr:
                          e_.activation(o[:, :], i[:, :], AF.Exp,
                                        bias=s[:, 0:1],
                                        accum_out=s[:, 1:2]),
                          waits=[e1, e2], inc=True)
                P.op("scalar", lambda e_, s=sr: e_.activation(
                    s[:, 2:3], s[:, 1:2], AF.Ln))
                e4 = P.op("scalar", lambda e_, s=sr: e_.activation(
                    s[:, 2:3], s[:, 2:3], AF.Exp, scale=-1.0), inc=True)
                e5 = P.op("gpsimd",
                          lambda e_, o=prb[r_], i=exs[r_], s=sr:
                          e_.tensor_scalar(o[:, :], i[:, :], s[:, 2:3],
                                           None, OP.mult),
                          waits=[e3, e4], inc=True)
                st_ev[it] = e5

            def stage_b(it):
                hh, qb = iters[it]
                h64 = hh * 64
                r_ = it % NR
                e5 = st_ev[it]
                for bb in range(2):
                    emit_transpose(P, prT[r_][:, bb * 128:(bb + 1) * 128],
                                   prb[r_][:, bb * 128:(bb + 1) * 128],
                                   waits=[e5])
                eT = P.now("vector")
                bank2, gate2, bi2 = P.pool.acquire()
                pe2 = None
                for bb in range(2):
                    pe2 = P.op("tensor",
                               lambda e_, o=bank2[0:64, 0:128],
                               l=sq(v2[:, qb + bb, h64:h64 + 64]),
                               r=prT[r_][:, bb * 128:(bb + 1) * 128],
                               st=(bb == 0):
                               e_.matmul(o, l, r, start=st, stop=(bb == 1)),
                               waits=[ev_v, eT, gate2], inc=(bb == 1))
                ve = P.op("vector",
                          lambda e_,
                          o=aoT.t[h64:h64 + 64, hp,
                                  qb * 128:(qb + 1) * 128].squeeze(),
                          i=bank2[0:64, 0:128]:
                          e_.tensor_copy(o, i), waits=[pe2], inc=True)
                P.pool.release(bi2, ve)

            for it in range(len(iters) + LAG):
                if it >= LAG:
                    stage_b(it - LAG)
                if it < len(iters):
                    stage_a(it)
        aoT.evt = P.now("vector")

        for (t0, tl) in [t for tg in TILES_OWN for t in tg]:
            for oc in range(DC):
                bank, gate, bi = P.pool.acquire()
                for fc in range(DC):
                    P.op("tensor",
                         lambda e_, o=bank[:, 0:tl],
                         l=sq(woT[:, fc, oc * 128:(oc + 1) * 128]),
                         r=sq(aoT[:, fc, t0 - HALO:t0 - HALO + tl]),
                         st=(fc == 0):
                         e_.matmul(o, l, r, start=st, stop=False),
                         waits=[ev_wo, aoT.evt, gate])
                pe = P.op("tensor",
                          lambda e_, o=bank[:, 0:tl],
                          l=bo_sb[0:1, oc * 128:(oc + 1) * 128],
                          r=P.ones_r[:, 0:tl]:
                          e_.matmul(o, l, r, start=False, stop=True),
                          waits=[ev_bo, P.ev_ones], inc=True)
                P.op("vector",
                     lambda e_, o=tbf[:, 0:tl], i=bank[:, 0:tl]:
                     e_.tensor_copy(o, i), waits=[pe])
                re = P.op("vector",
                          lambda e_, o=sq(P.xT[:, oc, t0:t0 + tl]),
                          i=tbf[:, 0:tl]:
                          e_.tensor_tensor(o, o, i, OP.add),
                          waits=[P.xT.evt], inc=True)
                P.pool.release(bi, re)
        P.xT.evt = P.now("vector")


def emit_elu(P, bank, tl, t1, t2, out_ap, pe):
    np_ = bank.shape[0]
    mn = P.op("vector", lambda e_, o=t1[0:np_, 0:tl], i=bank[0:np_, 0:tl]:
              e_.tensor_scalar(o, i, 0.0, None, OP.min),
              waits=[pe, getattr(P, "_elu_last", None)], inc=True)
    a1 = P.op("scalar", lambda e_, o=t1[0:np_, 0:tl]:
              e_.activation(o, o, AF.Exp), waits=[mn], inc=True)
    P.op("vector", lambda e_, o=t2[0:np_, 0:tl], i=bank[0:np_, 0:tl]:
         e_.tensor_scalar(o, i, 0.0, None, OP.max), waits=[pe])
    ve = P.op("vector", lambda e_, o=out_ap, a=t1[0:np_, 0:tl],
              b=t2[0:np_, 0:tl]:
              e_.tensor_tensor(o, a, b, OP.add), waits=[a1], inc=True)
    P._elu_last = a1
    return ve


def emit_attn_l1(P, dt_in):
    nc = P.nc
    with ExitStack() as les:
        xhat = Buf(les.enter_context(
            nc.sbuf_tensor(P.uid("a1_xhat"), [128, DC, NT_OWN], BF16)))
        wk = les.enter_context(nc.sbuf_tensor(P.uid("a1_wk"), [128, DC, D], BF16))
        wv = les.enter_context(nc.sbuf_tensor(P.uid("a1_wv"), [128, DC, D], BF16))
        wo = les.enter_context(nc.sbuf_tensor(P.uid("a1_wo"), [128, DC, D], BF16))
        biasr = les.enter_context(nc.sbuf_tensor(P.uid("a1_bias"), [1, 4, D], BF16))
        evk = P.dma("gpsimd", wk[:, :, :],
                    dt_in["lWk"].ap().rearrange("(dc p) f -> p dc f", p=128))
        evv = P.dma("gpsimd", wv[:, :, :],
                    dt_in["lWv"].ap().rearrange("(dc p) f -> p dc f", p=128))
        evo = P.dma("gpsimd", wo[:, :, :],
                    dt_in["lWo"].ap().rearrange("(fc p) o -> p fc o", p=128))
        ev_b = [P.dma("gpsimd", row(biasr[:, i:i + 1, :]), dt_in[n][:, :])
                for i, n in enumerate(("lbq", "lbk", "lbv", "lbo"))]

        emit_ln(P, [t for tg in TILES_OWN for t in tg], xhat, toff=HALO)

        kfc = les.enter_context(nc.sbuf_tensor(P.uid("a1_kfc"), [128, D], BF16))
        vc = les.enter_context(nc.sbuf_tensor(P.uid("a1_vc"), [128, D], BF16))
        t1 = les.enter_context(nc.sbuf_tensor(P.uid("a1_t1"), [128, 512], F32))
        t2 = les.enter_context(nc.sbuf_tensor(P.uid("a1_t2"), [128, 512], F32))
        kvs = les.enter_context(nc.sbuf_tensor(P.uid("a1_kvs"), [64, 1040], F32))
        kvz = les.enter_context(nc.sbuf_tensor(P.uid("a1_kvz"), [64, 1040], F32))
        kvb = les.enter_context(nc.sbuf_tensor(P.uid("a1_kvb"), [64, 1040], BF16))
        smk = les.enter_context(nc.sbuf_tensor(P.uid("a1_smk"), [128, 2], F32))
        dens = [les.enter_context(nc.sbuf_tensor(P.uid("a1_dens"),
                                                  [128, 16], F32))
                for _ in range(2)]
        tbf = les.enter_context(nc.sbuf_tensor(P.uid("a1_tbf"), [128, 512], BF16))
        ev_sm = P.dma("sync", smk[:, :], dt_in["seqmask"][:, :])

        wq = les.enter_context(
            nc.sbuf_tensor(P.uid("a1_wq"), [128, DC, D], BF16))
        evq = P.dma("gpsimd", wq[:, :, :],
                    dt_in["lWq"].ap().rearrange("(dc p) f -> p dc f", p=128))
        qfT = [les.enter_context(
            nc.sbuf_tensor(P.uid("a1_qfT"), [64, NH, 512], BF16))
            for _ in range(2)]
        yz = les.enter_context(nc.sbuf_tensor(P.uid("a1_yz"),
                                              [128, 4, D], BF16))
        yzT = les.enter_context(nc.sbuf_tensor(P.uid("a1_yzT"),
                                               [128, DC, 512], BF16))

        kvbankA, gA, iA = P.pool.acquire()
        kvbankB, gB_, iB = P.pool.acquire()
        kvbankC, gC_, iC = P.pool.acquire()
        started = [False, False, False]
        pe_kv = None
        units = [(tc, half) for tc in range(16) for half in range(2)]
        u_kf = {}
        u_vc = {}

        def stage_kf(u):
            tc, half = units[u]
            trow = tc * 128
            bank, gate, bi = P.pool.acquire()
            peh = None
            for dc in range(DC):
                peh = P.op("tensor",
                           lambda e_, o=bank[:, 0:512],
                           l=sq(xhat[:, dc, trow:trow + 128]),
                           r=sq(wk[:, dc, half * 512:half * 512 + 512]),
                           st=(dc == 0):
                           e_.matmul(o, l, r, start=st, stop=False),
                           waits=[evk, xhat.evt, gate])
            peh = P.op("tensor",
                       lambda e_, o=bank[:, 0:512],
                       r=row(biasr[0:1, 1:2, half * 512:half * 512 + 512]):
                       e_.matmul(o, P.ones_r[0:1, 0:128], r, start=False,
                                 stop=True),
                       waits=[ev_b[1], P.ev_ones], inc=True)
            ve = emit_elu(P, bank, 512, t1, t2,
                          kfc[:, half * 512:half * 512 + 512], peh)
            P.pool.release(bi, ve)
            u_kf[u] = ve

        def stage_v(u):
            tc, half = units[u]
            trow = tc * 128
            bank, gate, bi = P.pool.acquire()
            peh = None
            for dc in range(DC):
                peh = P.op("tensor",
                           lambda e_, o=bank[:, 0:512],
                           l=sq(xhat[:, dc, trow:trow + 128]),
                           r=sq(wv[:, dc, half * 512:half * 512 + 512]),
                           st=(dc == 0):
                           e_.matmul(o, l, r, start=st, stop=False),
                           waits=[evv, xhat.evt, gate])
            peh = P.op("tensor",
                       lambda e_, o=bank[:, 0:512],
                       r=row(biasr[0:1, 2:3, half * 512:half * 512 + 512]):
                       e_.matmul(o, P.ones_r[0:1, 0:128], r, start=False,
                                 stop=True),
                       waits=[ev_b[2], P.ev_ones], inc=True)
            ve = P.op("vector",
                      lambda e_, o=vc[:, half * 512:half * 512 + 512],
                      i=bank[:, 0:512]: e_.tensor_copy(o, i),
                      waits=[peh], inc=True)
            P.pool.release(bi, ve)
            u_vc[u] = ve

        def stage_kv(u):
            nonlocal pe_kv
            tc, half = units[u]
            for hh in range(8):
                h = half * 8 + hh
                bk = kvbankA if half == 0 else kvbankB
                bix = half
                st = not started[bix]
                started[bix] = True
                P.op("tensor",
                     lambda e_, o=bk[0:64, hh * 64:(hh + 1) * 64],
                     l=kfc[:, h * 64:(h + 1) * 64],
                     r=vc[:, h * 64:(h + 1) * 64], s=st:
                     e_.matmul(o, l, r, start=s, stop=False,
                               skip_group_check=True),
                     waits=[u_kf[u], u_vc[u],
                            (gA if half == 0 else gB_) if st else None])
                stC = not started[2]
                started[2] = True
                pe_kv = P.op("tensor",
                             lambda e_, o=kvbankC[0:64, h:h + 1],
                             l=kfc[:, h * 64:(h + 1) * 64], s=stC:
                             e_.matmul(o, l, P.ones_c[:, 0:1], start=s,
                                       stop=(u == len(units) - 1
                                             and hh == 7),
                                       skip_group_check=True),
                             waits=[P.ev_ones, gC_ if stC else None],
                             inc=(u == len(units) - 1 and hh == 7))

        stage_kf(0)
        stage_v(0)
        for u in range(len(units)):
            if u + 1 < len(units):
                stage_kf(u + 1)
            stage_kv(u)
            if u + 1 < len(units):
                stage_v(u + 1)
        kvs3 = kvs[:, :].rearrange("p (h c) -> p h c", c=65)
        ev1 = P.op("vector", lambda e_,
                   o=kvs3[:, 0:8, 0:64],
                   i=kvbankA[0:64, 0:512].rearrange("p (h c) -> p h c", c=64):
                   e_.tensor_copy(o, i), waits=[pe_kv], inc=True)
        ev2 = P.op("vector", lambda e_,
                   o=kvs3[:, 8:16, 0:64],
                   i=kvbankB[0:64, 0:512].rearrange("p (h c) -> p h c", c=64):
                   e_.tensor_copy(o, i), waits=[pe_kv], inc=True)
        ev3 = P.op("vector", lambda e_,
                   o=kvs3[:, 0:16, 64:65],
                   i=kvbankC[0:64, 0:16].rearrange("p (h c) -> p h c", c=1):
                   e_.tensor_copy(o, i), waits=[pe_kv], inc=True)
        if DEBUG:
            dkv = nc.dram_tensor("dbg_kvs", [64, 1040], F32,
                                 kind="ExternalOutput")
            P.dma("sync", dkv[:, :], kvs[:, :], waits=[ev1, ev2, ev3])
        P.pool.release(iA, ev1)
        P.pool.release(iB, ev2)
        P.pool.release(iC, ev3)
        z1 = P.op("vector", lambda e_: e_.tensor_scalar(
            kvz[:, :], kvs[:, :], smk[0:64, 0:1], None, OP.mult),
            waits=[ev1, ev2, ev3, ev_sm], inc=True)
        d1 = P.dma("sync", P.cc_in[0:64, :], kvz[:, :], waits=[z1])
        z2 = P.op("vector", lambda e_: e_.tensor_scalar(
            kvz[:, :], kvs[:, :], smk[0:64, 1:2], None, OP.mult),
            waits=[d1], inc=True)
        d2 = P.dma("sync", P.cc_in[64:128, :], kvz[:, :], waits=[z2])
        ccs = P.sems["cc"]
        ccs.n += 1
        P.q["gpsimd"].append((
            ((P.sems["dsy"], P.sems["dsy"].n),),
            lambda e: e.collective_compute(
                "AllReduce", OP.add,
                replica_groups=[list(range(NCORES))],
                ins=[P.cc_in.ap().opt()],
                outs=[P.cc_out.ap().opt()]),
            ((ccs, 1),), None))
        ev_cc = (ccs, ccs.n)
        l1 = P.dma("sync", kvz[:, :], P.cc_out[0:64, :], waits=[ev_cc])
        m1 = P.op("vector", lambda e_: e_.tensor_scalar(
            kvs[:, :], kvz[:, :], smk[0:64, 0:1], None, OP.mult),
            waits=[l1], inc=True)
        l2 = P.dma("sync", kvz[:, :], P.cc_out[64:128, :], waits=[m1])
        P.op("vector", lambda e_: e_.tensor_scalar(
            kvz[:, :], kvz[:, :], smk[0:64, 1:2], None, OP.mult), waits=[l2])
        P.op("vector", lambda e_: e_.tensor_tensor(
            kvs[:, :], kvs[:, :], kvz[:, :], OP.add))
        ev_kvb = P.op("vector", lambda e_: e_.tensor_copy(
            kvb[:, :], kvs[:, :]), inc=True)
        if DEBUG:
            dkb = nc.dram_tensor("dbg_kvb", [64, 1040], BF16,
                                 kind="ExternalOutput")
            P.dma("sync", dkb[:, :], kvb[:, :], waits=[ev_kvb])

        all_tiles = [t for tg in TILES_OWN for t in tg]
        qev = [None, None]

        def emit_qproj(ti):
            t0, tl = all_tiles[ti]
            sl = ti % 2
            for h in range(NH):
                bank, gate, bi = P.pool.acquire()
                for dc in range(DC):
                    P.op("tensor",
                         lambda e_, o=bank[0:64, 0:tl],
                         l=sq(wq[:, dc, h * 64:(h + 1) * 64]),
                         r=sq(xhat[:, dc, t0 - HALO:t0 - HALO + tl]),
                         st=(dc == 0):
                         e_.matmul(o, l, r, start=st, stop=False),
                         waits=[evq, xhat.evt, gate])
                pe = P.op("tensor",
                          lambda e_, o=bank[0:64, 0:tl],
                          l=row(biasr[0:1, 0:1, h * 64:(h + 1) * 64]),
                          r=P.ones_r[:, 0:tl]:
                          e_.matmul(o, l, r, start=False, stop=True),
                          waits=[ev_b[0], P.ev_ones], inc=True)
                ve = emit_elu(P, bank[0:64, :], tl, t1, t2,
                              sq(qfT[sl][0:64, h, 0:tl]), pe)
                P.pool.release(bi, ve)
            qev[sl] = P.now("vector")

        prev_z = None
        emit_qproj(0)
        emit_qproj(1)
        for ti, (t0, tl) in enumerate(all_tiles):
            sl = ti % 2
            ev_qf = qev[sl]
            for k in range(tl // 128):
                par = k % 2
                dn = dens[par]
                bY = [P.pool.acquire() for _ in range(2)]
                bD = P.pool.acquire()
                peY = [None, None]
                peD = None
                for h in range(NH):
                    bk, gk, _ = bY[h // 8]
                    cb = (h % 8) * 64
                    st = (h % 8 == 0)
                    peY[h // 8] = P.op(
                        "tensor",
                        lambda e_, o=bk[:, cb:cb + 64],
                        l=sq(qfT[sl][0:64, h, k * 128:(k + 1) * 128]),
                        r=kvb[0:64, h * 65:h * 65 + 64], s=st:
                        e_.matmul(o, l, r, start=s, stop=False,
                                  skip_group_check=True),
                        waits=[ev_qf, ev_kvb, gk if st else None],
                        inc=(h % 8 == 7))
                    peD = P.op(
                        "tensor",
                        lambda e_, o=bD[0][:, h:h + 1],
                        l=sq(qfT[sl][0:64, h, k * 128:(k + 1) * 128]),
                        r=kvb[0:64, h * 65 + 64:h * 65 + 65], s=(h == 0):
                        e_.matmul(o, l, r, start=s, stop=(h == NH - 1),
                                  skip_group_check=True),
                        waits=[bD[1] if h == 0 else None],
                        inc=(h == NH - 1))
                zc = P.op("vector", lambda e_, o=dn, i=bD[0][:, 0:16]:
                          e_.tensor_scalar(o[:, :], i, EPS_LA, None, OP.add),
                          waits=[peD], inc=True)
                P.pool.release(bD[2], zc)
                P.op("scalar", lambda e_, o=dn: e_.activation(
                    o[:, :], o[:, :], AF.Ln), waits=[zc])
                za = P.op("scalar", lambda e_, o=dn: e_.activation(
                    o[:, :], o[:, :], AF.Exp, scale=-1.0), inc=True)
                last = [None, None]
                for h in range(NH):
                    bk, gk, bik = bY[h // 8]
                    cb = (h % 8) * 64
                    last[h // 8] = P.op(
                        "vector",
                        lambda e_, o=sq(yz[:, k, h * 64:(h + 1) * 64]),
                        i=bk[:, cb:cb + 64], s=dn[:, h:h + 1]:
                        e_.tensor_scalar(o, i, s, None, OP.mult),
                        waits=[za, peY[h // 8]], inc=True)
                for j in range(2):
                    P.pool.release(bY[j][2], last[j])
            if ti + 2 < len(all_tiles):
                emit_qproj(ti + 2)
            for k in range(tl // 128):
                for fc in range(DC):
                    emit_transpose(P, sq(yzT[:, fc, k * 128:(k + 1) * 128]),
                                   sq(yz[:, k, fc * 128:(fc + 1) * 128]))
            ev_yzT = P.now("vector")
            for oc in range(DC):
                bank, gate, bi = P.pool.acquire()
                for fc in range(DC):
                    P.op("tensor",
                         lambda e_, o=bank[:, 0:tl],
                         l=sq(wo[:, fc, oc * 128:(oc + 1) * 128]),
                         r=sq(yzT[:, fc, 0:tl]), st=(fc == 0):
                         e_.matmul(o, l, r, start=st, stop=False),
                         waits=[evo, ev_yzT, gate])
                pe = P.op("tensor",
                          lambda e_, o=bank[:, 0:tl],
                          l=row(biasr[0:1, 3:4, oc * 128:(oc + 1) * 128]),
                          r=P.ones_r[:, 0:tl]:
                          e_.matmul(o, l, r, start=False, stop=True),
                          waits=[ev_b[3], P.ev_ones], inc=True)
                P.op("vector",
                     lambda e_, o=tbf[:, 0:tl], i=bank[:, 0:tl]:
                     e_.tensor_copy(o, i), waits=[pe])
                re = P.op("vector",
                          lambda e_, o=sq(P.xT[:, oc, t0:t0 + tl]),
                          i=tbf[:, 0:tl]:
                          e_.tensor_tensor(o, o, i, OP.add),
                          waits=[P.xT.evt], inc=True)
                P.pool.release(bi, re)
        P.xT.evt = P.now("vector")


_LAYCACHE = {}


def _lay1(w):
    key = id(w)
    if key in _LAYCACHE:
        return _LAYCACHE[key]
    w = np.asarray(w)
    r = np.ascontiguousarray(
        w.reshape(NE, DC, 128, 4, 512).transpose(0, 3, 2, 1, 4)
    ).reshape(NE, 4, 128, 4096)
    _LAYCACHE[key] = r
    return r


def _lay2(w):
    key = id(w)
    if key in _LAYCACHE:
        return _LAYCACHE[key]
    w = np.asarray(w)
    r = np.ascontiguousarray(
        w.reshape(NE, HFC, 128, 4, 256).transpose(0, 3, 2, 1, 4)
    ).reshape(NE, 4, 128, 4096)
    _LAYCACHE[key] = r
    return r


def _swap_mat():
    p = np.zeros((128, 128), dtype=np.float32)
    for i in range(128):
        d = i % 64
        j = (i - d) + (d + 32) % 64
        p[j, i] = 1.0
    return p


def _rope_tables(core):
    cs = core % 4
    rows = np.arange(NT)
    pos = np.maximum(cs * NT_OWN - HALO + rows, 0).astype(np.float64)
    pp = np.arange(128)
    j = (pp % 64) % 32
    inv = 10000.0 ** (-j / 32.0)
    ang = pos[None, :] * inv[:, None]
    cos = np.cos(ang).astype(np.float32)
    sin = np.sin(ang).astype(np.float32)
    sgn = np.where((pp % 64) < 32, -1.0, 1.0).astype(np.float32)
    return cos, (sin * sgn[:, None]).astype(np.float32)


def _masks(core):
    ii = np.arange(128)[:, None]
    jj = np.arange(256)[None, :]
    prev_ok = (jj < 128) & (jj >= ii)
    cur_ok = (jj >= 128) & ((jj - 128) <= ii)
    maskr = np.where(prev_ok | cur_ok, 0.0, -1e30).astype(np.float32)
    if core % 4 == 0:
        mask0 = np.where(cur_ok, 0.0, -1e30).astype(np.float32)
    else:
        mask0 = maskr
    return mask0, maskr


def _host_core_inputs(core, need, x_rows, inputs):
    m = {"xin": x_rows, "ident": np.eye(128, dtype=np.float32)}
    ffn0 = bool(need & {"s1", "s3"})
    ffn1 = bool(need & {"s4", "s6"})
    if ffn0 or ffn1:
        sel = np.zeros((8, 8, 128), dtype=np.float32)
        for e in range(8):
            sel[e, e, :] = 1.0
        m["esel"] = sel.reshape(8, 8 * 128)
    if ffn0:
        m["gWg"] = np.ascontiguousarray(inputs["L0_ffn_Wg"].T)
        m["gW1"] = _lay1(inputs["L0_ffn_W1"])
        m["gb1"] = np.asarray(inputs["L0_ffn_b1"]).reshape(1, -1)
        m["gW2"] = _lay2(inputs["L0_ffn_W2"])
        m["gb2"] = inputs["L0_ffn_b2"]
    if ffn1:
        m["hWg"] = np.ascontiguousarray(inputs["L1_ffn_router_w"].T)
        m["hW1"] = _lay1(inputs["L1_ffn_w1"])
        m["hW3"] = _lay1(inputs["L1_ffn_w3"])
        m["hW2"] = _lay2(inputs["L1_ffn_w2"])
    if "s2" in need:
        m["pswap"] = _swap_mat()
        for k, n in (("aWqT", "L0_attn_Wq"), ("aWkT", "L0_attn_Wk"),
                     ("aWvT", "L0_attn_Wv"), ("aWoT", "L0_attn_Wo")):
            m[k] = np.ascontiguousarray(inputs[n].T)
        for k, n in (("abq", "L0_attn_bq"), ("abk", "L0_attn_bk"),
                     ("abv", "L0_attn_bv"), ("abo", "L0_attn_bo")):
            m[k] = np.ascontiguousarray(inputs[n]).reshape(1, -1)
        m["mask0"], m["maskr"] = _masks(core)
        m["ropecos"], m["ropesin"] = _rope_tables(core)
    if "s5" in need:
        for k, n in (("lWq", "L1_attn_Wq"), ("lWk", "L1_attn_Wk"),
                     ("lWv", "L1_attn_Wv")):
            m[k] = np.ascontiguousarray(np.asarray(inputs[n]).reshape(D, D))
        m["lWo"] = np.ascontiguousarray(
            np.asarray(inputs["L1_attn_Wo"]).reshape(D, D))
        for k, n in (("lbq", "L1_attn_bq"), ("lbk", "L1_attn_bk"),
                     ("lbv", "L1_attn_bv"), ("lbo", "L1_attn_bo")):
            m[k] = np.ascontiguousarray(np.asarray(inputs[n])).reshape(1, -1)
        sm = np.zeros((128, 2), dtype=np.float32)
        sm[:, 0 if core < 4 else 1] = 1.0
        m["seqmask"] = sm
    return m


def _shard_rows(xfull, dtype):
    shards = []
    for c in range(NCORES):
        lo = c * NT_OWN - HALO
        if c % 4 == 0:
            pad = np.zeros((HALO, D), dtype=dtype)
            rows = np.concatenate(
                [pad, xfull[c * NT_OWN:(c + 1) * NT_OWN].astype(dtype,
                                                                copy=False)],
                axis=0)
        else:
            rows = xfull[lo:(c + 1) * NT_OWN].astype(dtype, copy=False)
        shards.append(np.ascontiguousarray(rows))
    return shards


def _run(phases, xfull, inputs, x_in_f32):
    need = set(phases)
    nc = build_program(phases, x_in_f32)
    dt = np.float32 if x_in_f32 else BF
    shards = _shard_rows(xfull, dt)
    in_maps = [_host_core_inputs(c, need, shards[c], inputs)
               for c in range(NCORES)]
    res = run_bass_kernel_spmd(nc, in_maps, core_ids=list(range(NCORES)))
    outs = [np.asarray(r["xout"]) for r in res.results]
    return np.concatenate(outs, axis=0)


def kernel(**inputs):
    x = np.asarray(inputs["L0_attn_x"], dtype=np.float32)
    xfull = np.ascontiguousarray(x.reshape(NT_OWN * NCORES, D))
    if FUSED:
        out = _run(ALL_PHASES, xfull, inputs, x_in_f32=True)
    else:
        cur = xfull
        f32 = True
        for ph in ALL_PHASES:
            out = _run((ph,), cur, inputs, x_in_f32=f32)
            cur = out
            f32 = False
    return np.ascontiguousarray(
        out.reshape(2, 8192, D).astype(np.float32, copy=False))
